# version 11
# baseline (speedup 1.0000x reference)
import numpy as np
import ml_dtypes
from contextlib import ExitStack

import concourse.bass as bass
import concourse.mybir as mybir
from concourse.bass_utils import run_bass_kernel_spmd
from concourse.bass import IndirectOffsetOnAxis as IOX

F32 = mybir.dt.float32
BF16 = mybir.dt.bfloat16
AF = mybir.ActivationFunctionType
ALU = mybir.AluOpType
AX = mybir.AxisListType

NCORES = 8
B_ALL, T, D = 16, 2048, 1024
BPC = B_ALL // NCORES
NT = T // 128
ND = D // 128
IFF = 2048
NE = 8
R = 32
CAP = 384
EPS = 1e-5

ENGS = ("t", "v", "s", "q", "g")


class Sem:
    __slots__ = ("h", "n")

    def __init__(self, h):
        self.h = h
        self.n = 0


class MK:

    def __init__(self, nc, es, dbg=None, max_ph=99):
        self.nc = nc
        self.es = es
        self.code = {e: [] for e in ENGS}
        self.dbg = dbg or {}
        self.max_ph = max_ph
        self.mute = False
        self.tc = self.sem("tc")
        self.vc = self.sem("vc")
        self.sc = self.sem("sc")
        self.qd = self.sem("qd")
        self.gd = self.sem("gd")
        self.fs = self.sem("fs")
        self.qx = [self.sem("qx0"), self.sem("qx1")]
        self.qs = [self.sem("qs0"), self.sem("qs1")]
        self.qc = [self.sem("qc0"), self.sem("qc1")]
        self.qo = [self.sem("qo0"), self.sem("qo1")]
        self.qa = self.sem("qa")
        self.gw = [self.sem("gw0"), self.sem("gw1")]
        self.gwd = self.sem("gwd")
        self.gc = self.sem("gc")
        self.qy = self.sem("qy")
        self.gsc = [self.sem("gsc0"), self.sem("gsc1")]
        self.ggr = [self.sem("ggr0"), self.sem("ggr1")]
        self.qz = self.sem("qz")
        self.qwb = [self.sem("qwb0"), self.sem("qwb1")]
        self.qwbd = self.sem("qwbd")
        self.qxs = [self.sem("qxs0"), self.sem("qxs1"), self.sem("qxs2")]
        self.qgw = [self.sem("qgw0"), self.sem("qgw1")]
        self.qwd = self.sem("qwd")
        self.cv_sh = self.sem("cv_sh")
        self.cv_ex = [self.sem(f"cv_ex{i}") for i in range(NE)]
        self.dma_sems = [self.qd, self.gd, self.qa, self.gwd, self.qy,
                         self.qz, self.qwbd, self.qwd] + \
            self.qwb + self.qxs + \
            self.qx + self.qs + self.qc + self.qo + self.gw + self.gsc + \
            self.ggr + self.qgw
        self.final_sems = self.dma_sems + [self.cv_sh] + self.cv_ex

    def sem(self, name):
        return Sem(self.es.enter_context(self.nc.semaphore(name)))

    def phase(self, n):
        self.mute = n > self.max_ph

    def emit(self, eng, fn):
        if self.mute:
            return
        self.code[eng].append(fn)

    def wait(self, eng, sem, val=None):
        v = sem.n if val is None else val
        if v <= 0:
            return
        self.emit(eng, lambda E, h=sem.h, v=v: E.wait_ge(h, v))

    def ins(self, eng, build, inc=None, k=1):
        if self.mute:
            return
        if inc is None:
            self.emit(eng, lambda E, b=build: b(E))
        else:
            inc.n += k
            self.emit(eng, lambda E, b=build, h=inc.h, k=k: b(E).then_inc(h, k))

    def dma(self, eng_q, out, in_, sem=None, **kw):
        if sem is None:
            sem = self.qd if eng_q == "q" else self.gd
        if self.mute:
            return sem.n
        sem.n += 16
        self.emit(eng_q, lambda E, o=out, i=in_, h=sem.h, kw=kw:
                  E.dma_start(out=o, in_=i, **kw).then_inc(h, 16))
        return sem.n

    def fence(self):
        if self.mute:
            return
        tgts = [(s, s.n) for s in [self.tc, self.vc, self.sc, self.gc] + self.dma_sems]
        self.fs.n += len(ENGS)
        tgt = self.fs.n
        for e in ENGS:
            for s, v in tgts:
                self.wait(e, s, v)
            self.emit(e, lambda E, h=self.fs.h: E.sem_inc(h, 1))
            self.emit(e, lambda E, h=self.fs.h, v=tgt: E.wait_ge(h, v))

    def tap(self, name, sb_ap):
        if name not in self.dbg:
            return
        for s in (self.tc, self.vc, self.sc):
            self.wait("g", s)
        self.dma("g", self.dbg[name][:, :], sb_ap)
        self.fence()

    def replay(self):
        nc = self.nc
        with nc.Block() as block:
            @block.tensor
            def _(E):
                for fn in self.code["t"]:
                    fn(E)

            @block.vector
            def _(E):
                for fn in self.code["v"]:
                    fn(E)

            @block.scalar
            def _(E):
                for fn in self.code["s"]:
                    fn(E)

            @block.sync
            def _(E):
                for fn in self.code["q"]:
                    fn(E)

            @block.gpsimd
            def _(E):
                for fn in self.code["g"]:
                    fn(E)


def build_program(dbg_names=(), max_ph=99, branch_filter=None, detect_races=True, nbat=BPC):
    nc = bass.Bass("TRN2", target_bir_lowering=False, detect_race_conditions=detect_races)
    es = ExitStack()
    U8 = mybir.dt.uint8

    x = nc.dram_tensor("x", [BPC, T, D], F32, kind="ExternalInput").ap()
    w_u = nc.dram_tensor("w_u", [D, R], F32, kind="ExternalInput").ap()
    w_gate = nc.dram_tensor("w_gate", [D, NE], F32, kind="ExternalInput").ap()
    w_shg = nc.dram_tensor("w_shg", [D, IFF], F32, kind="ExternalInput").ap()
    w_shu = nc.dram_tensor("w_shu", [D, IFF], F32, kind="ExternalInput").ap()
    w_shd = nc.dram_tensor("w_shd", [IFF, D], F32, kind="ExternalInput").ap()
    w_rtg = nc.dram_tensor("w_rtg", [NE, D, IFF], F32, kind="ExternalInput").ap()
    w_rtu = nc.dram_tensor("w_rtu", [NE, D, IFF], F32, kind="ExternalInput").ap()
    w_rtd = nc.dram_tensor("w_rtd", [NE, IFF, D], F32, kind="ExternalInput").ap()
    ident_in = nc.dram_tensor("ident", [128, 3 * 128], BF16, kind="ExternalInput").ap()
    cvec_in = nc.dram_tensor("cvec", [128, 16], F32, kind="ExternalInput").ap()
    out = nc.dram_tensor("out", [BPC, T, D], F32, kind="ExternalOutput").ap()
    scr_s = nc.dram_tensor("scr_s", [T, D], F32, kind="Internal").ap()
    scr_xsc = nc.dram_tensor("scr_xsc", [NE * CAP, D], BF16, kind="Internal").ap()
    scr_y = nc.dram_tensor("scr_y", [NE * CAP, D], F32, kind="Internal").ap()
    scr_wg = nc.dram_tensor("scr_wg", [9, D, IFF], BF16, kind="Internal").ap()
    scr_wu = nc.dram_tensor("scr_wu", [9, D, IFF], BF16, kind="Internal").ap()
    scr_wdn = nc.dram_tensor("scr_wdn", [9, IFF, D], BF16, kind="Internal").ap()

    dbg = {}
    for nm, shp, dt in [
        ("dbg_xnt", [128, ND * T], F32), ("dbg_logits", [128, NT * NE], F32),
        ("dbg_score", [128, NT], F32), ("dbg_mask", [128, NT * NE], F32),
        ("dbg_hi", [128, NT * D], F32), ("dbg_lo", [128, NT * D], F32),
        ("dbg_yb", [32, T], F32), ("dbg_n", [1, T], F32),
        ("dbg_z", [128, NT], F32), ("dbg_acc", [128, NT * D], F32),
        ("dbg_xst", [128, ND * T], F32),
    ]:
        if nm in dbg_names:
            dbg[nm] = nc.dram_tensor(nm, shp, dt, kind="ExternalOutput").ap()

    mk = MK(nc, es, dbg, max_ph=max_ph)
    tc, vc, sc, qd, gd = mk.tc, mk.vc, mk.sc, mk.qd, mk.gd
    breg = es.enter_context(nc.gpsimd.register("breg"))
    mk.emit("g", lambda E: E.reg_mov(breg, NE * CAP - 1))

    identb3 = es.enter_context(nc.sbuf_tensor("identb", [128, 3 * 128], BF16))
    ident = identb3[:, 0:128]
    tril_b = identb3[:, 128:256]
    ones_b = identb3[:, 256:384]
    cvec = es.enter_context(nc.sbuf_tensor("cvecb", [128, 16], F32))
    u_bf = es.enter_context(nc.sbuf_tensor("u_bf", [128, ND * R], BF16))
    gw_hi = es.enter_context(nc.sbuf_tensor("gw_hi", [128, ND * NE], BF16))
    gw_lo = es.enter_context(nc.sbuf_tensor("gw_lo", [128, ND * NE], BF16))
    gw_f = es.enter_context(nc.sbuf_tensor("gw_f", [128, ND * NE], F32))
    gw_t = es.enter_context(nc.sbuf_tensor("gw_t", [128, ND * NE], F32))
    logits_sb = es.enter_context(nc.sbuf_tensor("logits_sb", [128, NT * NE], F32))
    tv_sb = es.enter_context(nc.sbuf_tensor("tv_sb", [128, NT], F32))
    score_sb = es.enter_context(nc.sbuf_tensor("score_sb", [128, NT], F32))
    mask_sb = es.enter_context(nc.sbuf_tensor("mask_sb", [128, NT * NE], F32))
    zacc = es.enter_context(nc.sbuf_tensor("zacc", [128, NT], F32))
    zscr = es.enter_context(nc.sbuf_tensor("zscr", [128, NT], F32))
    zrec = es.enter_context(nc.sbuf_tensor("zrec", [128, NT], F32))
    silu_g = [es.enter_context(nc.sbuf_tensor(f"silu_g{p}", [128, 512], F32)) for p in range(4)]
    mcol = es.enter_context(nc.sbuf_tensor("mcol", [128, 4], F32))
    mask_bf = es.enter_context(nc.sbuf_tensor("mask_bf", [128, NT * NE], BF16))
    pos_sb = es.enter_context(nc.sbuf_tensor("pos_sb", [128, NT * NE], F32))
    sidx_sb = es.enter_context(nc.sbuf_tensor("sidx_sb", [128, NT], mybir.dt.uint32))
    tmp8a = es.enter_context(nc.sbuf_tensor("tmp8a", [128, NE], F32))
    tmp8b = es.enter_context(nc.sbuf_tensor("tmp8b", [128, NE], F32))
    ridx_f = es.enter_context(nc.sbuf_tensor("ridx_f", [128, 4], F32))
    zero_tile = es.enter_context(nc.sbuf_tensor("zero_tile", [128, D], BF16))
    ssq = es.enter_context(nc.sbuf_tensor("ssq", [128, 4], F32))
    rstd = es.enter_context(nc.sbuf_tensor("rstd", [128, 4], F32))

    AR_BYTES = 196608
    arena = es.enter_context(nc.sbuf_tensor("arena", [128, AR_BYTES], U8))
    arf = arena.bitcast(F32)
    arb = arena.bitcast(BF16)

    def vf(off, n):
        return arf[:, off // 4: off // 4 + n]

    def vb(off, n):
        return arb[:, off // 2: off // 2 + n]

    dmaj = vb(0, ND * T)
    dmaj3 = dmaj.rearrange("p (c t) -> p c t", c=ND)
    hi_tm = vb(32768, NT * D)
    lo_tm = vb(65536, NT * D)
    xst = vf(98304, 2 * D)
    tm32a = vf(106496, 2 * D)
    tm32b = vf(114688, 2 * D)
    loch = vb(122880, 2 * D)
    auga = vb(126976, T)[0:36]
    augb = vb(131072, T)[0:36]
    ybf = vb(135168, T)[0:32]
    ysq = vf(139264, 512)[0:32]
    nrow = vf(141312, T)[0:1]
    nt1 = vf(149504, T)[0:1]
    nnhi = vb(157696, T)[0:1]
    nnlo = vb(161792, T)[0:1]
    ones_row = vb(165888, T)[0:1]
    qcol = vb(169984, NT * 512)
    sstg = vf(186368, 2 * D)
    xsb = vb(169984, 2 * D)
    acc = vf(32768, NT * D)
    actt = vb(98304, ND * T)
    wgb = vb(131072, 2 * ND * 512)
    wub = vb(147456, 2 * ND * 512)
    wdb = vb(163840, 16 * 1024)
    cxt = vf(98304, 2 * D)
    csn = vf(106496, 2 * D)
    cgt = vf(114688, 2 * D)
    cot = vf(122880, 2 * D)

    psA = es.enter_context(nc.psum_tensor("psA", [128, 4096], F32))
    psAb = psA.bitcast(BF16)
    ptrh = psAb[:, 0:2048]
    ptrl = psAb[:, 2048:4096]
    plg = psA[:, 2048:2048 + NE]
    pyp = psA[:, 2560:3072][0:32]
    pnp = psA[:, 3072:3584][0:1]
    pq = psA[:, 0:2048]
    ps = psA[:, 2048:4096]
    ptx = psAb[:, 0:2048]
    pgu = psA[:, 0:4096]

    mk.dma("q", identb3[:, :], ident_in[:, :])
    mk.dma("q", cvec[:, :], cvec_in[:, :])
    mk.dma("q", gw_f.rearrange("p (c e) -> p c e", c=ND),
           w_gate.rearrange("(c p) e -> p c e", p=128))
    for c in range(ND):
        mk.dma("g", gw_hi[:, c * NE:(c + 1) * NE], w_gate[c * 128:(c + 1) * 128, :])
        mk.dma("g", u_bf[:, c * R:(c + 1) * R], w_u[c * 128:(c + 1) * 128, :])
    mk.dma("g", scr_wg[8], w_shg[:, :], sem=mk.cv_sh, max_dma_last_dim=8192)
    mk.dma("g", scr_wu[8], w_shu[:, :], sem=mk.cv_sh, max_dma_last_dim=8192)
    mk.dma("g", scr_wdn[8], w_shd[:, :], sem=mk.cv_sh, max_dma_last_dim=8192)
    for e_ in range(NE):
        mk.dma("g", scr_wg[e_], w_rtg[e_], sem=mk.cv_ex[e_], max_dma_last_dim=8192)
        mk.dma("g", scr_wu[e_], w_rtu[e_], sem=mk.cv_ex[e_], max_dma_last_dim=8192)
        mk.dma("g", scr_wdn[e_], w_rtd[e_], sem=mk.cv_ex[e_], max_dma_last_dim=8192)
    mk.wait("v", gd)
    mk.wait("v", qd)
    mk.ins("v", lambda E: E.tensor_copy(gw_t[:, :], gw_hi[:, :]), inc=vc)
    mk.wait("v", vc)
    mk.ins("v", lambda E: E.tensor_tensor(gw_t[:, :], gw_f[:, :], gw_t[:, :], ALU.subtract), inc=vc)
    mk.wait("v", vc)
    mk.ins("v", lambda E: E.tensor_copy(gw_lo[:, :], gw_t[:, :]), inc=vc)
    mk.ins("v", lambda E: E.memset(zero_tile[:, :], 0.0), inc=vc)
    zf = vc.n
    mk.wait("q", vc, zf)
    for zi in range((NE * CAP) // 128):
        mk.dma("q", scr_xsc[zi * 128:(zi + 1) * 128, :], zero_tile[:, :], sem=mk.qz)

    branches = [("e", e, w_rtg, w_rtu, w_rtd) for e in range(NE)] + \
               [("sh", None, w_shg, w_shu, w_shd)]
    if branch_filter is not None:
        branches = [br for br in branches
                    if (br[0] == "sh" and "sh" in branch_filter) or
                       (br[0] == "e" and br[1] in branch_filter)]

    for b in range(nbat):
        mk.phase(1)
        xst_free = [0, 0]
        xnf_free = [0, 0]
        cent_free = [0, 0]
        ptrh_free = [0, 0]
        ptrl_free = [0, 0]
        loch_free = [0, 0]
        lg_free = 0
        pend_tv = []

        def emit_tv(kk, lgv):
            mk.wait("v", sc, lgv)
            mk.ins("v", lambda E, o=tv_sb[:, kk:kk + 1],
                   i=logits_sb[:, kk * NE:(kk + 1) * NE]:
                   E.reduce_max(o, i, axis=AX.X), inc=vc)
            mk.wait("v", vc)
            mk.ins("v", lambda E, o=mask_sb[:, kk * NE:(kk + 1) * NE],
                   i=logits_sb[:, kk * NE:(kk + 1) * NE], s_=tv_sb[:, kk:kk + 1]:
                   E.tensor_scalar(out=o, in0=i, scalar1=s_, scalar2=None,
                                   op0=ALU.is_equal), inc=vc)

        for k in range(NT):
            p = k % 2
            if len(pend_tv) >= 2:
                emit_tv(*pend_tv.pop(0))
            xs_ = xst[:, p * D:(p + 1) * D]
            cent = tm32a[:, p * D:(p + 1) * D]
            xnf = tm32b[:, p * D:(p + 1) * D]
            hi_sl = hi_tm[:, k * D:(k + 1) * D]
            lo_sl = lo_tm[:, k * D:(k + 1) * D]
            mk.wait("q", vc, xst_free[p])
            x_ld = mk.dma("q", xs_, x[b, k * 128:(k + 1) * 128, :], sem=mk.qx[p])
            mk.wait("v", mk.qx[p], x_ld)
            mk.ins("v", lambda E, o=mcol[:, p:p + 1], i=xs_:
                   E.reduce_sum(o, i, axis=AX.X), inc=vc)
            mk.wait("v", vc)
            mk.ins("v", lambda E, o=mcol[:, p:p + 1]: E.tensor_scalar(
                out=o, in0=o, scalar1=-1.0 / D, scalar2=None, op0=ALU.mult), inc=vc)
            mk.wait("v", vc)
            mk.wait("v", sc, cent_free[p])
            mk.ins("v", lambda E, o=cent, i=xs_, m=mcol[:, p:p + 1]: E.tensor_scalar(
                out=o, in0=i, scalar1=m, scalar2=None, op0=ALU.add), inc=vc)
            xst_free[p] = vc.n
            mk.wait("s", vc)
            mk.wait("s", vc, xnf_free[p])
            mk.ins("s", lambda E, o=xnf, i=cent, a=ssq[:, p:p + 1]: E.activation(
                out=o, in_=i, func=AF.Square, accum_out=a), inc=sc)
            mk.wait("s", sc)
            mk.ins("s", lambda E, o=rstd[:, p:p + 1], i=ssq[:, p:p + 1]: E.activation(
                out=o, in_=i, func=AF.Ln, scale=1.0 / D, bias=cvec[:, 0:1]), inc=sc)
            mk.wait("s", sc)
            mk.ins("s", lambda E, o=rstd[:, p:p + 1], i=rstd[:, p:p + 1]: E.activation(
                out=o, in_=i, func=AF.Exp, scale=-0.5), inc=sc)
            mk.wait("s", sc)
            mk.ins("s", lambda E, o=xnf, i=cent, r=rstd[:, p:p + 1]: E.activation(
                out=o, in_=i, func=AF.Copy, scale=r), inc=sc)
            cent_free[p] = sc.n
            mk.wait("v", sc)
            mk.ins("v", lambda E, o=hi_sl, i=xnf: E.tensor_copy(o, i), inc=vc)
            mk.wait("v", vc)
            mk.ins("v", lambda E, o=cent, i=hi_sl: E.tensor_copy(o, i), inc=vc)
            mk.wait("v", vc)
            mk.ins("v", lambda E, o=lo_sl, a=xnf, c_=cent: E.tensor_tensor(
                o, a, c_, ALU.subtract), inc=vc)
            xnf_free[p] = vc.n
            hilo_done = vc.n
            mk.wait("t", vc, hilo_done)
            mk.wait("t", sc, ptrh_free[p])
            for c in range(ND):
                mk.ins("t", lambda E,
                       o=ptrh[:, p * 1024 + c * 128: p * 1024 + (c + 1) * 128],
                       i=hi_sl[:, c * 128:(c + 1) * 128], d=ident[:, :]:
                       E.transpose(o, i, d), inc=(tc if c == ND - 1 else None))
            trh_done = tc.n
            mk.wait("t", sc, ptrl_free[p])
            for c in range(ND):
                mk.ins("t", lambda E,
                       o=ptrl[:, p * 1024 + c * 128: p * 1024 + (c + 1) * 128],
                       i=lo_sl[:, c * 128:(c + 1) * 128], d=ident[:, :]:
                       E.transpose(o, i, d), inc=(tc if c == ND - 1 else None))
            trl_done = tc.n
            mk.wait("s", tc, trh_done)
            mk.ins("s", lambda E, o=dmaj3[:, :, k * 128:(k + 1) * 128],
                   i=ptrh.rearrange("p (q c t) -> p q c t", q=2, c=ND)[:, p]:
                   E.copy(o, i), inc=sc)
            ptrh_free[p] = sc.n
            mk.wait("s", tc, trl_done)
            mk.wait("s", tc, loch_free[p])
            mk.ins("s", lambda E, o=loch[:, p * 1024:(p + 1) * 1024],
                   i=ptrl[:, p * 1024:(p + 1) * 1024]: E.copy(o, i), inc=sc)
            ptrl_free[p] = sc.n
            evac_done = sc.n
            mk.wait("t", sc, evac_done)
            nmm = 3 * ND
            j = 0
            for pass_i in range(3):
                wsb = gw_hi if pass_i != 1 else gw_lo
                for c in range(ND):
                    if pass_i < 2:
                        lh = dmaj[:, c * T + k * 128: c * T + (k + 1) * 128]
                    else:
                        lh = loch[:, p * 1024 + c * 128: p * 1024 + (c + 1) * 128]
                    mk.ins("t", lambda E, o=plg[:, :], l=lh,
                           r=wsb[:, c * NE:(c + 1) * NE],
                           st=(j == 0), sp=(j == nmm - 1):
                           E.matmul(o, l, r, start=st, stop=sp),
                           inc=(tc if j == nmm - 1 else None))
                    j += 1
            loch_free[p] = tc.n
            mk.wait("s", tc)
            mk.ins("s", lambda E, o=logits_sb[:, k * NE:(k + 1) * NE],
                   i=plg[:, :]: E.copy(o, i), inc=sc)
            lg_free = sc.n
            pend_tv.append((k, sc.n))
        for pt in pend_tv:
            emit_tv(*pt)
        pend_tv = []
        mk.tap("dbg_hi", hi_tm[:, :])
        mk.tap("dbg_lo", lo_tm[:, :])
        mk.tap("dbg_xnt", dmaj[:, :])
        mk.tap("dbg_logits", logits_sb[:, :])
        mk.tap("dbg_mask", mask_sb[:, :])
        mk.fence()

        mk.phase(2)
        mk.ins("v", lambda E: E.memset(ones_row[:, :], 1.0), inc=vc)
        for js in range(4):
            for c in range(ND):
                mk.ins("t", lambda E, o=pyp[:, :], l=u_bf[:, c * R:(c + 1) * R],
                       r=dmaj[:, c * T + js * 512: c * T + js * 512 + 512],
                       st=(c == 0), sp=(c == ND - 1):
                       E.matmul(o, l, r, start=st, stop=sp),
                       inc=(tc if c == ND - 1 else None))
            mk.wait("s", tc)
            mk.ins("s", lambda E, o=ybf[:, js * 512:(js + 1) * 512],
                   i=pyp[:, :]: E.copy(o, i), inc=sc)
            mk.wait("s", sc)
            mk.ins("s", lambda E, o=ysq[:, :],
                   i=ybf[:, js * 512:(js + 1) * 512]:
                   E.activation(out=o, in_=i, func=AF.Square), inc=sc)
            mk.wait("t", sc)
            mk.ins("t", lambda E, o=pnp[:, :], l=cvec[0:32, 1:2],
                   r=ysq[:, :]:
                   E.matmul(o, l, r, start=True, stop=True), inc=tc)
            mk.wait("s", tc)
            mk.ins("s", lambda E, o=nrow[:, js * 512:(js + 1) * 512],
                   i=pnp[:, :]: E.copy(o, i), inc=sc)
            mk.wait("t", sc)
        mk.wait("v", sc)
        mk.ins("v", lambda E: E.tensor_copy(nnhi[:, :], nrow[:, :]), inc=vc)
        mk.wait("v", vc)
        mk.ins("v", lambda E: E.tensor_copy(nt1[:, :], nnhi[:, :]), inc=vc)
        mk.wait("v", vc)
        mk.ins("v", lambda E: E.tensor_scalar(out=nnhi[:, :], in0=nnhi[:, :],
               scalar1=-1.0, scalar2=None, op0=ALU.mult), inc=vc)
        mk.wait("v", vc)
        mk.ins("v", lambda E: E.tensor_tensor(nt1[:, :], nrow[:, :], nt1[:, :],
                                              ALU.subtract), inc=vc)
        mk.wait("v", vc)
        mk.ins("v", lambda E: E.tensor_scalar(out=nnlo[:, :], in0=nt1[:, :],
               scalar1=-1.0, scalar2=None, op0=ALU.mult), inc=vc)
        mk.wait("v", vc)
        mk.ins("v", lambda E: E.tensor_scalar(out=auga[0:32, :], in0=ybf[:, :],
               scalar1=2.0, scalar2=None, op0=ALU.mult), inc=vc)
        mk.wait("v", vc)
        mk.ins("v", lambda E: E.tensor_copy(augb[0:32, :], ybf[:, :]), inc=vc)
        mk.wait("q", vc)
        mk.dma("q", auga[32:33, :], nnhi[:, :], sem=mk.qa)
        mk.dma("q", auga[33:34, :], nnlo[:, :], sem=mk.qa)
        mk.dma("q", auga[34:35, :], ones_row[:, :], sem=mk.qa)
        mk.dma("q", auga[35:36, :], ones_row[:, :], sem=mk.qa)
        mk.dma("q", augb[32:33, :], ones_row[:, :], sem=mk.qa)
        mk.dma("q", augb[33:34, :], ones_row[:, :], sem=mk.qa)
        mk.dma("q", augb[34:35, :], nnhi[:, :], sem=mk.qa)
        aug_ld = mk.dma("q", augb[35:36, :], nnlo[:, :], sem=mk.qa)
        mk.tap("dbg_yb", ybf[:, :])
        mk.tap("dbg_n", nrow[:, :])
        mk.wait("t", mk.qa, aug_ld)
        ppos = psA[:, 3584:3584 + NE]
        mk.wait("v", vc)
        mk.ins("v", lambda E: E.tensor_copy(mask_bf[:, :], mask_sb[:, :]), inc=vc)
        mbf_done = vc.n
        pos_ev = 0
        for I in range(NT):
            mk.wait("t", vc, mbf_done)
            mk.wait("t", sc, pos_ev)
            for Jj in range(I + 1):
                lh = ones_b if Jj < I else tril_b
                mk.ins("t", lambda E, o=ppos[:, :], l=lh,
                       r=mask_bf[:, Jj * NE:(Jj + 1) * NE],
                       st=(Jj == 0), sp=(Jj == I):
                       E.matmul(o, l, r, start=st, stop=sp),
                       inc=(tc if Jj == I else None))
            mk.wait("s", tc)
            mk.ins("s", lambda E, o=pos_sb[:, I * NE:(I + 1) * NE], i=ppos[:, :]:
                   E.copy(o, i), inc=sc)
            pos_ev = sc.n
        mk.wait("v", sc, pos_ev)
        for I in range(NT):
            mk.ins("v", lambda E, o=tmp8a[:, :], a=mask_sb[:, I * NE:(I + 1) * NE],
                   b_=pos_sb[:, I * NE:(I + 1) * NE]:
                   E.tensor_tensor(o, a, b_, ALU.mult), inc=vc)
            mk.wait("v", vc)
            mk.ins("v", lambda E, o=ridx_f[:, 0:1], i=tmp8a[:, :]:
                   E.reduce_sum(o, i, axis=AX.X), inc=vc)
            mk.ins("v", lambda E, o=tmp8b[:, :], a=mask_sb[:, I * NE:(I + 1) * NE],
                   b_=cvec[:, 2:10]:
                   E.tensor_tensor(o, a, b_, ALU.mult), inc=vc)
            mk.wait("v", vc)
            mk.ins("v", lambda E, o=ridx_f[:, 1:2], i=tmp8b[:, :]:
                   E.reduce_sum(o, i, axis=AX.X), inc=vc)
            mk.wait("v", vc)
            mk.ins("v", lambda E, o=ridx_f[:, 0:1]: E.tensor_scalar(
                out=o, in0=o, scalar1=-1.0, scalar2=None, op0=ALU.add), inc=vc)
            mk.wait("v", vc)
            mk.ins("v", lambda E, o=ridx_f[:, 2:3], i=ridx_f[:, 1:2],
                   s2=ridx_f[:, 0:1]: E.scalar_tensor_tensor(
                   out=o, in0=i, scalar=float(CAP), in1=s2,
                   op0=ALU.mult, op1=ALU.add), inc=vc)
            mk.wait("v", vc)
            mk.ins("v", lambda E, o=ridx_f[:, 3:4], i=ridx_f[:, 0:1]:
                   E.tensor_scalar(out=o, in0=i, scalar1=float(CAP), scalar2=None,
                                   op0=ALU.is_ge), inc=vc)
            mk.wait("v", vc)
            mk.ins("v", lambda E, o=ridx_f[:, 2:3], i=ridx_f[:, 3:4],
                   s2=ridx_f[:, 2:3]: E.scalar_tensor_tensor(
                   out=o, in0=i, scalar=1.0e6, in1=s2,
                   op0=ALU.mult, op1=ALU.add), inc=vc)
            mk.wait("v", vc)
            mk.ins("v", lambda E, o=sidx_sb[:, I:I + 1], i=ridx_f[:, 2:3]:
                   E.tensor_copy(o, i), inc=vc)
            mk.wait("v", vc)
        mk.fence()

        mk.phase(3)
        mk.wait("v", vc)
        mk.ins("v", lambda E: E.memset(zacc[:, :], 0.0), inc=vc)
        mk.wait("v", vc)
        pq_free = [0, 0, 0, 0]
        ps_free = [0, 0, 0, 0]
        qcol_free = 0
        sstg_free = [0, 0]
        zprev = 0
        for gi in range(4):
            mk.wait("s", tc, qcol_free)
            mk.wait("s", vc, zprev)
            for J in range(NT):
                bank = J % 4
                mk.wait("t", sc, pq_free[bank])
                mk.ins("t", lambda E, o=pq[:, bank * 512:(bank + 1) * 512],
                       l=auga[:, J * 128:(J + 1) * 128],
                       r=augb[:, gi * 512:(gi + 1) * 512]:
                       E.matmul(o, l, r, start=True, stop=True), inc=tc)
                mk.wait("s", tc)
                mk.ins("s", lambda E, o=qcol[:, J * 512:(J + 1) * 512],
                       i=pq[:, bank * 512:(bank + 1) * 512],
                       a=zscr[:, J:J + 1]: E.activation(
                       out=o, in_=i, func=AF.Exp, accum_out=a), inc=sc)
                pq_free[bank] = sc.n
            exp_done = sc.n
            mk.wait("v", sc, exp_done)
            mk.ins("v", lambda E: E.tensor_tensor(zacc[:, :], zacc[:, :],
                   zscr[:, :], ALU.add), inc=vc)
            zprev = vc.n
            mk.wait("t", sc, exp_done)
            for il in range(4):
                I = gi * 4 + il
                bk0, bk1 = 2 * (il % 2), 2 * (il % 2) + 1
                for dh, bank in ((0, bk0), (1, bk1)):
                    mk.wait("t", vc, ps_free[bank])
                    for J in range(NT):
                        mk.ins("t", lambda E,
                               o=ps[:, bank * 512:(bank + 1) * 512],
                               l=qcol[:, J * 512 + il * 128: J * 512 + (il + 1) * 128],
                               r=hi_tm[:, J * D + dh * 512: J * D + dh * 512 + 512],
                               st=(J == 0), sp=(J == NT - 1):
                               E.matmul(o, l, r, start=st, stop=sp),
                               inc=(tc if J == NT - 1 else None))
                smm_done = tc.n
                sp_ = il % 2
                mk.wait("v", tc, smm_done)
                mk.wait("v", mk.qs[sp_], sstg_free[sp_])
                mk.ins("v", lambda E, o=sstg[:, sp_ * D: sp_ * D + 512],
                       i=ps[:, bk0 * 512:(bk0 + 1) * 512]:
                       E.tensor_copy(o, i), inc=vc)
                mk.wait("v", vc)
                mk.ins("v", lambda E, o=sstg[:, sp_ * D + 512: (sp_ + 1) * D],
                       i=ps[:, bk1 * 512:(bk1 + 1) * 512]:
                       E.tensor_copy(o, i), inc=vc)
                ps_free[bk0] = vc.n
                ps_free[bk1] = vc.n
                mk.wait("q", vc)
                sstg_free[sp_] = mk.dma(
                    "q", scr_s[I * 128:(I + 1) * 128, :],
                    sstg[:, sp_ * D:(sp_ + 1) * D], sem=mk.qs[sp_])
            qcol_free = tc.n
        mk.tap("dbg_z", zacc[:, :])
        mk.fence()

        mk.phase(4)
        mk.wait("s", vc)
        mk.ins("s", lambda E: E.activation(out=score_sb[:, :], in_=tv_sb[:, :],
                                           func=AF.Sigmoid), inc=sc)
        sc_score = sc.n
        if b == 0:
            mk.wait("g", mk.qz)
        xsb_free = [0, 0]
        for k in range(NT):
            p = k % 2
            tmp = tm32a[:, p * D:(p + 1) * D]
            mk.wait("v", sc, sc_score)
            mk.ins("v", lambda E, o=tmp, a=hi_tm[:, k * D:(k + 1) * D],
                   b_=lo_tm[:, k * D:(k + 1) * D]:
                   E.tensor_tensor(o, a, b_, ALU.add), inc=vc)
            mk.wait("v", vc)
            mk.wait("v", mk.gsc[p], xsb_free[p])
            mk.ins("v", lambda E, o=xsb[:, p * D:(p + 1) * D], i=tmp,
                   s_=score_sb[:, k:k + 1]: E.tensor_scalar(
                   out=o, in0=i, scalar1=s_, scalar2=None, op0=ALU.mult), inc=vc)
            xs_done = vc.n
            mk.wait("g", vc, xs_done)
            if not mk.mute:
                mk.gsc[p].n += 16
            mk.emit("g", lambda E, i=xsb[:, p * D:(p + 1) * D],
                    ix=sidx_sb[:, k:k + 1], h=mk.gsc[p].h:
                    E.indirect_dma_start(out=scr_xsc[:, :],
                                         out_offset=IOX(ap=ix, axis=0),
                                         in_=i, in_offset=None,
                                         bounds_check=breg,
                                         oob_is_err=False).then_inc(h, 16))
            xsb_free[p] = mk.gsc[p].n
        mk.fence()

        mk.phase(5)
        actt_sh = vb(98304, ND * T)
        xsT_e3 = [vb(98304 + p * 6144, ND * CAP).rearrange("p_ (c s) -> p_ c s", c=ND)
                  for p in range(2)]
        acte = vb(110592, NT * CAP)
        wg_free = [0, 0]
        wgwb_free = [0, 0]
        wd_free = 0
        wdwb_free = 0
        dn_free = [0, 0, 0, 0]
        gbank_sfree = [0] * 4
        ubank_vfree = [0] * 4
        silu_free = [0, 0, 0, 0]
        spi = 0

        def load_wgu(br_i, i0, pw):
            for c in range(ND):
                mk.dma("q", wgb[:, (pw * ND + c) * 512:(pw * ND + c + 1) * 512],
                       scr_wg[br_i, c * 128:(c + 1) * 128, i0:i0 + 512], sem=mk.qgw[pw])
                mk.dma("s", wub[:, (pw * ND + c) * 512:(pw * ND + c + 1) * 512],
                       scr_wu[br_i, c * 128:(c + 1) * 128, i0:i0 + 512], sem=mk.qgw[pw])
            return (pw, mk.qgw[pw].n)

        def load_wd(br_i, ic_list, dst_off):
            for j, ic in enumerate(ic_list):
                mk.dma("s" if j % 2 else "q",
                       wdb[:, dst_off + j * 1024: dst_off + (j + 1) * 1024],
                       scr_wdn[br_i, ic * 128:(ic + 1) * 128, :], sem=mk.qwd)
            return mk.qwd.n

        sh_i = 8
        do_shared = branch_filter is None or "sh" in branch_filter
        if do_shared:
            if b == 0:
                mk.wait("q", mk.cv_sh)
                mk.wait("s", mk.cv_sh)
            for half in range(2):
                wg_loaded = [None, None]
                for sp_i in range(2):
                    pw = spi % 2
                    mk.wait("q", tc, wg_free[pw])
                    mk.wait("s", tc, wg_free[pw])
                    i0 = half * 1024 + sp_i * 512
                    wg_loaded[sp_i] = load_wgu(sh_i, i0, pw)
                    spi += 1
                mk.wait("q", tc, wd_free)
                mk.wait("s", tc, wd_free)
                ics = list(range(half * 8, half * 8 + 8))
                wd_ld = load_wd(sh_i, ics, 0)
                for sp_i in range(2):
                    pw, _wgl = wg_loaded[sp_i]
                    mk.wait("t", mk.qgw[pw], _wgl)
                    for il in range(4):
                        ic = sp_i * 4 + il
                        for t4 in range(4):
                            mk.wait("t", sc, gbank_sfree[t4])
                            mk.wait("t", vc, dn_free[t4])
                        for c in range(ND):
                            for t4 in range(4):
                                mk.ins("t", lambda E,
                                       o=pgu[:, t4 * 512:(t4 + 1) * 512],
                                       l=wgb[:, (pw * ND + c) * 512 + il * 128:
                                              (pw * ND + c) * 512 + (il + 1) * 128],
                                       r=dmaj[:, c * T + t4 * 512: c * T + (t4 + 1) * 512],
                                       st=(c == 0), sp=(c == ND - 1):
                                       E.matmul(o, l, r, start=st, stop=sp),
                                       inc=(tc if (c == ND - 1 and t4 == 3) else None))
                        gmm_done = tc.n
                        for t4 in range(4):
                            mk.wait("t", vc, ubank_vfree[t4])
                        for c in range(ND):
                            for t4 in range(4):
                                mk.ins("t", lambda E,
                                       o=pgu[:, (4 + t4) * 512:(5 + t4) * 512],
                                       l=wub[:, (pw * ND + c) * 512 + il * 128:
                                              (pw * ND + c) * 512 + (il + 1) * 128],
                                       r=dmaj[:, c * T + t4 * 512: c * T + (t4 + 1) * 512],
                                       st=(c == 0), sp=(c == ND - 1):
                                       E.matmul(o, l, r, start=st, stop=sp),
                                       inc=(tc if (c == ND - 1 and t4 == 3) else None))
                        umm_done = tc.n
                        for t4 in range(4):
                            mk.wait("s", tc, gmm_done)
                            mk.wait("s", vc, silu_free[t4])
                            mk.ins("s", lambda E, o=silu_g[t4][:, :],
                                   i=pgu[:, t4 * 512:(t4 + 1) * 512]:
                                   E.activation(out=o, in_=i, func=AF.Silu), inc=sc)
                            gbank_sfree[t4] = sc.n
                            mk.wait("v", sc)
                            mk.wait("v", tc, umm_done)
                            mk.ins("v", lambda E,
                                   o=actt_sh[:, ic * T + t4 * 512: ic * T + (t4 + 1) * 512],
                                   i=pgu[:, (4 + t4) * 512:(5 + t4) * 512],
                                   s_=silu_g[t4][:, :]:
                                   E.tensor_tensor(o, i, s_, ALU.mult), inc=vc)
                            ubank_vfree[t4] = vc.n
                            silu_free[t4] = vc.n
                    wg_free[pw] = tc.n
                act_done = vc.n
                mk.wait("t", mk.qwd, wd_ld)
                mk.wait("t", vc, act_done)
                for tch in range(NT):
                    for dh in range(2):
                        bank = 2 * (tch % 2) + dh
                        mk.wait("t", vc, dn_free[bank])
                        mk.wait("t", sc, gbank_sfree[bank])
                        for j in range(8):
                            mk.ins("t", lambda E,
                                   o=pgu[:, bank * 512:(bank + 1) * 512],
                                   l=actt_sh[:, j * T + tch * 128:
                                             j * T + (tch + 1) * 128],
                                   r=wdb[:, j * 1024 + dh * 512: j * 1024 + dh * 512 + 512],
                                   st=(j == 0), sp=(j == 7):
                                   E.matmul(o, l, r, start=st, stop=sp),
                                   inc=(tc if j == 7 else None))
                        dn_done = tc.n
                        a_sl = acc[:, tch * D + dh * 512: tch * D + dh * 512 + 512]
                        mk.wait("v", tc, dn_done)
                        if half == 0:
                            mk.ins("v", lambda E, o=a_sl,
                                   i=pgu[:, bank * 512:(bank + 1) * 512]:
                                   E.tensor_copy(o, i), inc=vc)
                        else:
                            mk.ins("v", lambda E, o=a_sl,
                                   i=pgu[:, bank * 512:(bank + 1) * 512]:
                                   E.tensor_tensor(o, o, i, ALU.add), inc=vc)
                        dn_free[bank] = vc.n
                wd_free = tc.n
        else:
            mk.wait("v", vc)
            mk.ins("v", lambda E: E.memset(acc[:, :], 0.0), inc=vc)
            mk.wait("v", vc)

        xstg3 = vb(0, 3 * 1024)
        ystg2 = vf(8192, 512)
        NSC = CAP // 128
        exp_list = [e for e in range(NE)
                    if branch_filter is None or e in branch_filter]
        mk.wait("q", mk.gsc[0])
        mk.wait("q", mk.gsc[1])
        dmaj_free = tc.n
        xstg_free = [dmaj_free] * 3
        ptr_free = [0, 0]
        xsT_free = [0, 0]
        xsT_done = [0, 0]
        acte_free = 0
        ystg_free = 0
        dnb_free = [0] * 6
        ptr_e = psAb[:, 0:2048]

        def issue_xstg_loads(e):
            lds = []
            for sc_ in range(NSC):
                mk.wait("q", tc, xstg_free[sc_])
                lds.append(mk.dma(
                    "q", xstg3[:, sc_ * D: sc_ * D + D],
                    scr_xsc[e * CAP + sc_ * 128: e * CAP + (sc_ + 1) * 128, :],
                    sem=mk.qxs[sc_]))
            return lds

        def prep_xsT(pbuf, lds):
            for sc_ in range(NSC):
                pxp = sc_ % 2
                mk.wait("t", mk.qxs[sc_], lds[sc_])
                mk.wait("t", sc, ptr_free[pxp])
                mk.wait("t", sc, gbank_sfree[pxp])
                mk.wait("t", vc, ubank_vfree[pxp])
                mk.wait("t", vc, dn_free[pxp])
                mk.wait("t", tc, xsT_free[pbuf])
                for c in range(ND):
                    mk.ins("t", lambda E,
                           o=ptr_e[:, pxp * 1024 + c * 128: pxp * 1024 + (c + 1) * 128],
                           i=xstg3[:, sc_ * D + c * 128: sc_ * D + (c + 1) * 128],
                           d=ident[:, :]: E.transpose(o, i, d),
                           inc=(tc if c == ND - 1 else None))
                xstg_free[sc_] = tc.n
                mk.wait("s", tc)
                mk.wait("s", tc, xsT_free[pbuf])
                mk.ins("s", lambda E,
                       o=xsT_e3[pbuf][:, :, sc_ * 128:(sc_ + 1) * 128],
                       i=ptr_e.rearrange("p (q c s) -> p q c s", q=2, c=ND)[:, pxp]:
                       E.copy(o, i), inc=sc)
                ptr_free[pxp] = sc.n
            xsT_done[pbuf] = sc.n

        for idx, e in enumerate(exp_list):
            pbuf = idx % 2
            if idx == 0:
                lds0 = issue_xstg_loads(e)
                prep_xsT(pbuf, lds0)
            wg_loaded = [None, None, None, None]

            def load_span(sp_i):
                nonlocal spi
                pw = spi % 2
                mk.wait("q", tc, wg_free[pw])
                mk.wait("s", tc, wg_free[pw])
                wg_loaded[sp_i] = load_wgu(e, sp_i * 512, pw)
                spi += 1

            if b == 0:
                mk.wait("q", mk.cv_ex[e])
                mk.wait("s", mk.cv_ex[e])
            load_span(0)
            load_span(1)
            mk.wait("q", tc, wd_free)
            mk.wait("s", tc, wd_free)
            wd_ld = load_wd(e, list(range(16)), 0)
            for sp_i in range(4):
                pw, _wgl = wg_loaded[sp_i]
                mk.wait("t", mk.qgw[pw], _wgl)
                mk.wait("t", sc, xsT_done[pbuf])
                for il in range(4):
                    ic = sp_i * 4 + il
                    gb = 2 * (il % 2)
                    ub = 2 * (il % 2) + 1
                    mk.wait("t", sc, gbank_sfree[gb])
                    mk.wait("t", vc, ubank_vfree[ub])
                    mk.wait("t", vc, dn_free[gb])
                    mk.wait("t", vc, dn_free[ub])
                    if gb >= 2:
                        mk.wait("t", sc, dnb_free[gb - 2])
                    if ub >= 2:
                        mk.wait("t", sc, dnb_free[ub - 2])
                    for c in range(ND):
                        mk.ins("t", lambda E,
                               o=pgu[:, gb * 512: gb * 512 + CAP],
                               l=wgb[:, (pw * ND + c) * 512 + il * 128:
                                      (pw * ND + c) * 512 + (il + 1) * 128],
                               r=xsT_e3[pbuf].rearrange("p c s -> p (c s)")[:, c * CAP:(c + 1) * CAP],
                               st=(c == 0), sp=(c == ND - 1):
                               E.matmul(o, l, r, start=st, stop=sp),
                               inc=(tc if c == ND - 1 else None))
                    gmm_done = tc.n
                    for c in range(ND):
                        mk.ins("t", lambda E,
                               o=pgu[:, ub * 512: ub * 512 + CAP],
                               l=wub[:, (pw * ND + c) * 512 + il * 128:
                                      (pw * ND + c) * 512 + (il + 1) * 128],
                               r=xsT_e3[pbuf].rearrange("p c s -> p (c s)")[:, c * CAP:(c + 1) * CAP],
                               st=(c == 0), sp=(c == ND - 1):
                               E.matmul(o, l, r, start=st, stop=sp),
                               inc=(tc if c == ND - 1 else None))
                    umm_done = tc.n
                    sgi = il % 2
                    mk.wait("s", tc, gmm_done)
                    mk.wait("s", vc, silu_free[sgi])
                    mk.ins("s", lambda E, o=silu_g[sgi][:, 0:CAP],
                           i=pgu[:, gb * 512: gb * 512 + CAP]:
                           E.activation(out=o, in_=i, func=AF.Silu), inc=sc)
                    gbank_sfree[gb] = sc.n
                    mk.wait("v", sc)
                    mk.wait("v", tc, umm_done)
                    mk.wait("v", tc, acte_free)
                    mk.ins("v", lambda E,
                           o=acte[:, ic * CAP:(ic + 1) * CAP],
                           i=pgu[:, ub * 512: ub * 512 + CAP],
                           s_=silu_g[sgi][:, 0:CAP]:
                           E.tensor_tensor(o, i, s_, ALU.mult), inc=vc)
                    ubank_vfree[ub] = vc.n
                    silu_free[sgi] = vc.n
                wg_free[pw] = tc.n
                if sp_i + 2 < 4:
                    load_span(sp_i + 2)
            act_done = vc.n
            xsT_free[pbuf] = tc.n
            if idx + 1 < len(exp_list):
                lds_n = issue_xstg_loads(exp_list[idx + 1])
            mk.wait("t", mk.qwd, wd_ld)
            mk.wait("t", vc, act_done)
            for bnk in range(6):
                mk.wait("t", sc, dnb_free[bnk])
            for ic in range(16):
                for sc_ in range(NSC):
                    for dh in range(2):
                        bank = 2 + sc_ * 2 + dh
                        mk.ins("t", lambda E,
                               o=pgu[:, bank * 512:(bank + 1) * 512],
                               l=acte[:, ic * CAP + sc_ * 128: ic * CAP + (sc_ + 1) * 128],
                               r=wdb[:, ic * 1024 + dh * 512: ic * 1024 + dh * 512 + 512],
                               st=(ic == 0), sp=(ic == 15):
                               E.matmul(o, l, r, start=st, stop=sp),
                               inc=(tc if (ic == 15 and sc_ == NSC - 1 and dh == 1) else None))
            dnm_done = tc.n
            acte_free = tc.n
            wd_free = tc.n
            if idx + 1 < len(exp_list):
                prep_xsT(1 - pbuf, lds_n)
            for sc_ in range(NSC):
                for dh in range(2):
                    bank = 2 + sc_ * 2 + dh
                    mk.wait("s", tc, dnm_done)
                    mk.wait("s", mk.qy, ystg_free)
                    mk.ins("s", lambda E, o=ystg2[:, :],
                           i=pgu[:, bank * 512:(bank + 1) * 512]:
                           E.copy(o, i), inc=sc)
                    dnb_free[sc_ * 2 + dh] = sc.n
                    mk.wait("q", sc)
                    ystg_free = mk.dma(
                        "q",
                        scr_y[e * CAP + sc_ * 128: e * CAP + (sc_ + 1) * 128,
                              dh * 512:(dh + 1) * 512],
                        ystg2[:, :], sem=mk.qy)
        mk.fence()

        mk.phase(6)
        grt = vf(131072, 2 * D)
        mk.wait("v", vc)
        mk.ins("v", lambda E: E.reciprocal(zrec[:, :], zacc[:, :]), inc=vc)
        zr_done = vc.n
        mk.wait("s", vc, zr_done)
        mk.wait("g", mk.qy)
        cxt_free = [0, 0]
        csn_free = [0, 0]
        cgt_free = [0, 0]
        grt_free = [0, 0]
        cot_free = [0, 0]
        for k in range(NT):
            p = k % 2
            mk.wait("q", vc, cxt_free[p])
            mk.dma("q", cxt[:, p * D:(p + 1) * D], x[b, k * 128:(k + 1) * 128, :],
                   sem=mk.qc[p])
            mk.wait("q", vc, csn_free[p])
            ld_done = mk.dma("q", csn[:, p * D:(p + 1) * D],
                             scr_s[k * 128:(k + 1) * 128, :], sem=mk.qc[p])
            mk.wait("v", sc, grt_free[p])
            mk.ins("v", lambda E, o=grt[:, p * D:(p + 1) * D]: E.memset(o, 0.0), inc=vc)
            gz = vc.n
            mk.wait("g", vc, gz)
            if not mk.mute:
                mk.ggr[p].n += 16
            mk.emit("g", lambda E, o=grt[:, p * D:(p + 1) * D],
                    ix=sidx_sb[:, k:k + 1], h=mk.ggr[p].h:
                    E.indirect_dma_start(out=o, out_offset=None,
                                         in_=scr_y[:, :],
                                         in_offset=IOX(ap=ix, axis=0),
                                         bounds_check=breg,
                                         oob_is_err=False).then_inc(h, 16))
            g_ld = mk.ggr[p].n
            mk.wait("v", mk.ggr[p], g_ld)
            mk.ins("v", lambda E, o=grt[:, p * D:(p + 1) * D],
                   a=grt[:, p * D:(p + 1) * D], b_=acc[:, k * D:(k + 1) * D]:
                   E.tensor_tensor(o, a, b_, ALU.add), inc=vc)
            ffn_done = vc.n
            mk.wait("s", vc, ffn_done)
            mk.wait("s", vc, cgt_free[p])
            mk.ins("s", lambda E, o=cgt[:, p * D:(p + 1) * D],
                   i=grt[:, p * D:(p + 1) * D]:
                   E.activation(out=o, in_=i, func=AF.Silu), inc=sc)
            grt_free[p] = sc.n
            mk.wait("v", sc)
            mk.wait("v", mk.qc[p], ld_done)
            mk.wait("v", vc, zr_done)
            mk.wait("v", mk.qo[p], cot_free[p])
            mk.ins("v", lambda E, o=cot[:, p * D:(p + 1) * D],
                   i=csn[:, p * D:(p + 1) * D], z=zrec[:, k:k + 1],
                   g_=cgt[:, p * D:(p + 1) * D]: E.scalar_tensor_tensor(
                   out=o, in0=i, scalar=z, in1=g_, op0=ALU.mult, op1=ALU.mult), inc=vc)
            csn_free[p] = vc.n
            cgt_free[p] = vc.n
            mk.wait("v", vc)
            mk.ins("v", lambda E, o=cot[:, p * D:(p + 1) * D],
                   a=cot[:, p * D:(p + 1) * D], b_=cxt[:, p * D:(p + 1) * D]:
                   E.tensor_tensor(o, a, b_, ALU.add), inc=vc)
            cxt_free[p] = vc.n
            mk.wait("q", vc)
            cot_free[p] = mk.dma("q", out[b, k * 128:(k + 1) * 128, :],
                                 cot[:, p * D:(p + 1) * D], sem=mk.qo[p])
        mk.fence()

    for s_ in mk.final_sems:
        mk.wait("q", s_)
    mk.replay()
    es.close()
    return nc


_IDENT = np.concatenate([
    np.eye(128, dtype=ml_dtypes.bfloat16),
    np.triu(np.ones((128, 128), dtype=ml_dtypes.bfloat16)),
    np.ones((128, 128), dtype=ml_dtypes.bfloat16),
], axis=1)
_CVEC = np.zeros((128, 16), np.float32)
_CVEC[:, 0] = EPS
_CVEC[:, 1] = 1.0
_CVEC[:, 2:10] = np.arange(8, dtype=np.float32)[None, :]


def kernel(L0_attn_x, L0_attn_U, L0_ffn_gate_w, L0_ffn_sh_gate, L0_ffn_sh_up,
           L0_ffn_sh_down, L0_ffn_rt_gate, L0_ffn_rt_up, L0_ffn_rt_down):
    xf = np.ascontiguousarray(np.asarray(L0_attn_x, dtype=np.float32))
    wts = {
        "w_u": np.ascontiguousarray(np.asarray(L0_attn_U, np.float32)),
        "w_gate": np.ascontiguousarray(np.asarray(L0_ffn_gate_w, np.float32)),
        "w_shg": np.ascontiguousarray(np.asarray(L0_ffn_sh_gate, np.float32)),
        "w_shu": np.ascontiguousarray(np.asarray(L0_ffn_sh_up, np.float32)),
        "w_shd": np.ascontiguousarray(np.asarray(L0_ffn_sh_down, np.float32)),
        "w_rtg": np.ascontiguousarray(np.asarray(L0_ffn_rt_gate, np.float32)),
        "w_rtu": np.ascontiguousarray(np.asarray(L0_ffn_rt_up, np.float32)),
        "w_rtd": np.ascontiguousarray(np.asarray(L0_ffn_rt_down, np.float32)),
        "ident": _IDENT,
        "cvec": _CVEC,
    }
    nc = build_program()
    in_maps = []
    for c in range(NCORES):
        m = dict(wts)
        m["x"] = np.ascontiguousarray(xf[c * BPC:(c + 1) * BPC])
        in_maps.append(m)
    res = run_bass_kernel_spmd(nc, in_maps, core_ids=list(range(NCORES)))
    return np.concatenate([r["out"] for r in res.results], axis=0)


# version 12
# speedup vs baseline: 1.0061x; 1.0061x over previous
import numpy as np
import ml_dtypes
from contextlib import ExitStack

import concourse.bass as bass
import concourse.mybir as mybir
from concourse.bass_utils import run_bass_kernel_spmd
from concourse.bass import IndirectOffsetOnAxis as IOX

F32 = mybir.dt.float32
BF16 = mybir.dt.bfloat16
AF = mybir.ActivationFunctionType
ALU = mybir.AluOpType
AX = mybir.AxisListType

NCORES = 8
B_ALL, T, D = 16, 2048, 1024
BPC = B_ALL // NCORES
NT = T // 128
ND = D // 128
IFF = 2048
NE = 8
R = 32
CAP = 384
EPS = 1e-5

ENGS = ("t", "v", "s", "q", "g")


class Sem:
    __slots__ = ("h", "n")

    def __init__(self, h):
        self.h = h
        self.n = 0


class MK:

    def __init__(self, nc, es, dbg=None, max_ph=99):
        self.nc = nc
        self.es = es
        self.code = {e: [] for e in ENGS}
        self.dbg = dbg or {}
        self.max_ph = max_ph
        self.mute = False
        self.tc = self.sem("tc")
        self.vc = self.sem("vc")
        self.sc = self.sem("sc")
        self.qd = self.sem("qd")
        self.gd = self.sem("gd")
        self.fs = self.sem("fs")
        self.qx = [self.sem("qx0"), self.sem("qx1")]
        self.qs = [self.sem("qs0"), self.sem("qs1")]
        self.qc = [self.sem("qc0"), self.sem("qc1")]
        self.qo = [self.sem("qo0"), self.sem("qo1")]
        self.qa = self.sem("qa")
        self.gw = [self.sem("gw0"), self.sem("gw1")]
        self.gwd = self.sem("gwd")
        self.gc = self.sem("gc")
        self.qy = self.sem("qy")
        self.gsc = [self.sem("gsc0"), self.sem("gsc1")]
        self.ggr = [self.sem("ggr0"), self.sem("ggr1")]
        self.qz = self.sem("qz")
        self.qwb = [self.sem("qwb0"), self.sem("qwb1")]
        self.qwbd = self.sem("qwbd")
        self.qxs = [self.sem("qxs0"), self.sem("qxs1"), self.sem("qxs2")]
        self.qgw = [self.sem("qgw0"), self.sem("qgw1")]
        self.qwd = self.sem("qwd")
        self.cv_sh = self.sem("cv_sh")
        self.cv_ex = [self.sem(f"cv_ex{i}") for i in range(NE)]
        self.dma_sems = [self.qd, self.gd, self.qa, self.gwd, self.qy,
                         self.qz, self.qwbd, self.qwd] + \
            self.qwb + self.qxs + \
            self.qx + self.qs + self.qc + self.qo + self.gw + self.gsc + \
            self.ggr + self.qgw
        self.final_sems = self.dma_sems + [self.cv_sh] + self.cv_ex

    def sem(self, name):
        return Sem(self.es.enter_context(self.nc.semaphore(name)))

    def phase(self, n):
        self.mute = n > self.max_ph

    def emit(self, eng, fn):
        if self.mute:
            return
        self.code[eng].append(fn)

    def wait(self, eng, sem, val=None):
        v = sem.n if val is None else val
        if v <= 0:
            return
        self.emit(eng, lambda E, h=sem.h, v=v: E.wait_ge(h, v))

    def ins(self, eng, build, inc=None, k=1):
        if self.mute:
            return
        if inc is None:
            self.emit(eng, lambda E, b=build: b(E))
        else:
            inc.n += k
            self.emit(eng, lambda E, b=build, h=inc.h, k=k: b(E).then_inc(h, k))

    def dma(self, eng_q, out, in_, sem=None, **kw):
        if sem is None:
            sem = self.qd if eng_q == "q" else self.gd
        if self.mute:
            return sem.n
        sem.n += 16
        self.emit(eng_q, lambda E, o=out, i=in_, h=sem.h, kw=kw:
                  E.dma_start(out=o, in_=i, **kw).then_inc(h, 16))
        return sem.n

    def fence(self):
        if self.mute:
            return
        tgts = [(s, s.n) for s in [self.tc, self.vc, self.sc, self.gc] + self.dma_sems]
        self.fs.n += len(ENGS)
        tgt = self.fs.n
        for e in ENGS:
            for s, v in tgts:
                self.wait(e, s, v)
            self.emit(e, lambda E, h=self.fs.h: E.sem_inc(h, 1))
            self.emit(e, lambda E, h=self.fs.h, v=tgt: E.wait_ge(h, v))

    def tap(self, name, sb_ap):
        if name not in self.dbg:
            return
        for s in (self.tc, self.vc, self.sc):
            self.wait("g", s)
        self.dma("g", self.dbg[name][:, :], sb_ap)
        self.fence()

    def replay(self):
        nc = self.nc
        with nc.Block() as block:
            @block.tensor
            def _(E):
                for fn in self.code["t"]:
                    fn(E)

            @block.vector
            def _(E):
                for fn in self.code["v"]:
                    fn(E)

            @block.scalar
            def _(E):
                for fn in self.code["s"]:
                    fn(E)

            @block.sync
            def _(E):
                for fn in self.code["q"]:
                    fn(E)

            @block.gpsimd
            def _(E):
                for fn in self.code["g"]:
                    fn(E)


def build_program(dbg_names=(), max_ph=99, branch_filter=None, detect_races=True, nbat=BPC):
    nc = bass.Bass("TRN2", target_bir_lowering=False, detect_race_conditions=detect_races)
    es = ExitStack()
    U8 = mybir.dt.uint8

    x = nc.dram_tensor("x", [BPC, T, D], F32, kind="ExternalInput").ap()
    w_u = nc.dram_tensor("w_u", [D, R], F32, kind="ExternalInput").ap()
    w_gate = nc.dram_tensor("w_gate", [D, NE], F32, kind="ExternalInput").ap()
    w_shg = nc.dram_tensor("w_shg", [D, IFF], F32, kind="ExternalInput").ap()
    w_shu = nc.dram_tensor("w_shu", [D, IFF], F32, kind="ExternalInput").ap()
    w_shd = nc.dram_tensor("w_shd", [IFF, D], F32, kind="ExternalInput").ap()
    w_rtg = nc.dram_tensor("w_rtg", [NE, D, IFF], F32, kind="ExternalInput").ap()
    w_rtu = nc.dram_tensor("w_rtu", [NE, D, IFF], F32, kind="ExternalInput").ap()
    w_rtd = nc.dram_tensor("w_rtd", [NE, IFF, D], F32, kind="ExternalInput").ap()
    ident_in = nc.dram_tensor("ident", [128, 3 * 128], BF16, kind="ExternalInput").ap()
    cvec_in = nc.dram_tensor("cvec", [128, 16], F32, kind="ExternalInput").ap()
    out = nc.dram_tensor("out", [BPC, T, D], F32, kind="ExternalOutput").ap()
    scr_s = nc.dram_tensor("scr_s", [T, D], F32, kind="Internal").ap()
    scr_xsc = nc.dram_tensor("scr_xsc", [NE * CAP, D], BF16, kind="Internal").ap()
    scr_y = nc.dram_tensor("scr_y", [NE * CAP, D], F32, kind="Internal").ap()
    scr_wg = nc.dram_tensor("scr_wg", [9, D, IFF], BF16, kind="Internal").ap()
    scr_wu = nc.dram_tensor("scr_wu", [9, D, IFF], BF16, kind="Internal").ap()
    scr_wdn = nc.dram_tensor("scr_wdn", [9, IFF, D], BF16, kind="Internal").ap()

    dbg = {}
    for nm, shp, dt in [
        ("dbg_xnt", [128, ND * T], F32), ("dbg_logits", [128, NT * NE], F32),
        ("dbg_score", [128, NT], F32), ("dbg_mask", [128, NT * NE], F32),
        ("dbg_hi", [128, NT * D], F32), ("dbg_lo", [128, NT * D], F32),
        ("dbg_yb", [32, T], F32), ("dbg_n", [1, T], F32),
        ("dbg_z", [128, NT], F32), ("dbg_acc", [128, NT * D], F32),
        ("dbg_xst", [128, ND * T], F32),
    ]:
        if nm in dbg_names:
            dbg[nm] = nc.dram_tensor(nm, shp, dt, kind="ExternalOutput").ap()

    mk = MK(nc, es, dbg, max_ph=max_ph)
    tc, vc, sc, qd, gd = mk.tc, mk.vc, mk.sc, mk.qd, mk.gd
    breg = es.enter_context(nc.gpsimd.register("breg"))
    mk.emit("g", lambda E: E.reg_mov(breg, NE * CAP - 1))

    identb3 = es.enter_context(nc.sbuf_tensor("identb", [128, 3 * 128], BF16))
    ident = identb3[:, 0:128]
    tril_b = identb3[:, 128:256]
    ones_b = identb3[:, 256:384]
    cvec = es.enter_context(nc.sbuf_tensor("cvecb", [128, 16], F32))
    u_bf = es.enter_context(nc.sbuf_tensor("u_bf", [128, ND * R], BF16))
    gw_hi = es.enter_context(nc.sbuf_tensor("gw_hi", [128, ND * NE], BF16))
    gw_lo = es.enter_context(nc.sbuf_tensor("gw_lo", [128, ND * NE], BF16))
    gw_f = es.enter_context(nc.sbuf_tensor("gw_f", [128, ND * NE], F32))
    gw_t = es.enter_context(nc.sbuf_tensor("gw_t", [128, ND * NE], F32))
    logits_sb = es.enter_context(nc.sbuf_tensor("logits_sb", [128, NT * NE], F32))
    tv_sb = es.enter_context(nc.sbuf_tensor("tv_sb", [128, NT], F32))
    score_sb = es.enter_context(nc.sbuf_tensor("score_sb", [128, NT], F32))
    mask_sb = es.enter_context(nc.sbuf_tensor("mask_sb", [128, NT * NE], F32))
    zacc = es.enter_context(nc.sbuf_tensor("zacc", [128, NT], F32))
    zscr = es.enter_context(nc.sbuf_tensor("zscr", [128, NT], F32))
    zrec = es.enter_context(nc.sbuf_tensor("zrec", [128, NT], F32))
    silu_g = [es.enter_context(nc.sbuf_tensor(f"silu_g{p}", [128, 512], F32)) for p in range(4)]
    mcol = es.enter_context(nc.sbuf_tensor("mcol", [128, 4], F32))
    mask_bf = es.enter_context(nc.sbuf_tensor("mask_bf", [128, NT * NE], BF16))
    pos_sb = es.enter_context(nc.sbuf_tensor("pos_sb", [128, NT * NE], F32))
    sidx_sb = es.enter_context(nc.sbuf_tensor("sidx_sb", [128, NT], mybir.dt.uint32))
    tmp8a = es.enter_context(nc.sbuf_tensor("tmp8a", [128, NE], F32))
    tmp8b = es.enter_context(nc.sbuf_tensor("tmp8b", [128, NE], F32))
    ridx_f = es.enter_context(nc.sbuf_tensor("ridx_f", [128, 4], F32))
    zero_tile = es.enter_context(nc.sbuf_tensor("zero_tile", [128, D], BF16))
    ssq = es.enter_context(nc.sbuf_tensor("ssq", [128, 4], F32))
    rstd = es.enter_context(nc.sbuf_tensor("rstd", [128, 4], F32))

    AR_BYTES = 196608
    arena = es.enter_context(nc.sbuf_tensor("arena", [128, AR_BYTES], U8))
    arf = arena.bitcast(F32)
    arb = arena.bitcast(BF16)

    def vf(off, n):
        return arf[:, off // 4: off // 4 + n]

    def vb(off, n):
        return arb[:, off // 2: off // 2 + n]

    dmaj = vb(0, ND * T)
    dmaj3 = dmaj.rearrange("p (c t) -> p c t", c=ND)
    hi_tm = vb(32768, NT * D)
    lo_tm = vb(65536, NT * D)
    xst = vf(98304, 2 * D)
    tm32a = vf(106496, 2 * D)
    tm32b = vf(114688, 2 * D)
    loch = vb(122880, 2 * D)
    auga = vb(126976, T)[0:36]
    augb = vb(131072, T)[0:36]
    ybf = vb(135168, T)[0:32]
    ysq = vf(139264, 512)[0:32]
    nrow = vf(141312, T)[0:1]
    nt1 = vf(149504, T)[0:1]
    nnhi = vb(157696, T)[0:1]
    nnlo = vb(161792, T)[0:1]
    ones_row = vb(165888, T)[0:1]
    qcol = vb(169984, NT * 512)
    sstg = vf(186368, 2 * D)
    xsb = vb(169984, 2 * D)
    acc = vf(32768, NT * D)
    actt = vb(98304, ND * T)
    wgb = vb(131072, 2 * ND * 512)
    wub = vb(147456, 2 * ND * 512)
    wdb = vb(163840, 16 * 1024)
    cxt = vf(98304, 2 * D)
    csn = vf(106496, 2 * D)
    cgt = vf(114688, 2 * D)
    cot = vf(122880, 2 * D)

    psA = es.enter_context(nc.psum_tensor("psA", [128, 4096], F32))
    psAb = psA.bitcast(BF16)
    ptrh = psAb[:, 0:2048]
    ptrl = psAb[:, 2048:4096]
    plg = psA[:, 2048:2048 + NE]
    pyp = psA[:, 2560:3072][0:32]
    pnp = psA[:, 3072:3584][0:1]
    pq = psA[:, 0:2048]
    ps = psA[:, 2048:4096]
    ptx = psAb[:, 0:2048]
    pgu = psA[:, 0:4096]

    mk.dma("q", identb3[:, :], ident_in[:, :])
    mk.dma("q", cvec[:, :], cvec_in[:, :])
    mk.dma("q", gw_f.rearrange("p (c e) -> p c e", c=ND),
           w_gate.rearrange("(c p) e -> p c e", p=128))
    for c in range(ND):
        mk.dma("g", gw_hi[:, c * NE:(c + 1) * NE], w_gate[c * 128:(c + 1) * 128, :])
        mk.dma("g", u_bf[:, c * R:(c + 1) * R], w_u[c * 128:(c + 1) * 128, :])
    mk.dma("g", scr_wg[8], w_shg[:, :], sem=mk.cv_sh, max_dma_last_dim=8192)
    mk.dma("g", scr_wu[8], w_shu[:, :], sem=mk.cv_sh, max_dma_last_dim=8192)
    mk.dma("g", scr_wdn[8], w_shd[:, :], sem=mk.cv_sh, max_dma_last_dim=8192)
    for e_ in range(NE):
        mk.dma("g", scr_wg[e_], w_rtg[e_], sem=mk.cv_ex[e_], max_dma_last_dim=8192)
        mk.dma("g", scr_wu[e_], w_rtu[e_], sem=mk.cv_ex[e_], max_dma_last_dim=8192)
        mk.dma("g", scr_wdn[e_], w_rtd[e_], sem=mk.cv_ex[e_], max_dma_last_dim=8192)
    mk.wait("v", gd)
    mk.wait("v", qd)
    mk.ins("v", lambda E: E.tensor_copy(gw_t[:, :], gw_hi[:, :]), inc=vc)
    mk.wait("v", vc)
    mk.ins("v", lambda E: E.tensor_tensor(gw_t[:, :], gw_f[:, :], gw_t[:, :], ALU.subtract), inc=vc)
    mk.wait("v", vc)
    mk.ins("v", lambda E: E.tensor_copy(gw_lo[:, :], gw_t[:, :]), inc=vc)
    mk.ins("v", lambda E: E.memset(zero_tile[:, :], 0.0), inc=vc)
    zf = vc.n
    mk.wait("q", vc, zf)
    for zi in range((NE * CAP) // 128):
        mk.dma("q", scr_xsc[zi * 128:(zi + 1) * 128, :], zero_tile[:, :], sem=mk.qz)

    branches = [("e", e, w_rtg, w_rtu, w_rtd) for e in range(NE)] + \
               [("sh", None, w_shg, w_shu, w_shd)]
    if branch_filter is not None:
        branches = [br for br in branches
                    if (br[0] == "sh" and "sh" in branch_filter) or
                       (br[0] == "e" and br[1] in branch_filter)]

    for b in range(nbat):
        mk.phase(1)
        xst_free = [0, 0]
        xnf_free = [0, 0]
        cent_free = [0, 0]
        ptrh_free = [0, 0]
        ptrl_free = [0, 0]
        loch_free = [0, 0]
        lg_free = 0
        pend_tv = []

        def emit_tv(kk, lgv):
            mk.wait("v", sc, lgv)
            mk.ins("v", lambda E, o=tv_sb[:, kk:kk + 1],
                   i=logits_sb[:, kk * NE:(kk + 1) * NE]:
                   E.reduce_max(o, i, axis=AX.X), inc=vc)
            mk.wait("v", vc)
            mk.ins("v", lambda E, o=mask_sb[:, kk * NE:(kk + 1) * NE],
                   i=logits_sb[:, kk * NE:(kk + 1) * NE], s_=tv_sb[:, kk:kk + 1]:
                   E.tensor_scalar(out=o, in0=i, scalar1=s_, scalar2=None,
                                   op0=ALU.is_equal), inc=vc)

        for k in range(NT):
            p = k % 2
            if len(pend_tv) >= 2:
                emit_tv(*pend_tv.pop(0))
            xs_ = xst[:, p * D:(p + 1) * D]
            cent = tm32a[:, p * D:(p + 1) * D]
            xnf = tm32b[:, p * D:(p + 1) * D]
            hi_sl = hi_tm[:, k * D:(k + 1) * D]
            lo_sl = lo_tm[:, k * D:(k + 1) * D]
            mk.wait("q", vc, xst_free[p])
            x_ld = mk.dma("q", xs_, x[b, k * 128:(k + 1) * 128, :], sem=mk.qx[p])
            mk.wait("v", mk.qx[p], x_ld)
            mk.ins("v", lambda E, o=mcol[:, p:p + 1], i=xs_:
                   E.reduce_sum(o, i, axis=AX.X), inc=vc)
            mk.wait("v", vc)
            mk.ins("v", lambda E, o=mcol[:, p:p + 1]: E.tensor_scalar(
                out=o, in0=o, scalar1=-1.0 / D, scalar2=None, op0=ALU.mult), inc=vc)
            mk.wait("v", vc)
            mk.wait("v", sc, cent_free[p])
            mk.ins("v", lambda E, o=cent, i=xs_, m=mcol[:, p:p + 1]: E.tensor_scalar(
                out=o, in0=i, scalar1=m, scalar2=None, op0=ALU.add), inc=vc)
            xst_free[p] = vc.n
            mk.wait("s", vc)
            mk.wait("s", vc, xnf_free[p])
            mk.ins("s", lambda E, o=xnf, i=cent, a=ssq[:, p:p + 1]: E.activation(
                out=o, in_=i, func=AF.Square, accum_out=a), inc=sc)
            mk.wait("s", sc)
            mk.ins("s", lambda E, o=rstd[:, p:p + 1], i=ssq[:, p:p + 1]: E.activation(
                out=o, in_=i, func=AF.Ln, scale=1.0 / D, bias=cvec[:, 0:1]), inc=sc)
            mk.wait("s", sc)
            mk.ins("s", lambda E, o=rstd[:, p:p + 1], i=rstd[:, p:p + 1]: E.activation(
                out=o, in_=i, func=AF.Exp, scale=-0.5), inc=sc)
            mk.wait("s", sc)
            mk.ins("s", lambda E, o=xnf, i=cent, r=rstd[:, p:p + 1]: E.activation(
                out=o, in_=i, func=AF.Copy, scale=r), inc=sc)
            cent_free[p] = sc.n
            mk.wait("v", sc)
            mk.ins("v", lambda E, o=hi_sl, i=xnf: E.tensor_copy(o, i), inc=vc)
            mk.wait("v", vc)
            mk.ins("v", lambda E, o=cent, i=hi_sl: E.tensor_copy(o, i), inc=vc)
            mk.wait("v", vc)
            mk.ins("v", lambda E, o=lo_sl, a=xnf, c_=cent: E.tensor_tensor(
                o, a, c_, ALU.subtract), inc=vc)
            xnf_free[p] = vc.n
            hilo_done = vc.n
            mk.wait("t", vc, hilo_done)
            mk.wait("t", sc, ptrh_free[p])
            for c in range(ND):
                mk.ins("t", lambda E,
                       o=ptrh[:, p * 1024 + c * 128: p * 1024 + (c + 1) * 128],
                       i=hi_sl[:, c * 128:(c + 1) * 128], d=ident[:, :]:
                       E.transpose(o, i, d), inc=(tc if c == ND - 1 else None))
            trh_done = tc.n
            mk.wait("t", sc, ptrl_free[p])
            for c in range(ND):
                mk.ins("t", lambda E,
                       o=ptrl[:, p * 1024 + c * 128: p * 1024 + (c + 1) * 128],
                       i=lo_sl[:, c * 128:(c + 1) * 128], d=ident[:, :]:
                       E.transpose(o, i, d), inc=(tc if c == ND - 1 else None))
            trl_done = tc.n
            mk.wait("s", tc, trh_done)
            mk.ins("s", lambda E, o=dmaj3[:, :, k * 128:(k + 1) * 128],
                   i=ptrh.rearrange("p (q c t) -> p q c t", q=2, c=ND)[:, p]:
                   E.copy(o, i), inc=sc)
            ptrh_free[p] = sc.n
            mk.wait("s", tc, trl_done)
            mk.wait("s", tc, loch_free[p])
            mk.ins("s", lambda E, o=loch[:, p * 1024:(p + 1) * 1024],
                   i=ptrl[:, p * 1024:(p + 1) * 1024]: E.copy(o, i), inc=sc)
            ptrl_free[p] = sc.n
            evac_done = sc.n
            mk.wait("t", sc, evac_done)
            nmm = 3 * ND
            j = 0
            for pass_i in range(3):
                wsb = gw_hi if pass_i != 1 else gw_lo
                for c in range(ND):
                    if pass_i < 2:
                        lh = dmaj[:, c * T + k * 128: c * T + (k + 1) * 128]
                    else:
                        lh = loch[:, p * 1024 + c * 128: p * 1024 + (c + 1) * 128]
                    mk.ins("t", lambda E, o=plg[:, :], l=lh,
                           r=wsb[:, c * NE:(c + 1) * NE],
                           st=(j == 0), sp=(j == nmm - 1):
                           E.matmul(o, l, r, start=st, stop=sp),
                           inc=(tc if j == nmm - 1 else None))
                    j += 1
            loch_free[p] = tc.n
            mk.wait("s", tc)
            mk.ins("s", lambda E, o=logits_sb[:, k * NE:(k + 1) * NE],
                   i=plg[:, :]: E.copy(o, i), inc=sc)
            lg_free = sc.n
            pend_tv.append((k, sc.n))
        for pt in pend_tv:
            emit_tv(*pt)
        pend_tv = []
        mk.tap("dbg_hi", hi_tm[:, :])
        mk.tap("dbg_lo", lo_tm[:, :])
        mk.tap("dbg_xnt", dmaj[:, :])
        mk.tap("dbg_logits", logits_sb[:, :])
        mk.tap("dbg_mask", mask_sb[:, :])
        mk.fence()

        mk.phase(2)
        mk.ins("v", lambda E: E.memset(ones_row[:, :], 1.0), inc=vc)
        for js in range(4):
            for c in range(ND):
                mk.ins("t", lambda E, o=pyp[:, :], l=u_bf[:, c * R:(c + 1) * R],
                       r=dmaj[:, c * T + js * 512: c * T + js * 512 + 512],
                       st=(c == 0), sp=(c == ND - 1):
                       E.matmul(o, l, r, start=st, stop=sp),
                       inc=(tc if c == ND - 1 else None))
            mk.wait("s", tc)
            mk.ins("s", lambda E, o=ybf[:, js * 512:(js + 1) * 512],
                   i=pyp[:, :]: E.copy(o, i), inc=sc)
            mk.wait("s", sc)
            mk.ins("s", lambda E, o=ysq[:, :],
                   i=ybf[:, js * 512:(js + 1) * 512]:
                   E.activation(out=o, in_=i, func=AF.Square), inc=sc)
            mk.wait("t", sc)
            mk.ins("t", lambda E, o=pnp[:, :], l=cvec[0:32, 1:2],
                   r=ysq[:, :]:
                   E.matmul(o, l, r, start=True, stop=True), inc=tc)
            mk.wait("s", tc)
            mk.ins("s", lambda E, o=nrow[:, js * 512:(js + 1) * 512],
                   i=pnp[:, :]: E.copy(o, i), inc=sc)
            mk.wait("t", sc)
        mk.wait("v", sc)
        mk.ins("v", lambda E: E.tensor_copy(nnhi[:, :], nrow[:, :]), inc=vc)
        mk.wait("v", vc)
        mk.ins("v", lambda E: E.tensor_copy(nt1[:, :], nnhi[:, :]), inc=vc)
        mk.wait("v", vc)
        mk.ins("v", lambda E: E.tensor_scalar(out=nnhi[:, :], in0=nnhi[:, :],
               scalar1=-1.0, scalar2=None, op0=ALU.mult), inc=vc)
        mk.wait("v", vc)
        mk.ins("v", lambda E: E.tensor_tensor(nt1[:, :], nrow[:, :], nt1[:, :],
                                              ALU.subtract), inc=vc)
        mk.wait("v", vc)
        mk.ins("v", lambda E: E.tensor_scalar(out=nnlo[:, :], in0=nt1[:, :],
               scalar1=-1.0, scalar2=None, op0=ALU.mult), inc=vc)
        mk.wait("v", vc)
        mk.ins("v", lambda E: E.tensor_scalar(out=auga[0:32, :], in0=ybf[:, :],
               scalar1=2.0, scalar2=None, op0=ALU.mult), inc=vc)
        mk.wait("v", vc)
        mk.ins("v", lambda E: E.tensor_copy(augb[0:32, :], ybf[:, :]), inc=vc)
        mk.wait("q", vc)
        mk.dma("q", auga[32:33, :], nnhi[:, :], sem=mk.qa)
        mk.dma("q", auga[33:34, :], nnlo[:, :], sem=mk.qa)
        mk.dma("q", auga[34:35, :], ones_row[:, :], sem=mk.qa)
        mk.dma("q", auga[35:36, :], ones_row[:, :], sem=mk.qa)
        mk.dma("q", augb[32:33, :], ones_row[:, :], sem=mk.qa)
        mk.dma("q", augb[33:34, :], ones_row[:, :], sem=mk.qa)
        mk.dma("q", augb[34:35, :], nnhi[:, :], sem=mk.qa)
        aug_ld = mk.dma("q", augb[35:36, :], nnlo[:, :], sem=mk.qa)
        mk.tap("dbg_yb", ybf[:, :])
        mk.tap("dbg_n", nrow[:, :])
        mk.wait("t", mk.qa, aug_ld)
        ppos = psA[:, 3584:3584 + NE]
        mk.wait("v", vc)
        mk.ins("v", lambda E: E.tensor_copy(mask_bf[:, :], mask_sb[:, :]), inc=vc)
        mbf_done = vc.n
        pos_ev = 0
        for I in range(NT):
            mk.wait("t", vc, mbf_done)
            mk.wait("t", sc, pos_ev)
            for Jj in range(I + 1):
                lh = ones_b if Jj < I else tril_b
                mk.ins("t", lambda E, o=ppos[:, :], l=lh,
                       r=mask_bf[:, Jj * NE:(Jj + 1) * NE],
                       st=(Jj == 0), sp=(Jj == I):
                       E.matmul(o, l, r, start=st, stop=sp),
                       inc=(tc if Jj == I else None))
            mk.wait("s", tc)
            mk.ins("s", lambda E, o=pos_sb[:, I * NE:(I + 1) * NE], i=ppos[:, :]:
                   E.copy(o, i), inc=sc)
            pos_ev = sc.n
        mk.wait("v", sc, pos_ev)
        for I in range(NT):
            mk.ins("v", lambda E, o=tmp8a[:, :], a=mask_sb[:, I * NE:(I + 1) * NE],
                   b_=pos_sb[:, I * NE:(I + 1) * NE]:
                   E.tensor_tensor(o, a, b_, ALU.mult), inc=vc)
            mk.wait("v", vc)
            mk.ins("v", lambda E, o=ridx_f[:, 0:1], i=tmp8a[:, :]:
                   E.reduce_sum(o, i, axis=AX.X), inc=vc)
            mk.ins("v", lambda E, o=tmp8b[:, :], a=mask_sb[:, I * NE:(I + 1) * NE],
                   b_=cvec[:, 2:10]:
                   E.tensor_tensor(o, a, b_, ALU.mult), inc=vc)
            mk.wait("v", vc)
            mk.ins("v", lambda E, o=ridx_f[:, 1:2], i=tmp8b[:, :]:
                   E.reduce_sum(o, i, axis=AX.X), inc=vc)
            mk.wait("v", vc)
            mk.ins("v", lambda E, o=ridx_f[:, 0:1]: E.tensor_scalar(
                out=o, in0=o, scalar1=-1.0, scalar2=None, op0=ALU.add), inc=vc)
            mk.wait("v", vc)
            mk.ins("v", lambda E, o=ridx_f[:, 2:3], i=ridx_f[:, 1:2],
                   s2=ridx_f[:, 0:1]: E.scalar_tensor_tensor(
                   out=o, in0=i, scalar=float(CAP), in1=s2,
                   op0=ALU.mult, op1=ALU.add), inc=vc)
            mk.wait("v", vc)
            mk.ins("v", lambda E, o=ridx_f[:, 3:4], i=ridx_f[:, 0:1]:
                   E.tensor_scalar(out=o, in0=i, scalar1=float(CAP), scalar2=None,
                                   op0=ALU.is_ge), inc=vc)
            mk.wait("v", vc)
            mk.ins("v", lambda E, o=ridx_f[:, 2:3], i=ridx_f[:, 3:4],
                   s2=ridx_f[:, 2:3]: E.scalar_tensor_tensor(
                   out=o, in0=i, scalar=1.0e6, in1=s2,
                   op0=ALU.mult, op1=ALU.add), inc=vc)
            mk.wait("v", vc)
            mk.ins("v", lambda E, o=sidx_sb[:, I:I + 1], i=ridx_f[:, 2:3]:
                   E.tensor_copy(o, i), inc=vc)
            mk.wait("v", vc)
        mk.fence()

        mk.phase(3)
        mk.wait("v", vc)
        mk.ins("v", lambda E: E.memset(zacc[:, :], 0.0), inc=vc)
        mk.wait("v", vc)
        pq_free = [0, 0, 0, 0]
        ps_free = [0, 0, 0, 0]
        qcol_free = 0
        sstg_free = [0, 0]
        zprev = 0
        for gi in range(4):
            mk.wait("s", tc, qcol_free)
            mk.wait("s", vc, zprev)
            for J in range(NT):
                bank = J % 4
                mk.wait("t", sc, pq_free[bank])
                mk.ins("t", lambda E, o=pq[:, bank * 512:(bank + 1) * 512],
                       l=auga[:, J * 128:(J + 1) * 128],
                       r=augb[:, gi * 512:(gi + 1) * 512]:
                       E.matmul(o, l, r, start=True, stop=True), inc=tc)
                mk.wait("s", tc)
                mk.ins("s", lambda E, o=qcol[:, J * 512:(J + 1) * 512],
                       i=pq[:, bank * 512:(bank + 1) * 512],
                       a=zscr[:, J:J + 1]: E.activation(
                       out=o, in_=i, func=AF.Exp, accum_out=a), inc=sc)
                pq_free[bank] = sc.n
            exp_done = sc.n
            mk.wait("v", sc, exp_done)
            mk.ins("v", lambda E: E.tensor_tensor(zacc[:, :], zacc[:, :],
                   zscr[:, :], ALU.add), inc=vc)
            zprev = vc.n
            mk.wait("t", sc, exp_done)
            for il in range(4):
                I = gi * 4 + il
                bk0, bk1 = 2 * (il % 2), 2 * (il % 2) + 1
                for dh, bank in ((0, bk0), (1, bk1)):
                    mk.wait("t", vc, ps_free[bank])
                    for J in range(NT):
                        mk.ins("t", lambda E,
                               o=ps[:, bank * 512:(bank + 1) * 512],
                               l=qcol[:, J * 512 + il * 128: J * 512 + (il + 1) * 128],
                               r=hi_tm[:, J * D + dh * 512: J * D + dh * 512 + 512],
                               st=(J == 0), sp=(J == NT - 1):
                               E.matmul(o, l, r, start=st, stop=sp),
                               inc=(tc if J == NT - 1 else None))
                smm_done = tc.n
                sp_ = il % 2
                mk.wait("v", tc, smm_done)
                mk.wait("v", mk.qs[sp_], sstg_free[sp_])
                mk.ins("v", lambda E, o=sstg[:, sp_ * D: sp_ * D + 512],
                       i=ps[:, bk0 * 512:(bk0 + 1) * 512]:
                       E.tensor_copy(o, i), inc=vc)
                mk.wait("v", vc)
                mk.ins("v", lambda E, o=sstg[:, sp_ * D + 512: (sp_ + 1) * D],
                       i=ps[:, bk1 * 512:(bk1 + 1) * 512]:
                       E.tensor_copy(o, i), inc=vc)
                ps_free[bk0] = vc.n
                ps_free[bk1] = vc.n
                mk.wait("q", vc)
                sstg_free[sp_] = mk.dma(
                    "q", scr_s[I * 128:(I + 1) * 128, :],
                    sstg[:, sp_ * D:(sp_ + 1) * D], sem=mk.qs[sp_])
            qcol_free = tc.n
        mk.tap("dbg_z", zacc[:, :])
        mk.fence()

        mk.phase(4)
        mk.wait("s", vc)
        mk.ins("s", lambda E: E.activation(out=score_sb[:, :], in_=tv_sb[:, :],
                                           func=AF.Sigmoid), inc=sc)
        sc_score = sc.n
        if b == 0:
            mk.wait("g", mk.qz)
        xsb_free = [0, 0]
        for k in range(NT):
            p = k % 2
            tmp = tm32a[:, p * D:(p + 1) * D]
            mk.wait("v", sc, sc_score)
            mk.ins("v", lambda E, o=tmp, a=hi_tm[:, k * D:(k + 1) * D],
                   b_=lo_tm[:, k * D:(k + 1) * D]:
                   E.tensor_tensor(o, a, b_, ALU.add), inc=vc)
            mk.wait("v", vc)
            mk.wait("v", mk.gsc[p], xsb_free[p])
            mk.ins("v", lambda E, o=xsb[:, p * D:(p + 1) * D], i=tmp,
                   s_=score_sb[:, k:k + 1]: E.tensor_scalar(
                   out=o, in0=i, scalar1=s_, scalar2=None, op0=ALU.mult), inc=vc)
            xs_done = vc.n
            mk.wait("g", vc, xs_done)
            if not mk.mute:
                mk.gsc[p].n += 16
            mk.emit("g", lambda E, i=xsb[:, p * D:(p + 1) * D],
                    ix=sidx_sb[:, k:k + 1], h=mk.gsc[p].h:
                    E.indirect_dma_start(out=scr_xsc[:, :],
                                         out_offset=IOX(ap=ix, axis=0),
                                         in_=i, in_offset=None,
                                         bounds_check=breg,
                                         oob_is_err=False).then_inc(h, 16))
            xsb_free[p] = mk.gsc[p].n
        mk.fence()

        mk.phase(5)
        actt_sh = vb(98304, ND * T)
        xsT_e3 = [vb(98304 + p * 6144, ND * CAP).rearrange("p_ (c s) -> p_ c s", c=ND)
                  for p in range(2)]
        acte = vb(110592, NT * CAP)
        wg_free = [0, 0]
        wgwb_free = [0, 0]
        wd_free = 0
        wdwb_free = 0
        dn_free = [0, 0, 0, 0]
        gbank_sfree = [0] * 4
        ubank_vfree = [0] * 4
        silu_free = [0, 0, 0, 0]
        spi = 0

        def load_wgu(br_i, i0, pw):
            for c in range(ND):
                mk.dma("q", wgb[:, (pw * ND + c) * 512:(pw * ND + c + 1) * 512],
                       scr_wg[br_i, c * 128:(c + 1) * 128, i0:i0 + 512], sem=mk.qgw[pw])
                mk.dma("s", wub[:, (pw * ND + c) * 512:(pw * ND + c + 1) * 512],
                       scr_wu[br_i, c * 128:(c + 1) * 128, i0:i0 + 512], sem=mk.qgw[pw])
            return (pw, mk.qgw[pw].n)

        def load_wd(br_i, ic_list, dst_off):
            for j, ic in enumerate(ic_list):
                mk.dma("s" if j % 2 else "q",
                       wdb[:, dst_off + j * 1024: dst_off + (j + 1) * 1024],
                       scr_wdn[br_i, ic * 128:(ic + 1) * 128, :], sem=mk.qwd)
            return mk.qwd.n

        sh_i = 8
        do_shared = branch_filter is None or "sh" in branch_filter
        if do_shared:
            if b == 0:
                mk.wait("q", mk.cv_sh)
                mk.wait("s", mk.cv_sh)
            for half in range(2):
                wg_loaded = [None, None]
                for sp_i in range(2):
                    pw = spi % 2
                    mk.wait("q", tc, wg_free[pw])
                    mk.wait("s", tc, wg_free[pw])
                    i0 = half * 1024 + sp_i * 512
                    wg_loaded[sp_i] = load_wgu(sh_i, i0, pw)
                    spi += 1
                mk.wait("q", tc, wd_free)
                mk.wait("s", tc, wd_free)
                ics = list(range(half * 8, half * 8 + 8))
                wd_ld = load_wd(sh_i, ics, 0)
                for sp_i in range(2):
                    pw, _wgl = wg_loaded[sp_i]
                    mk.wait("t", mk.qgw[pw], _wgl)
                    for il in range(4):
                        ic = sp_i * 4 + il
                        for t4 in range(4):
                            mk.wait("t", sc, gbank_sfree[t4])
                            mk.wait("t", vc, dn_free[t4])
                        for c in range(ND):
                            for t4 in range(4):
                                mk.ins("t", lambda E,
                                       o=pgu[:, t4 * 512:(t4 + 1) * 512],
                                       l=wgb[:, (pw * ND + c) * 512 + il * 128:
                                              (pw * ND + c) * 512 + (il + 1) * 128],
                                       r=dmaj[:, c * T + t4 * 512: c * T + (t4 + 1) * 512],
                                       st=(c == 0), sp=(c == ND - 1):
                                       E.matmul(o, l, r, start=st, stop=sp),
                                       inc=(tc if (c == ND - 1 and t4 == 3) else None))
                        gmm_done = tc.n
                        for t4 in range(4):
                            mk.wait("t", vc, ubank_vfree[t4])
                        for c in range(ND):
                            for t4 in range(4):
                                mk.ins("t", lambda E,
                                       o=pgu[:, (4 + t4) * 512:(5 + t4) * 512],
                                       l=wub[:, (pw * ND + c) * 512 + il * 128:
                                              (pw * ND + c) * 512 + (il + 1) * 128],
                                       r=dmaj[:, c * T + t4 * 512: c * T + (t4 + 1) * 512],
                                       st=(c == 0), sp=(c == ND - 1):
                                       E.matmul(o, l, r, start=st, stop=sp),
                                       inc=(tc if (c == ND - 1 and t4 == 3) else None))
                        umm_done = tc.n
                        for t4 in range(4):
                            mk.wait("s", tc, gmm_done)
                            mk.wait("s", vc, silu_free[t4])
                            mk.ins("s", lambda E, o=silu_g[t4][:, :],
                                   i=pgu[:, t4 * 512:(t4 + 1) * 512]:
                                   E.activation(out=o, in_=i, func=AF.Silu), inc=sc)
                            gbank_sfree[t4] = sc.n
                            mk.wait("v", sc)
                            mk.wait("v", tc, umm_done)
                            mk.ins("v", lambda E,
                                   o=actt_sh[:, ic * T + t4 * 512: ic * T + (t4 + 1) * 512],
                                   i=pgu[:, (4 + t4) * 512:(5 + t4) * 512],
                                   s_=silu_g[t4][:, :]:
                                   E.tensor_tensor(o, i, s_, ALU.mult), inc=vc)
                            ubank_vfree[t4] = vc.n
                            silu_free[t4] = vc.n
                    wg_free[pw] = tc.n
                act_done = vc.n
                mk.wait("t", mk.qwd, wd_ld)
                mk.wait("t", vc, act_done)
                for tch in range(NT):
                    for dh in range(2):
                        bank = 2 * (tch % 2) + dh
                        mk.wait("t", vc, dn_free[bank])
                        mk.wait("t", sc, gbank_sfree[bank])
                        for j in range(8):
                            mk.ins("t", lambda E,
                                   o=pgu[:, bank * 512:(bank + 1) * 512],
                                   l=actt_sh[:, j * T + tch * 128:
                                             j * T + (tch + 1) * 128],
                                   r=wdb[:, j * 1024 + dh * 512: j * 1024 + dh * 512 + 512],
                                   st=(j == 0), sp=(j == 7):
                                   E.matmul(o, l, r, start=st, stop=sp),
                                   inc=(tc if j == 7 else None))
                        dn_done = tc.n
                        a_sl = acc[:, tch * D + dh * 512: tch * D + dh * 512 + 512]
                        mk.wait("v", tc, dn_done)
                        if half == 0:
                            mk.ins("v", lambda E, o=a_sl,
                                   i=pgu[:, bank * 512:(bank + 1) * 512]:
                                   E.tensor_copy(o, i), inc=vc)
                        else:
                            mk.ins("v", lambda E, o=a_sl,
                                   i=pgu[:, bank * 512:(bank + 1) * 512]:
                                   E.tensor_tensor(o, o, i, ALU.add), inc=vc)
                        dn_free[bank] = vc.n
                wd_free = tc.n
        else:
            mk.wait("v", vc)
            mk.ins("v", lambda E: E.memset(acc[:, :], 0.0), inc=vc)
            mk.wait("v", vc)

        xstg3 = vb(0, 3 * 1024)
        ystg2 = vf(8192, 512)
        NSC = CAP // 128
        exp_list = [e for e in range(NE)
                    if branch_filter is None or e in branch_filter]
        mk.wait("q", mk.gsc[0])
        mk.wait("q", mk.gsc[1])
        dmaj_free = tc.n
        xstg_free = [dmaj_free] * 3
        ptr_free = [0, 0]
        xsT_free = [0, 0]
        xsT_done = [0, 0]
        acte_free = 0
        ystg_free = 0
        dnb_free = [0] * 6
        ptr_e = psAb[:, 0:2048]

        def issue_xstg_loads(e):
            lds = []
            for sc_ in range(NSC):
                mk.wait("q", tc, xstg_free[sc_])
                lds.append(mk.dma(
                    "q", xstg3[:, sc_ * D: sc_ * D + D],
                    scr_xsc[e * CAP + sc_ * 128: e * CAP + (sc_ + 1) * 128, :],
                    sem=mk.qxs[sc_]))
            return lds

        def prep_xsT(pbuf, lds):
            for sc_ in range(NSC):
                pxp = sc_ % 2
                mk.wait("t", mk.qxs[sc_], lds[sc_])
                mk.wait("t", sc, ptr_free[pxp])
                mk.wait("t", sc, gbank_sfree[pxp])
                mk.wait("t", vc, ubank_vfree[pxp])
                mk.wait("t", vc, dn_free[pxp])
                mk.wait("t", tc, xsT_free[pbuf])
                for c in range(ND):
                    mk.ins("t", lambda E,
                           o=ptr_e[:, pxp * 1024 + c * 128: pxp * 1024 + (c + 1) * 128],
                           i=xstg3[:, sc_ * D + c * 128: sc_ * D + (c + 1) * 128],
                           d=ident[:, :]: E.transpose(o, i, d),
                           inc=(tc if c == ND - 1 else None))
                xstg_free[sc_] = tc.n
                mk.wait("s", tc)
                mk.wait("s", tc, xsT_free[pbuf])
                mk.ins("s", lambda E,
                       o=xsT_e3[pbuf][:, :, sc_ * 128:(sc_ + 1) * 128],
                       i=ptr_e.rearrange("p (q c s) -> p q c s", q=2, c=ND)[:, pxp]:
                       E.copy(o, i), inc=sc)
                ptr_free[pxp] = sc.n
            xsT_done[pbuf] = sc.n

        pend_w = None
        for idx, e in enumerate(exp_list):
            pbuf = idx % 2
            if idx == 0:
                lds0 = issue_xstg_loads(e)
                prep_xsT(pbuf, lds0)
            wg_loaded = [None, None, None, None]

            def load_span(sp_i, ee=None):
                nonlocal spi
                ee = e if ee is None else ee
                pw = spi % 2
                mk.wait("q", tc, wg_free[pw])
                mk.wait("s", tc, wg_free[pw])
                wg_loaded[sp_i] = load_wgu(ee, sp_i * 512, pw)
                spi += 1

            if pend_w is None:
                if b == 0:
                    mk.wait("q", mk.cv_ex[e])
                    mk.wait("s", mk.cv_ex[e])
                load_span(0)
                load_span(1)
            else:
                wg_loaded[0], wg_loaded[1] = pend_w
            mk.wait("q", tc, wd_free)
            mk.wait("s", tc, wd_free)
            wd_ld = load_wd(e, list(range(16)), 0)
            for sp_i in range(4):
                pw, _wgl = wg_loaded[sp_i]
                mk.wait("t", mk.qgw[pw], _wgl)
                mk.wait("t", sc, xsT_done[pbuf])
                for il in range(4):
                    ic = sp_i * 4 + il
                    gb = 2 * (il % 2)
                    ub = 2 * (il % 2) + 1
                    mk.wait("t", sc, gbank_sfree[gb])
                    mk.wait("t", vc, ubank_vfree[ub])
                    mk.wait("t", vc, dn_free[gb])
                    mk.wait("t", vc, dn_free[ub])
                    if gb >= 2:
                        mk.wait("t", sc, dnb_free[gb - 2])
                    if ub >= 2:
                        mk.wait("t", sc, dnb_free[ub - 2])
                    for c in range(ND):
                        mk.ins("t", lambda E,
                               o=pgu[:, gb * 512: gb * 512 + CAP],
                               l=wgb[:, (pw * ND + c) * 512 + il * 128:
                                      (pw * ND + c) * 512 + (il + 1) * 128],
                               r=xsT_e3[pbuf].rearrange("p c s -> p (c s)")[:, c * CAP:(c + 1) * CAP],
                               st=(c == 0), sp=(c == ND - 1):
                               E.matmul(o, l, r, start=st, stop=sp),
                               inc=(tc if c == ND - 1 else None))
                    gmm_done = tc.n
                    for c in range(ND):
                        mk.ins("t", lambda E,
                               o=pgu[:, ub * 512: ub * 512 + CAP],
                               l=wub[:, (pw * ND + c) * 512 + il * 128:
                                      (pw * ND + c) * 512 + (il + 1) * 128],
                               r=xsT_e3[pbuf].rearrange("p c s -> p (c s)")[:, c * CAP:(c + 1) * CAP],
                               st=(c == 0), sp=(c == ND - 1):
                               E.matmul(o, l, r, start=st, stop=sp),
                               inc=(tc if c == ND - 1 else None))
                    umm_done = tc.n
                    sgi = il % 2
                    mk.wait("s", tc, gmm_done)
                    mk.wait("s", vc, silu_free[sgi])
                    mk.ins("s", lambda E, o=silu_g[sgi][:, 0:CAP],
                           i=pgu[:, gb * 512: gb * 512 + CAP]:
                           E.activation(out=o, in_=i, func=AF.Silu), inc=sc)
                    gbank_sfree[gb] = sc.n
                    mk.wait("v", sc)
                    mk.wait("v", tc, umm_done)
                    mk.wait("v", tc, acte_free)
                    mk.ins("v", lambda E,
                           o=acte[:, ic * CAP:(ic + 1) * CAP],
                           i=pgu[:, ub * 512: ub * 512 + CAP],
                           s_=silu_g[sgi][:, 0:CAP]:
                           E.tensor_tensor(o, i, s_, ALU.mult), inc=vc)
                    ubank_vfree[ub] = vc.n
                    silu_free[sgi] = vc.n
                wg_free[pw] = tc.n
                if sp_i + 2 < 4:
                    load_span(sp_i + 2)
            act_done = vc.n
            xsT_free[pbuf] = tc.n
            if idx + 1 < len(exp_list):
                e_next = exp_list[idx + 1]
                if b == 0:
                    mk.wait("q", mk.cv_ex[e_next])
                    mk.wait("s", mk.cv_ex[e_next])
                load_span(0, ee=e_next)
                load_span(1, ee=e_next)
                pend_w = (wg_loaded[0], wg_loaded[1])
                lds_n = issue_xstg_loads(e_next)
            mk.wait("t", mk.qwd, wd_ld)
            mk.wait("t", vc, act_done)
            for bnk in range(6):
                mk.wait("t", sc, dnb_free[bnk])
            for ic in range(16):
                for sc_ in range(NSC):
                    for dh in range(2):
                        bank = 2 + sc_ * 2 + dh
                        mk.ins("t", lambda E,
                               o=pgu[:, bank * 512:(bank + 1) * 512],
                               l=acte[:, ic * CAP + sc_ * 128: ic * CAP + (sc_ + 1) * 128],
                               r=wdb[:, ic * 1024 + dh * 512: ic * 1024 + dh * 512 + 512],
                               st=(ic == 0), sp=(ic == 15):
                               E.matmul(o, l, r, start=st, stop=sp),
                               inc=(tc if (ic == 15 and sc_ == NSC - 1 and dh == 1) else None))
            dnm_done = tc.n
            acte_free = tc.n
            wd_free = tc.n
            if idx + 1 < len(exp_list):
                prep_xsT(1 - pbuf, lds_n)
            for sc_ in range(NSC):
                for dh in range(2):
                    bank = 2 + sc_ * 2 + dh
                    mk.wait("s", tc, dnm_done)
                    mk.wait("s", mk.qy, ystg_free)
                    mk.ins("s", lambda E, o=ystg2[:, :],
                           i=pgu[:, bank * 512:(bank + 1) * 512]:
                           E.copy(o, i), inc=sc)
                    dnb_free[sc_ * 2 + dh] = sc.n
                    mk.wait("q", sc)
                    ystg_free = mk.dma(
                        "q",
                        scr_y[e * CAP + sc_ * 128: e * CAP + (sc_ + 1) * 128,
                              dh * 512:(dh + 1) * 512],
                        ystg2[:, :], sem=mk.qy)
        mk.fence()

        mk.phase(6)
        grt = vf(131072, 2 * D)
        mk.wait("v", vc)
        mk.ins("v", lambda E: E.reciprocal(zrec[:, :], zacc[:, :]), inc=vc)
        zr_done = vc.n
        mk.wait("s", vc, zr_done)
        mk.wait("g", mk.qy)
        cxt_free = [0, 0]
        csn_free = [0, 0]
        cgt_free = [0, 0]
        grt_free = [0, 0]
        cot_free = [0, 0]
        for k in range(NT):
            p = k % 2
            mk.wait("q", vc, cxt_free[p])
            mk.dma("q", cxt[:, p * D:(p + 1) * D], x[b, k * 128:(k + 1) * 128, :],
                   sem=mk.qc[p])
            mk.wait("q", vc, csn_free[p])
            ld_done = mk.dma("q", csn[:, p * D:(p + 1) * D],
                             scr_s[k * 128:(k + 1) * 128, :], sem=mk.qc[p])
            mk.wait("v", sc, grt_free[p])
            mk.ins("v", lambda E, o=grt[:, p * D:(p + 1) * D]: E.memset(o, 0.0), inc=vc)
            gz = vc.n
            mk.wait("g", vc, gz)
            if not mk.mute:
                mk.ggr[p].n += 16
            mk.emit("g", lambda E, o=grt[:, p * D:(p + 1) * D],
                    ix=sidx_sb[:, k:k + 1], h=mk.ggr[p].h:
                    E.indirect_dma_start(out=o, out_offset=None,
                                         in_=scr_y[:, :],
                                         in_offset=IOX(ap=ix, axis=0),
                                         bounds_check=breg,
                                         oob_is_err=False).then_inc(h, 16))
            g_ld = mk.ggr[p].n
            mk.wait("v", mk.ggr[p], g_ld)
            mk.ins("v", lambda E, o=grt[:, p * D:(p + 1) * D],
                   a=grt[:, p * D:(p + 1) * D], b_=acc[:, k * D:(k + 1) * D]:
                   E.tensor_tensor(o, a, b_, ALU.add), inc=vc)
            ffn_done = vc.n
            mk.wait("s", vc, ffn_done)
            mk.wait("s", vc, cgt_free[p])
            mk.ins("s", lambda E, o=cgt[:, p * D:(p + 1) * D],
                   i=grt[:, p * D:(p + 1) * D]:
                   E.activation(out=o, in_=i, func=AF.Silu), inc=sc)
            grt_free[p] = sc.n
            mk.wait("v", sc)
            mk.wait("v", mk.qc[p], ld_done)
            mk.wait("v", vc, zr_done)
            mk.wait("v", mk.qo[p], cot_free[p])
            mk.ins("v", lambda E, o=cot[:, p * D:(p + 1) * D],
                   i=csn[:, p * D:(p + 1) * D], z=zrec[:, k:k + 1],
                   g_=cgt[:, p * D:(p + 1) * D]: E.scalar_tensor_tensor(
                   out=o, in0=i, scalar=z, in1=g_, op0=ALU.mult, op1=ALU.mult), inc=vc)
            csn_free[p] = vc.n
            cgt_free[p] = vc.n
            mk.wait("v", vc)
            mk.ins("v", lambda E, o=cot[:, p * D:(p + 1) * D],
                   a=cot[:, p * D:(p + 1) * D], b_=cxt[:, p * D:(p + 1) * D]:
                   E.tensor_tensor(o, a, b_, ALU.add), inc=vc)
            cxt_free[p] = vc.n
            mk.wait("q", vc)
            cot_free[p] = mk.dma("q", out[b, k * 128:(k + 1) * 128, :],
                                 cot[:, p * D:(p + 1) * D], sem=mk.qo[p])
        mk.fence()

    for s_ in mk.final_sems:
        mk.wait("q", s_)
    mk.replay()
    es.close()
    return nc


_IDENT = np.concatenate([
    np.eye(128, dtype=ml_dtypes.bfloat16),
    np.triu(np.ones((128, 128), dtype=ml_dtypes.bfloat16)),
    np.ones((128, 128), dtype=ml_dtypes.bfloat16),
], axis=1)
_CVEC = np.zeros((128, 16), np.float32)
_CVEC[:, 0] = EPS
_CVEC[:, 1] = 1.0
_CVEC[:, 2:10] = np.arange(8, dtype=np.float32)[None, :]


def kernel(L0_attn_x, L0_attn_U, L0_ffn_gate_w, L0_ffn_sh_gate, L0_ffn_sh_up,
           L0_ffn_sh_down, L0_ffn_rt_gate, L0_ffn_rt_up, L0_ffn_rt_down):
    xf = np.ascontiguousarray(np.asarray(L0_attn_x, dtype=np.float32))
    wts = {
        "w_u": np.ascontiguousarray(np.asarray(L0_attn_U, np.float32)),
        "w_gate": np.ascontiguousarray(np.asarray(L0_ffn_gate_w, np.float32)),
        "w_shg": np.ascontiguousarray(np.asarray(L0_ffn_sh_gate, np.float32)),
        "w_shu": np.ascontiguousarray(np.asarray(L0_ffn_sh_up, np.float32)),
        "w_shd": np.ascontiguousarray(np.asarray(L0_ffn_sh_down, np.float32)),
        "w_rtg": np.ascontiguousarray(np.asarray(L0_ffn_rt_gate, np.float32)),
        "w_rtu": np.ascontiguousarray(np.asarray(L0_ffn_rt_up, np.float32)),
        "w_rtd": np.ascontiguousarray(np.asarray(L0_ffn_rt_down, np.float32)),
        "ident": _IDENT,
        "cvec": _CVEC,
    }
    nc = build_program()
    in_maps = []
    for c in range(NCORES):
        m = dict(wts)
        m["x"] = np.ascontiguousarray(xf[c * BPC:(c + 1) * BPC])
        in_maps.append(m)
    res = run_bass_kernel_spmd(nc, in_maps, core_ids=list(range(NCORES)))
    return np.concatenate([r["out"] for r in res.results], axis=0)


# version 13
# speedup vs baseline: 1.0986x; 1.0919x over previous
import numpy as np
import ml_dtypes
from contextlib import ExitStack

import concourse.bass as bass
import concourse.mybir as mybir
from concourse.bass_utils import run_bass_kernel_spmd
from concourse.bass import IndirectOffsetOnAxis as IOX

F32 = mybir.dt.float32
BF16 = mybir.dt.bfloat16
AF = mybir.ActivationFunctionType
ALU = mybir.AluOpType
AX = mybir.AxisListType

NCORES = 8
B_ALL, T, D = 16, 2048, 1024
BPC = B_ALL // NCORES
NT = T // 128
ND = D // 128
IFF = 2048
NE = 8
R = 32
CAP = 384
EPS = 1e-5

ENGS = ("t", "v", "s", "q", "g")


class Sem:
    __slots__ = ("h", "n")

    def __init__(self, h):
        self.h = h
        self.n = 0


class MK:

    def __init__(self, nc, es, dbg=None, max_ph=99):
        self.nc = nc
        self.es = es
        self.code = {e: [] for e in ENGS}
        self.dbg = dbg or {}
        self.max_ph = max_ph
        self.mute = False
        self.tc = self.sem("tc")
        self.vc = self.sem("vc")
        self.sc = self.sem("sc")
        self.qd = self.sem("qd")
        self.gd = self.sem("gd")
        self.fs = self.sem("fs")
        self.qx = [self.sem("qx0"), self.sem("qx1")]
        self.qs = [self.sem("qs0"), self.sem("qs1")]
        self.qc = [self.sem("qc0"), self.sem("qc1")]
        self.qo = [self.sem("qo0"), self.sem("qo1")]
        self.qa = self.sem("qa")
        self.gw = [self.sem("gw0"), self.sem("gw1")]
        self.gwd = self.sem("gwd")
        self.gc = self.sem("gc")
        self.qy = self.sem("qy")
        self.gsc = [self.sem("gsc0"), self.sem("gsc1")]
        self.ggr = [self.sem("ggr0"), self.sem("ggr1")]
        self.qz = self.sem("qz")
        self.qwb = [self.sem("qwb0"), self.sem("qwb1")]
        self.qwbd = self.sem("qwbd")
        self.qxs = [self.sem("qxs0"), self.sem("qxs1"), self.sem("qxs2")]
        self.qgw = [self.sem("qgw0"), self.sem("qgw1")]
        self.qwd = self.sem("qwd")
        self.cv_sh = self.sem("cv_sh")
        self.cv_ex = [self.sem(f"cv_ex{i}") for i in range(NE)]
        self.dma_sems = [self.qd, self.gd, self.qa, self.gwd, self.qy,
                         self.qz, self.qwbd, self.qwd] + \
            self.qwb + self.qxs + \
            self.qx + self.qs + self.qc + self.qo + self.gw + self.gsc + \
            self.ggr + self.qgw
        self.final_sems = self.dma_sems + [self.cv_sh] + self.cv_ex

    def sem(self, name):
        return Sem(self.es.enter_context(self.nc.semaphore(name)))

    def phase(self, n):
        self.mute = n > self.max_ph

    def emit(self, eng, fn):
        if self.mute:
            return
        self.code[eng].append(fn)

    def wait(self, eng, sem, val=None):
        v = sem.n if val is None else val
        if v <= 0:
            return
        self.emit(eng, lambda E, h=sem.h, v=v: E.wait_ge(h, v))

    def ins(self, eng, build, inc=None, k=1):
        if self.mute:
            return
        if inc is None:
            self.emit(eng, lambda E, b=build: b(E))
        else:
            inc.n += k
            self.emit(eng, lambda E, b=build, h=inc.h, k=k: b(E).then_inc(h, k))

    def dma(self, eng_q, out, in_, sem=None, **kw):
        if sem is None:
            sem = self.qd if eng_q == "q" else self.gd
        if self.mute:
            return sem.n
        sem.n += 16
        self.emit(eng_q, lambda E, o=out, i=in_, h=sem.h, kw=kw:
                  E.dma_start(out=o, in_=i, **kw).then_inc(h, 16))
        return sem.n

    def fence(self):
        if self.mute:
            return
        tgts = [(s, s.n) for s in [self.tc, self.vc, self.sc, self.gc] + self.dma_sems]
        self.fs.n += len(ENGS)
        tgt = self.fs.n
        for e in ENGS:
            for s, v in tgts:
                self.wait(e, s, v)
            self.emit(e, lambda E, h=self.fs.h: E.sem_inc(h, 1))
            self.emit(e, lambda E, h=self.fs.h, v=tgt: E.wait_ge(h, v))

    def tap(self, name, sb_ap):
        if name not in self.dbg:
            return
        for s in (self.tc, self.vc, self.sc):
            self.wait("g", s)
        self.dma("g", self.dbg[name][:, :], sb_ap)
        self.fence()

    def replay(self):
        nc = self.nc
        with nc.Block() as block:
            @block.tensor
            def _(E):
                for fn in self.code["t"]:
                    fn(E)

            @block.vector
            def _(E):
                for fn in self.code["v"]:
                    fn(E)

            @block.scalar
            def _(E):
                for fn in self.code["s"]:
                    fn(E)

            @block.sync
            def _(E):
                for fn in self.code["q"]:
                    fn(E)

            @block.gpsimd
            def _(E):
                for fn in self.code["g"]:
                    fn(E)


def build_program(dbg_names=(), max_ph=99, branch_filter=None, detect_races=True, nbat=BPC):
    nc = bass.Bass("TRN2", target_bir_lowering=False, detect_race_conditions=detect_races)
    es = ExitStack()
    U8 = mybir.dt.uint8

    x = nc.dram_tensor("x", [BPC, T, D], F32, kind="ExternalInput").ap()
    w_u = nc.dram_tensor("w_u", [D, R], F32, kind="ExternalInput").ap()
    w_gate = nc.dram_tensor("w_gate", [D, NE], F32, kind="ExternalInput").ap()
    w_shg = nc.dram_tensor("w_shg", [D, IFF], F32, kind="ExternalInput").ap()
    w_shu = nc.dram_tensor("w_shu", [D, IFF], F32, kind="ExternalInput").ap()
    w_shd = nc.dram_tensor("w_shd", [IFF, D], F32, kind="ExternalInput").ap()
    w_rtg = nc.dram_tensor("w_rtg", [NE, D, IFF], F32, kind="ExternalInput").ap()
    w_rtu = nc.dram_tensor("w_rtu", [NE, D, IFF], F32, kind="ExternalInput").ap()
    w_rtd = nc.dram_tensor("w_rtd", [NE, IFF, D], F32, kind="ExternalInput").ap()
    ident_in = nc.dram_tensor("ident", [128, 3 * 128], BF16, kind="ExternalInput").ap()
    cvec_in = nc.dram_tensor("cvec", [128, 16], F32, kind="ExternalInput").ap()
    out = nc.dram_tensor("out", [BPC, T, D], F32, kind="ExternalOutput").ap()
    scr_s = nc.dram_tensor("scr_s", [T, D], F32, kind="Internal").ap()
    scr_xsc = nc.dram_tensor("scr_xsc", [NE * CAP, D], BF16, kind="Internal").ap()
    scr_y = nc.dram_tensor("scr_y", [NE * CAP, D], F32, kind="Internal").ap()
    scr_wg = nc.dram_tensor("scr_wg", [9, D, IFF], BF16, kind="Internal").ap()
    scr_wu = nc.dram_tensor("scr_wu", [9, D, IFF], BF16, kind="Internal").ap()
    scr_wdn = nc.dram_tensor("scr_wdn", [9, IFF, D], BF16, kind="Internal").ap()

    dbg = {}
    for nm, shp, dt in [
        ("dbg_xnt", [128, ND * T], F32), ("dbg_logits", [128, NT * NE], F32),
        ("dbg_score", [128, NT], F32), ("dbg_mask", [128, NT * NE], F32),
        ("dbg_hi", [128, NT * D], F32), ("dbg_lo", [128, NT * D], F32),
        ("dbg_yb", [32, T], F32), ("dbg_n", [1, T], F32),
        ("dbg_z", [128, NT], F32), ("dbg_acc", [128, NT * D], F32),
        ("dbg_xst", [128, ND * T], F32),
    ]:
        if nm in dbg_names:
            dbg[nm] = nc.dram_tensor(nm, shp, dt, kind="ExternalOutput").ap()

    mk = MK(nc, es, dbg, max_ph=max_ph)
    tc, vc, sc, qd, gd = mk.tc, mk.vc, mk.sc, mk.qd, mk.gd
    breg = es.enter_context(nc.gpsimd.register("breg"))
    mk.emit("g", lambda E: E.reg_mov(breg, NE * CAP - 1))

    identb3 = es.enter_context(nc.sbuf_tensor("identb", [128, 3 * 128], BF16))
    ident = identb3[:, 0:128]
    tril_b = identb3[:, 128:256]
    ones_b = identb3[:, 256:384]
    cvec = es.enter_context(nc.sbuf_tensor("cvecb", [128, 16], F32))
    u_bf = es.enter_context(nc.sbuf_tensor("u_bf", [128, ND * R], BF16))
    gw_hi = es.enter_context(nc.sbuf_tensor("gw_hi", [128, ND * NE], BF16))
    gw_lo = es.enter_context(nc.sbuf_tensor("gw_lo", [128, ND * NE], BF16))
    gw_f = es.enter_context(nc.sbuf_tensor("gw_f", [128, ND * NE], F32))
    gw_t = es.enter_context(nc.sbuf_tensor("gw_t", [128, ND * NE], F32))
    logits_sb = es.enter_context(nc.sbuf_tensor("logits_sb", [128, NT * NE], F32))
    tv_sb = es.enter_context(nc.sbuf_tensor("tv_sb", [128, NT], F32))
    score_sb = es.enter_context(nc.sbuf_tensor("score_sb", [128, NT], F32))
    mask_sb = es.enter_context(nc.sbuf_tensor("mask_sb", [128, NT * NE], F32))
    zacc = es.enter_context(nc.sbuf_tensor("zacc", [128, NT], F32))
    zscr = es.enter_context(nc.sbuf_tensor("zscr", [128, NT], F32))
    zrec = es.enter_context(nc.sbuf_tensor("zrec", [128, NT], F32))
    silu_g = [es.enter_context(nc.sbuf_tensor(f"silu_g{p}", [128, 512], F32)) for p in range(4)]
    mcol = es.enter_context(nc.sbuf_tensor("mcol", [128, 4], F32))
    mask_bf = es.enter_context(nc.sbuf_tensor("mask_bf", [128, NT * NE], BF16))
    pos_sb = es.enter_context(nc.sbuf_tensor("pos_sb", [128, NT * NE], F32))
    sidx_sb = es.enter_context(nc.sbuf_tensor("sidx_sb", [128, NT], mybir.dt.uint32))
    tmp8a = es.enter_context(nc.sbuf_tensor("tmp8a", [128, NE], F32))
    tmp8b = es.enter_context(nc.sbuf_tensor("tmp8b", [128, NE], F32))
    ridx_f = es.enter_context(nc.sbuf_tensor("ridx_f", [128, 4], F32))
    zero_tile = es.enter_context(nc.sbuf_tensor("zero_tile", [128, D], BF16))
    ssq = es.enter_context(nc.sbuf_tensor("ssq", [128, 4], F32))
    rstd = es.enter_context(nc.sbuf_tensor("rstd", [128, 4], F32))

    AR_BYTES = 196608
    arena = es.enter_context(nc.sbuf_tensor("arena", [128, AR_BYTES], U8))
    arf = arena.bitcast(F32)
    arb = arena.bitcast(BF16)

    def vf(off, n):
        return arf[:, off // 4: off // 4 + n]

    def vb(off, n):
        return arb[:, off // 2: off // 2 + n]

    dmaj = vb(0, ND * T)
    dmaj3 = dmaj.rearrange("p (c t) -> p c t", c=ND)
    hi_tm = vb(32768, NT * D)
    lo_tm = vb(65536, NT * D)
    xst = vf(98304, 2 * D)
    tm32a = vf(106496, 2 * D)
    tm32b = vf(114688, 2 * D)
    loch = vb(122880, 2 * D)
    auga = vb(126976, T)[0:36]
    augb = vb(131072, T)[0:36]
    ybf = vb(135168, T)[0:32]
    ysq = vf(139264, 512)[0:32]
    nrow = vf(141312, T)[0:1]
    nt1 = vf(149504, T)[0:1]
    nnhi = vb(157696, T)[0:1]
    nnlo = vb(161792, T)[0:1]
    ones_row = vb(165888, T)[0:1]
    qcol = vb(169984, NT * 512)
    sstg = vf(186368, 2 * D)
    xsb = vb(169984, 2 * D)
    acc = vf(32768, NT * D)
    actt = vb(98304, ND * T)
    wgb = vb(131072, 2 * ND * 512)
    wub = vb(147456, 2 * ND * 512)
    wdb = vb(163840, 16 * 1024)
    cxt = vf(98304, 2 * D)
    csn = vf(106496, 2 * D)
    cgt = vf(114688, 2 * D)
    cot = vf(122880, 2 * D)

    psA = es.enter_context(nc.psum_tensor("psA", [128, 4096], F32))
    psAb = psA.bitcast(BF16)
    ptrh = psAb[:, 0:2048]
    ptrl = psAb[:, 2048:4096]
    plg = psA[:, 2048:2048 + NE]
    pyp = psA[:, 2560:3072][0:32]
    pnp = psA[:, 3072:3584][0:1]
    pq = psA[:, 0:2048]
    ps = psA[:, 2048:4096]
    ptx = psAb[:, 0:2048]
    pgu = psA[:, 0:4096]

    mk.dma("q", identb3[:, :], ident_in[:, :])
    mk.dma("q", cvec[:, :], cvec_in[:, :])
    mk.dma("q", gw_f.rearrange("p (c e) -> p c e", c=ND),
           w_gate.rearrange("(c p) e -> p c e", p=128))
    for c in range(ND):
        mk.dma("g", gw_hi[:, c * NE:(c + 1) * NE], w_gate[c * 128:(c + 1) * 128, :])
        mk.dma("g", u_bf[:, c * R:(c + 1) * R], w_u[c * 128:(c + 1) * 128, :])
    mk.dma("g", scr_wg[8], w_shg[:, :], sem=mk.cv_sh, max_dma_last_dim=8192)
    mk.dma("g", scr_wu[8], w_shu[:, :], sem=mk.cv_sh, max_dma_last_dim=8192)
    mk.dma("g", scr_wdn[8], w_shd[:, :], sem=mk.cv_sh, max_dma_last_dim=8192)
    for e_ in range(NE):
        mk.dma("g", scr_wg[e_], w_rtg[e_], sem=mk.cv_ex[e_], max_dma_last_dim=8192)
        mk.dma("g", scr_wu[e_], w_rtu[e_], sem=mk.cv_ex[e_], max_dma_last_dim=8192)
        mk.dma("g", scr_wdn[e_], w_rtd[e_], sem=mk.cv_ex[e_], max_dma_last_dim=8192)
    mk.wait("v", gd)
    mk.wait("v", qd)
    mk.ins("v", lambda E: E.tensor_copy(gw_t[:, :], gw_hi[:, :]), inc=vc)
    mk.wait("v", vc)
    mk.ins("v", lambda E: E.tensor_tensor(gw_t[:, :], gw_f[:, :], gw_t[:, :], ALU.subtract), inc=vc)
    mk.wait("v", vc)
    mk.ins("v", lambda E: E.tensor_copy(gw_lo[:, :], gw_t[:, :]), inc=vc)
    mk.ins("v", lambda E: E.memset(zero_tile[:, :], 0.0), inc=vc)
    zf = vc.n
    mk.wait("q", vc, zf)
    for zi in range((NE * CAP) // 128):
        mk.dma("q", scr_xsc[zi * 128:(zi + 1) * 128, :], zero_tile[:, :], sem=mk.qz)

    branches = [("e", e, w_rtg, w_rtu, w_rtd) for e in range(NE)] + \
               [("sh", None, w_shg, w_shu, w_shd)]
    if branch_filter is not None:
        branches = [br for br in branches
                    if (br[0] == "sh" and "sh" in branch_filter) or
                       (br[0] == "e" and br[1] in branch_filter)]

    for b in range(nbat):
        mk.phase(1)
        xst_free = [0, 0]
        xnf_free = [0, 0]
        cent_free = [0, 0]
        ptrh_free = [0, 0]
        ptrl_free = [0, 0]
        loch_free = [0, 0]
        lg_free = 0
        pend_tv = []

        def emit_tv(kk, lgv):
            mk.wait("v", sc, lgv)
            mk.ins("v", lambda E, o=tv_sb[:, kk:kk + 1],
                   i=logits_sb[:, kk * NE:(kk + 1) * NE]:
                   E.reduce_max(o, i, axis=AX.X), inc=vc)
            mk.wait("v", vc)
            mk.ins("v", lambda E, o=mask_sb[:, kk * NE:(kk + 1) * NE],
                   i=logits_sb[:, kk * NE:(kk + 1) * NE], s_=tv_sb[:, kk:kk + 1]:
                   E.tensor_scalar(out=o, in0=i, scalar1=s_, scalar2=None,
                                   op0=ALU.is_equal), inc=vc)

        def stage1(k):
            p = k % 2
            xs_ = xst[:, p * D:(p + 1) * D]
            cent = tm32a[:, p * D:(p + 1) * D]
            mk.wait("q", vc, xst_free[p])
            x_ld = mk.dma("q", xs_, x[b, k * 128:(k + 1) * 128, :], sem=mk.qx[p])
            mk.wait("v", mk.qx[p], x_ld)
            mk.ins("v", lambda E, o=mcol[:, p:p + 1], i=xs_:
                   E.reduce_sum(o, i, axis=AX.X), inc=vc)
            mk.wait("v", vc)
            mk.ins("v", lambda E, o=mcol[:, p:p + 1]: E.tensor_scalar(
                out=o, in0=o, scalar1=-1.0 / D, scalar2=None, op0=ALU.mult), inc=vc)
            mk.wait("v", vc)
            mk.wait("v", sc, cent_free[p])
            mk.ins("v", lambda E, o=cent, i=xs_, m=mcol[:, p:p + 1]: E.tensor_scalar(
                out=o, in0=i, scalar1=m, scalar2=None, op0=ALU.add), inc=vc)
            xst_free[p] = vc.n
            return vc.n

        def stage2(k, cent_done):
            p = k % 2
            cent = tm32a[:, p * D:(p + 1) * D]
            xnf = tm32b[:, p * D:(p + 1) * D]
            hi_sl = hi_tm[:, k * D:(k + 1) * D]
            lo_sl = lo_tm[:, k * D:(k + 1) * D]
            mk.wait("s", vc, cent_done)
            mk.wait("s", vc, xnf_free[p])
            mk.ins("s", lambda E, o=xnf, i=cent, a=ssq[:, p:p + 1]: E.activation(
                out=o, in_=i, func=AF.Square, accum_out=a), inc=sc)
            mk.wait("s", sc)
            mk.ins("s", lambda E, o=rstd[:, p:p + 1], i=ssq[:, p:p + 1]: E.activation(
                out=o, in_=i, func=AF.Ln, scale=1.0 / D, bias=cvec[:, 0:1]), inc=sc)
            mk.wait("s", sc)
            mk.ins("s", lambda E, o=rstd[:, p:p + 1], i=rstd[:, p:p + 1]: E.activation(
                out=o, in_=i, func=AF.Exp, scale=-0.5), inc=sc)
            mk.wait("s", sc)
            mk.ins("s", lambda E, o=xnf, i=cent, r=rstd[:, p:p + 1]: E.activation(
                out=o, in_=i, func=AF.Copy, scale=r), inc=sc)
            cent_free[p] = sc.n
            mk.wait("s", sc)
            mk.ins("s", lambda E, o=hi_sl, i=xnf: E.copy(o, i), inc=sc)
            hi_done = sc.n
            mk.wait("v", sc, hi_done)
            mk.ins("v", lambda E, o=lo_sl, a=hi_sl, b_=xnf:
                   E.scalar_tensor_tensor(out=o, in0=a, scalar=-1.0, in1=b_,
                                          op0=ALU.mult, op1=ALU.add), inc=vc)
            xnf_free[p] = vc.n
            hilo_done = vc.n
            mk.wait("t", sc, hi_done)
            mk.wait("t", vc, hilo_done)
            mk.wait("t", sc, ptrh_free[p])
            for c in range(ND):
                mk.ins("t", lambda E,
                       o=ptrh[:, p * 1024 + c * 128: p * 1024 + (c + 1) * 128],
                       i=hi_sl[:, c * 128:(c + 1) * 128], d=ident[:, :]:
                       E.transpose(o, i, d), inc=(tc if c == ND - 1 else None))
            trh_done = tc.n
            mk.wait("t", vc, ptrl_free[p])
            for c in range(ND):
                mk.ins("t", lambda E,
                       o=ptrl[:, p * 1024 + c * 128: p * 1024 + (c + 1) * 128],
                       i=lo_sl[:, c * 128:(c + 1) * 128], d=ident[:, :]:
                       E.transpose(o, i, d), inc=(tc if c == ND - 1 else None))
            trl_done = tc.n
            mk.wait("s", tc, trh_done)
            mk.ins("s", lambda E, o=dmaj3[:, :, k * 128:(k + 1) * 128],
                   i=ptrh.rearrange("p (q c t) -> p q c t", q=2, c=ND)[:, p]:
                   E.copy(o, i), inc=sc)
            ptrh_free[p] = sc.n
            hi_evac = sc.n
            mk.wait("v", tc, trl_done)
            mk.wait("v", tc, loch_free[p])
            mk.ins("v", lambda E, o=loch[:, p * 1024:(p + 1) * 1024],
                   i=ptrl[:, p * 1024:(p + 1) * 1024]: E.tensor_copy(o, i), inc=vc)
            ptrl_free[p] = vc.n
            lo_evac = vc.n
            mk.wait("t", sc, hi_evac)
            mk.wait("t", vc, lo_evac)
            mk.wait("t", sc, lg_free2[0])
            nmm = 3 * ND
            j = 0
            for pass_i in range(3):
                wsb = gw_hi if pass_i != 1 else gw_lo
                for c in range(ND):
                    if pass_i < 2:
                        lh = dmaj[:, c * T + k * 128: c * T + (k + 1) * 128]
                    else:
                        lh = loch[:, p * 1024 + c * 128: p * 1024 + (c + 1) * 128]
                    mk.ins("t", lambda E, o=plg[:, :], l=lh,
                           r=wsb[:, c * NE:(c + 1) * NE],
                           st=(j == 0), sp=(j == nmm - 1):
                           E.matmul(o, l, r, start=st, stop=sp),
                           inc=(tc if j == nmm - 1 else None))
                    j += 1
            loch_free[p] = tc.n
            mk.wait("s", tc)
            mk.ins("s", lambda E, o=logits_sb[:, k * NE:(k + 1) * NE],
                   i=plg[:, :]: E.copy(o, i), inc=sc)
            lg_free2[0] = sc.n
            pend_tv.append((k, sc.n))

        lg_free2 = [0]
        prev = None
        for k in range(NT):
            cd = stage1(k)
            if len(pend_tv) >= 2:
                emit_tv(*pend_tv.pop(0))
            if prev is not None:
                stage2(*prev)
            prev = (k, cd)
        stage2(*prev)
        for pt in pend_tv:
            emit_tv(*pt)
        pend_tv = []
        mk.tap("dbg_hi", hi_tm[:, :])
        mk.tap("dbg_lo", lo_tm[:, :])
        mk.tap("dbg_xnt", dmaj[:, :])
        mk.tap("dbg_logits", logits_sb[:, :])
        mk.tap("dbg_mask", mask_sb[:, :])
        mk.fence()

        mk.phase(2)
        mk.ins("v", lambda E: E.memset(ones_row[:, :], 1.0), inc=vc)
        for js in range(4):
            for c in range(ND):
                mk.ins("t", lambda E, o=pyp[:, :], l=u_bf[:, c * R:(c + 1) * R],
                       r=dmaj[:, c * T + js * 512: c * T + js * 512 + 512],
                       st=(c == 0), sp=(c == ND - 1):
                       E.matmul(o, l, r, start=st, stop=sp),
                       inc=(tc if c == ND - 1 else None))
            mk.wait("s", tc)
            mk.ins("s", lambda E, o=ybf[:, js * 512:(js + 1) * 512],
                   i=pyp[:, :]: E.copy(o, i), inc=sc)
            mk.wait("s", sc)
            mk.ins("s", lambda E, o=ysq[:, :],
                   i=ybf[:, js * 512:(js + 1) * 512]:
                   E.activation(out=o, in_=i, func=AF.Square), inc=sc)
            mk.wait("t", sc)
            mk.ins("t", lambda E, o=pnp[:, :], l=cvec[0:32, 1:2],
                   r=ysq[:, :]:
                   E.matmul(o, l, r, start=True, stop=True), inc=tc)
            mk.wait("s", tc)
            mk.ins("s", lambda E, o=nrow[:, js * 512:(js + 1) * 512],
                   i=pnp[:, :]: E.copy(o, i), inc=sc)
            mk.wait("t", sc)
        mk.wait("v", sc)
        mk.ins("v", lambda E: E.tensor_copy(nnhi[:, :], nrow[:, :]), inc=vc)
        mk.wait("v", vc)
        mk.ins("v", lambda E: E.tensor_copy(nt1[:, :], nnhi[:, :]), inc=vc)
        mk.wait("v", vc)
        mk.ins("v", lambda E: E.tensor_scalar(out=nnhi[:, :], in0=nnhi[:, :],
               scalar1=-1.0, scalar2=None, op0=ALU.mult), inc=vc)
        mk.wait("v", vc)
        mk.ins("v", lambda E: E.tensor_tensor(nt1[:, :], nrow[:, :], nt1[:, :],
                                              ALU.subtract), inc=vc)
        mk.wait("v", vc)
        mk.ins("v", lambda E: E.tensor_scalar(out=nnlo[:, :], in0=nt1[:, :],
               scalar1=-1.0, scalar2=None, op0=ALU.mult), inc=vc)
        mk.wait("v", vc)
        mk.ins("v", lambda E: E.tensor_scalar(out=auga[0:32, :], in0=ybf[:, :],
               scalar1=2.0, scalar2=None, op0=ALU.mult), inc=vc)
        mk.wait("v", vc)
        mk.ins("v", lambda E: E.tensor_copy(augb[0:32, :], ybf[:, :]), inc=vc)
        mk.wait("q", vc)
        mk.dma("q", auga[32:33, :], nnhi[:, :], sem=mk.qa)
        mk.dma("q", auga[33:34, :], nnlo[:, :], sem=mk.qa)
        mk.dma("q", auga[34:35, :], ones_row[:, :], sem=mk.qa)
        mk.dma("q", auga[35:36, :], ones_row[:, :], sem=mk.qa)
        mk.dma("q", augb[32:33, :], ones_row[:, :], sem=mk.qa)
        mk.dma("q", augb[33:34, :], ones_row[:, :], sem=mk.qa)
        mk.dma("q", augb[34:35, :], nnhi[:, :], sem=mk.qa)
        aug_ld = mk.dma("q", augb[35:36, :], nnlo[:, :], sem=mk.qa)
        mk.tap("dbg_yb", ybf[:, :])
        mk.tap("dbg_n", nrow[:, :])
        mk.wait("t", mk.qa, aug_ld)
        ppos = psA[:, 3584:3584 + NE]
        mk.wait("v", vc)
        mk.ins("v", lambda E: E.tensor_copy(mask_bf[:, :], mask_sb[:, :]), inc=vc)
        mbf_done = vc.n
        pos_ev = 0
        for I in range(NT):
            mk.wait("t", vc, mbf_done)
            mk.wait("t", sc, pos_ev)
            for Jj in range(I + 1):
                lh = ones_b if Jj < I else tril_b
                mk.ins("t", lambda E, o=ppos[:, :], l=lh,
                       r=mask_bf[:, Jj * NE:(Jj + 1) * NE],
                       st=(Jj == 0), sp=(Jj == I):
                       E.matmul(o, l, r, start=st, stop=sp),
                       inc=(tc if Jj == I else None))
            mk.wait("s", tc)
            mk.ins("s", lambda E, o=pos_sb[:, I * NE:(I + 1) * NE], i=ppos[:, :]:
                   E.copy(o, i), inc=sc)
            pos_ev = sc.n
        mk.wait("v", sc, pos_ev)
        for I in range(NT):
            mk.ins("v", lambda E, o=tmp8a[:, :], a=mask_sb[:, I * NE:(I + 1) * NE],
                   b_=pos_sb[:, I * NE:(I + 1) * NE]:
                   E.tensor_tensor(o, a, b_, ALU.mult), inc=vc)
            mk.wait("v", vc)
            mk.ins("v", lambda E, o=ridx_f[:, 0:1], i=tmp8a[:, :]:
                   E.reduce_sum(o, i, axis=AX.X), inc=vc)
            mk.ins("v", lambda E, o=tmp8b[:, :], a=mask_sb[:, I * NE:(I + 1) * NE],
                   b_=cvec[:, 2:10]:
                   E.tensor_tensor(o, a, b_, ALU.mult), inc=vc)
            mk.wait("v", vc)
            mk.ins("v", lambda E, o=ridx_f[:, 1:2], i=tmp8b[:, :]:
                   E.reduce_sum(o, i, axis=AX.X), inc=vc)
            mk.wait("v", vc)
            mk.ins("v", lambda E, o=ridx_f[:, 0:1]: E.tensor_scalar(
                out=o, in0=o, scalar1=-1.0, scalar2=None, op0=ALU.add), inc=vc)
            mk.wait("v", vc)
            mk.ins("v", lambda E, o=ridx_f[:, 2:3], i=ridx_f[:, 1:2],
                   s2=ridx_f[:, 0:1]: E.scalar_tensor_tensor(
                   out=o, in0=i, scalar=float(CAP), in1=s2,
                   op0=ALU.mult, op1=ALU.add), inc=vc)
            mk.wait("v", vc)
            mk.ins("v", lambda E, o=ridx_f[:, 3:4], i=ridx_f[:, 0:1]:
                   E.tensor_scalar(out=o, in0=i, scalar1=float(CAP), scalar2=None,
                                   op0=ALU.is_ge), inc=vc)
            mk.wait("v", vc)
            mk.ins("v", lambda E, o=ridx_f[:, 2:3], i=ridx_f[:, 3:4],
                   s2=ridx_f[:, 2:3]: E.scalar_tensor_tensor(
                   out=o, in0=i, scalar=1.0e6, in1=s2,
                   op0=ALU.mult, op1=ALU.add), inc=vc)
            mk.wait("v", vc)
            mk.ins("v", lambda E, o=sidx_sb[:, I:I + 1], i=ridx_f[:, 2:3]:
                   E.tensor_copy(o, i), inc=vc)
            mk.wait("v", vc)
        mk.fence()

        mk.phase(3)
        mk.wait("v", vc)
        mk.ins("v", lambda E: E.memset(zacc[:, :], 0.0), inc=vc)
        mk.wait("v", vc)
        pq_free = [0, 0, 0, 0]
        ps_free = [0, 0, 0, 0]
        qcol_free = 0
        sstg_free = [0, 0]
        zprev = 0
        for gi in range(4):
            mk.wait("s", tc, qcol_free)
            mk.wait("s", vc, zprev)
            for J in range(NT):
                bank = J % 4
                mk.wait("t", sc, pq_free[bank])
                mk.ins("t", lambda E, o=pq[:, bank * 512:(bank + 1) * 512],
                       l=auga[:, J * 128:(J + 1) * 128],
                       r=augb[:, gi * 512:(gi + 1) * 512]:
                       E.matmul(o, l, r, start=True, stop=True), inc=tc)
                mk.wait("s", tc)
                mk.ins("s", lambda E, o=qcol[:, J * 512:(J + 1) * 512],
                       i=pq[:, bank * 512:(bank + 1) * 512],
                       a=zscr[:, J:J + 1]: E.activation(
                       out=o, in_=i, func=AF.Exp, accum_out=a), inc=sc)
                pq_free[bank] = sc.n
            exp_done = sc.n
            mk.wait("v", sc, exp_done)
            mk.ins("v", lambda E: E.tensor_tensor(zacc[:, :], zacc[:, :],
                   zscr[:, :], ALU.add), inc=vc)
            zprev = vc.n
            mk.wait("t", sc, exp_done)
            for il in range(4):
                I = gi * 4 + il
                bk0, bk1 = 2 * (il % 2), 2 * (il % 2) + 1
                for dh, bank in ((0, bk0), (1, bk1)):
                    mk.wait("t", vc, ps_free[bank])
                    for J in range(NT):
                        mk.ins("t", lambda E,
                               o=ps[:, bank * 512:(bank + 1) * 512],
                               l=qcol[:, J * 512 + il * 128: J * 512 + (il + 1) * 128],
                               r=hi_tm[:, J * D + dh * 512: J * D + dh * 512 + 512],
                               st=(J == 0), sp=(J == NT - 1):
                               E.matmul(o, l, r, start=st, stop=sp),
                               inc=(tc if J == NT - 1 else None))
                smm_done = tc.n
                sp_ = il % 2
                mk.wait("v", tc, smm_done)
                mk.wait("v", mk.qs[sp_], sstg_free[sp_])
                mk.ins("v", lambda E, o=sstg[:, sp_ * D: sp_ * D + 512],
                       i=ps[:, bk0 * 512:(bk0 + 1) * 512]:
                       E.tensor_copy(o, i), inc=vc)
                mk.wait("v", vc)
                mk.ins("v", lambda E, o=sstg[:, sp_ * D + 512: (sp_ + 1) * D],
                       i=ps[:, bk1 * 512:(bk1 + 1) * 512]:
                       E.tensor_copy(o, i), inc=vc)
                ps_free[bk0] = vc.n
                ps_free[bk1] = vc.n
                mk.wait("q", vc)
                sstg_free[sp_] = mk.dma(
                    "q", scr_s[I * 128:(I + 1) * 128, :],
                    sstg[:, sp_ * D:(sp_ + 1) * D], sem=mk.qs[sp_])
            qcol_free = tc.n
        mk.tap("dbg_z", zacc[:, :])
        mk.fence()

        mk.phase(4)
        mk.wait("s", vc)
        mk.ins("s", lambda E: E.activation(out=score_sb[:, :], in_=tv_sb[:, :],
                                           func=AF.Sigmoid), inc=sc)
        sc_score = sc.n
        if b == 0:
            mk.wait("g", mk.qz)
        xsb_free = [0, 0]
        for k in range(NT):
            p = k % 2
            tmp = tm32a[:, p * D:(p + 1) * D]
            mk.wait("v", sc, sc_score)
            mk.ins("v", lambda E, o=tmp, a=hi_tm[:, k * D:(k + 1) * D],
                   b_=lo_tm[:, k * D:(k + 1) * D]:
                   E.tensor_tensor(o, a, b_, ALU.add), inc=vc)
            mk.wait("v", vc)
            mk.wait("v", mk.gsc[p], xsb_free[p])
            mk.ins("v", lambda E, o=xsb[:, p * D:(p + 1) * D], i=tmp,
                   s_=score_sb[:, k:k + 1]: E.tensor_scalar(
                   out=o, in0=i, scalar1=s_, scalar2=None, op0=ALU.mult), inc=vc)
            xs_done = vc.n
            mk.wait("g", vc, xs_done)
            if not mk.mute:
                mk.gsc[p].n += 16
            mk.emit("g", lambda E, i=xsb[:, p * D:(p + 1) * D],
                    ix=sidx_sb[:, k:k + 1], h=mk.gsc[p].h:
                    E.indirect_dma_start(out=scr_xsc[:, :],
                                         out_offset=IOX(ap=ix, axis=0),
                                         in_=i, in_offset=None,
                                         bounds_check=breg,
                                         oob_is_err=False).then_inc(h, 16))
            xsb_free[p] = mk.gsc[p].n
        mk.fence()

        mk.phase(5)
        actt_sh = vb(98304, ND * T)
        xsT_e3 = [vb(98304 + p * 6144, ND * CAP).rearrange("p_ (c s) -> p_ c s", c=ND)
                  for p in range(2)]
        acte = vb(110592, NT * CAP)
        wg_free = [0, 0]
        wgwb_free = [0, 0]
        wd_free = 0
        wdwb_free = 0
        dn_free = [0, 0, 0, 0]
        gbank_sfree = [0] * 4
        ubank_vfree = [0] * 4
        silu_free = [0, 0, 0, 0]
        spi = 0

        def load_wgu(br_i, i0, pw):
            for c in range(ND):
                mk.dma("q", wgb[:, (pw * ND + c) * 512:(pw * ND + c + 1) * 512],
                       scr_wg[br_i, c * 128:(c + 1) * 128, i0:i0 + 512], sem=mk.qgw[pw])
                mk.dma("s", wub[:, (pw * ND + c) * 512:(pw * ND + c + 1) * 512],
                       scr_wu[br_i, c * 128:(c + 1) * 128, i0:i0 + 512], sem=mk.qgw[pw])
            return (pw, mk.qgw[pw].n)

        def load_wd(br_i, ic_list, dst_off):
            for j, ic in enumerate(ic_list):
                mk.dma("s" if j % 2 else "q",
                       wdb[:, dst_off + j * 1024: dst_off + (j + 1) * 1024],
                       scr_wdn[br_i, ic * 128:(ic + 1) * 128, :], sem=mk.qwd)
            return mk.qwd.n

        sh_i = 8
        do_shared = branch_filter is None or "sh" in branch_filter
        if do_shared:
            if b == 0:
                mk.wait("q", mk.cv_sh)
                mk.wait("s", mk.cv_sh)
            for half in range(2):
                wg_loaded = [None, None]
                for sp_i in range(2):
                    pw = spi % 2
                    mk.wait("q", tc, wg_free[pw])
                    mk.wait("s", tc, wg_free[pw])
                    i0 = half * 1024 + sp_i * 512
                    wg_loaded[sp_i] = load_wgu(sh_i, i0, pw)
                    spi += 1
                mk.wait("q", tc, wd_free)
                mk.wait("s", tc, wd_free)
                ics = list(range(half * 8, half * 8 + 8))
                wd_ld = load_wd(sh_i, ics, 0)
                for sp_i in range(2):
                    pw, _wgl = wg_loaded[sp_i]
                    mk.wait("t", mk.qgw[pw], _wgl)
                    for il in range(4):
                        ic = sp_i * 4 + il
                        for t4 in range(4):
                            mk.wait("t", sc, gbank_sfree[t4])
                            mk.wait("t", vc, dn_free[t4])
                        for c in range(ND):
                            for t4 in range(4):
                                mk.ins("t", lambda E,
                                       o=pgu[:, t4 * 512:(t4 + 1) * 512],
                                       l=wgb[:, (pw * ND + c) * 512 + il * 128:
                                              (pw * ND + c) * 512 + (il + 1) * 128],
                                       r=dmaj[:, c * T + t4 * 512: c * T + (t4 + 1) * 512],
                                       st=(c == 0), sp=(c == ND - 1):
                                       E.matmul(o, l, r, start=st, stop=sp),
                                       inc=(tc if (c == ND - 1 and t4 == 3) else None))
                        gmm_done = tc.n
                        for t4 in range(4):
                            mk.wait("t", vc, ubank_vfree[t4])
                        for c in range(ND):
                            for t4 in range(4):
                                mk.ins("t", lambda E,
                                       o=pgu[:, (4 + t4) * 512:(5 + t4) * 512],
                                       l=wub[:, (pw * ND + c) * 512 + il * 128:
                                              (pw * ND + c) * 512 + (il + 1) * 128],
                                       r=dmaj[:, c * T + t4 * 512: c * T + (t4 + 1) * 512],
                                       st=(c == 0), sp=(c == ND - 1):
                                       E.matmul(o, l, r, start=st, stop=sp),
                                       inc=(tc if (c == ND - 1 and t4 == 3) else None))
                        umm_done = tc.n
                        for t4 in range(4):
                            mk.wait("s", tc, gmm_done)
                            mk.wait("s", vc, silu_free[t4])
                            mk.ins("s", lambda E, o=silu_g[t4][:, :],
                                   i=pgu[:, t4 * 512:(t4 + 1) * 512]:
                                   E.activation(out=o, in_=i, func=AF.Silu), inc=sc)
                            gbank_sfree[t4] = sc.n
                            mk.wait("v", sc)
                            mk.wait("v", tc, umm_done)
                            mk.ins("v", lambda E,
                                   o=actt_sh[:, ic * T + t4 * 512: ic * T + (t4 + 1) * 512],
                                   i=pgu[:, (4 + t4) * 512:(5 + t4) * 512],
                                   s_=silu_g[t4][:, :]:
                                   E.tensor_tensor(o, i, s_, ALU.mult), inc=vc)
                            ubank_vfree[t4] = vc.n
                            silu_free[t4] = vc.n
                    wg_free[pw] = tc.n
                act_done = vc.n
                mk.wait("t", mk.qwd, wd_ld)
                mk.wait("t", vc, act_done)
                for tch in range(NT):
                    for dh in range(2):
                        bank = 2 * (tch % 2) + dh
                        mk.wait("t", vc, dn_free[bank])
                        mk.wait("t", sc, gbank_sfree[bank])
                        for j in range(8):
                            mk.ins("t", lambda E,
                                   o=pgu[:, bank * 512:(bank + 1) * 512],
                                   l=actt_sh[:, j * T + tch * 128:
                                             j * T + (tch + 1) * 128],
                                   r=wdb[:, j * 1024 + dh * 512: j * 1024 + dh * 512 + 512],
                                   st=(j == 0), sp=(j == 7):
                                   E.matmul(o, l, r, start=st, stop=sp),
                                   inc=(tc if j == 7 else None))
                        dn_done = tc.n
                        a_sl = acc[:, tch * D + dh * 512: tch * D + dh * 512 + 512]
                        mk.wait("v", tc, dn_done)
                        if half == 0:
                            mk.ins("v", lambda E, o=a_sl,
                                   i=pgu[:, bank * 512:(bank + 1) * 512]:
                                   E.tensor_copy(o, i), inc=vc)
                        else:
                            mk.ins("v", lambda E, o=a_sl,
                                   i=pgu[:, bank * 512:(bank + 1) * 512]:
                                   E.tensor_tensor(o, o, i, ALU.add), inc=vc)
                        dn_free[bank] = vc.n
                wd_free = tc.n
        else:
            mk.wait("v", vc)
            mk.ins("v", lambda E: E.memset(acc[:, :], 0.0), inc=vc)
            mk.wait("v", vc)

        xstg3 = vb(0, 3 * 1024)
        ystg2 = vf(8192, 512)
        NSC = CAP // 128
        exp_list = [e for e in range(NE)
                    if branch_filter is None or e in branch_filter]
        mk.wait("q", mk.gsc[0])
        mk.wait("q", mk.gsc[1])
        dmaj_free = tc.n
        xstg_free = [dmaj_free] * 3
        ptr_free = [0, 0]
        xsT_free = [0, 0]
        xsT_done = [0, 0]
        acte_free = 0
        ystg_free = 0
        dnb_free = [0] * 6
        ptr_e = psAb[:, 0:2048]

        def issue_xstg_loads(e):
            lds = []
            for sc_ in range(NSC):
                mk.wait("q", tc, xstg_free[sc_])
                lds.append(mk.dma(
                    "q", xstg3[:, sc_ * D: sc_ * D + D],
                    scr_xsc[e * CAP + sc_ * 128: e * CAP + (sc_ + 1) * 128, :],
                    sem=mk.qxs[sc_]))
            return lds

        def prep_xsT(pbuf, lds):
            for sc_ in range(NSC):
                pxp = sc_ % 2
                mk.wait("t", mk.qxs[sc_], lds[sc_])
                mk.wait("t", sc, ptr_free[pxp])
                mk.wait("t", sc, gbank_sfree[pxp])
                mk.wait("t", vc, ubank_vfree[pxp])
                mk.wait("t", vc, dn_free[pxp])
                mk.wait("t", tc, xsT_free[pbuf])
                for c in range(ND):
                    mk.ins("t", lambda E,
                           o=ptr_e[:, pxp * 1024 + c * 128: pxp * 1024 + (c + 1) * 128],
                           i=xstg3[:, sc_ * D + c * 128: sc_ * D + (c + 1) * 128],
                           d=ident[:, :]: E.transpose(o, i, d),
                           inc=(tc if c == ND - 1 else None))
                xstg_free[sc_] = tc.n
                mk.wait("s", tc)
                mk.wait("s", tc, xsT_free[pbuf])
                mk.ins("s", lambda E,
                       o=xsT_e3[pbuf][:, :, sc_ * 128:(sc_ + 1) * 128],
                       i=ptr_e.rearrange("p (q c s) -> p q c s", q=2, c=ND)[:, pxp]:
                       E.copy(o, i), inc=sc)
                ptr_free[pxp] = sc.n
            xsT_done[pbuf] = sc.n

        pend_w = None
        for idx, e in enumerate(exp_list):
            pbuf = idx % 2
            if idx == 0:
                lds0 = issue_xstg_loads(e)
                prep_xsT(pbuf, lds0)
            wg_loaded = [None, None, None, None]

            def load_span(sp_i, ee=None):
                nonlocal spi
                ee = e if ee is None else ee
                pw = spi % 2
                mk.wait("q", tc, wg_free[pw])
                mk.wait("s", tc, wg_free[pw])
                wg_loaded[sp_i] = load_wgu(ee, sp_i * 512, pw)
                spi += 1

            if pend_w is None:
                if b == 0:
                    mk.wait("q", mk.cv_ex[e])
                    mk.wait("s", mk.cv_ex[e])
                load_span(0)
                load_span(1)
            else:
                wg_loaded[0], wg_loaded[1] = pend_w
            mk.wait("q", tc, wd_free)
            mk.wait("s", tc, wd_free)
            wd_ld = load_wd(e, list(range(16)), 0)
            for sp_i in range(4):
                pw, _wgl = wg_loaded[sp_i]
                mk.wait("t", mk.qgw[pw], _wgl)
                mk.wait("t", sc, xsT_done[pbuf])
                for il in range(4):
                    ic = sp_i * 4 + il
                    gb = 2 * (il % 2)
                    ub = 2 * (il % 2) + 1
                    mk.wait("t", sc, gbank_sfree[gb])
                    mk.wait("t", vc, ubank_vfree[ub])
                    mk.wait("t", vc, dn_free[gb])
                    mk.wait("t", vc, dn_free[ub])
                    if gb >= 2:
                        mk.wait("t", sc, dnb_free[gb - 2])
                    if ub >= 2:
                        mk.wait("t", sc, dnb_free[ub - 2])
                    for c in range(ND):
                        mk.ins("t", lambda E,
                               o=pgu[:, gb * 512: gb * 512 + CAP],
                               l=wgb[:, (pw * ND + c) * 512 + il * 128:
                                      (pw * ND + c) * 512 + (il + 1) * 128],
                               r=xsT_e3[pbuf].rearrange("p c s -> p (c s)")[:, c * CAP:(c + 1) * CAP],
                               st=(c == 0), sp=(c == ND - 1):
                               E.matmul(o, l, r, start=st, stop=sp),
                               inc=(tc if c == ND - 1 else None))
                    gmm_done = tc.n
                    for c in range(ND):
                        mk.ins("t", lambda E,
                               o=pgu[:, ub * 512: ub * 512 + CAP],
                               l=wub[:, (pw * ND + c) * 512 + il * 128:
                                      (pw * ND + c) * 512 + (il + 1) * 128],
                               r=xsT_e3[pbuf].rearrange("p c s -> p (c s)")[:, c * CAP:(c + 1) * CAP],
                               st=(c == 0), sp=(c == ND - 1):
                               E.matmul(o, l, r, start=st, stop=sp),
                               inc=(tc if c == ND - 1 else None))
                    umm_done = tc.n
                    sgi = il % 2
                    mk.wait("s", tc, gmm_done)
                    mk.wait("s", vc, silu_free[sgi])
                    mk.ins("s", lambda E, o=silu_g[sgi][:, 0:CAP],
                           i=pgu[:, gb * 512: gb * 512 + CAP]:
                           E.activation(out=o, in_=i, func=AF.Silu), inc=sc)
                    gbank_sfree[gb] = sc.n
                    mk.wait("v", sc)
                    mk.wait("v", tc, umm_done)
                    mk.wait("v", tc, acte_free)
                    mk.ins("v", lambda E,
                           o=acte[:, ic * CAP:(ic + 1) * CAP],
                           i=pgu[:, ub * 512: ub * 512 + CAP],
                           s_=silu_g[sgi][:, 0:CAP]:
                           E.tensor_tensor(o, i, s_, ALU.mult), inc=vc)
                    ubank_vfree[ub] = vc.n
                    silu_free[sgi] = vc.n
                wg_free[pw] = tc.n
                if sp_i + 2 < 4:
                    load_span(sp_i + 2)
            act_done = vc.n
            xsT_free[pbuf] = tc.n
            if idx + 1 < len(exp_list):
                e_next = exp_list[idx + 1]
                if b == 0:
                    mk.wait("q", mk.cv_ex[e_next])
                    mk.wait("s", mk.cv_ex[e_next])
                load_span(0, ee=e_next)
                load_span(1, ee=e_next)
                pend_w = (wg_loaded[0], wg_loaded[1])
                lds_n = issue_xstg_loads(e_next)
            mk.wait("t", mk.qwd, wd_ld)
            mk.wait("t", vc, act_done)
            for bnk in range(6):
                mk.wait("t", sc, dnb_free[bnk])
            for ic in range(16):
                for sc_ in range(NSC):
                    for dh in range(2):
                        bank = 2 + sc_ * 2 + dh
                        mk.ins("t", lambda E,
                               o=pgu[:, bank * 512:(bank + 1) * 512],
                               l=acte[:, ic * CAP + sc_ * 128: ic * CAP + (sc_ + 1) * 128],
                               r=wdb[:, ic * 1024 + dh * 512: ic * 1024 + dh * 512 + 512],
                               st=(ic == 0), sp=(ic == 15):
                               E.matmul(o, l, r, start=st, stop=sp),
                               inc=(tc if (ic == 15 and sc_ == NSC - 1 and dh == 1) else None))
            dnm_done = tc.n
            acte_free = tc.n
            wd_free = tc.n
            if idx + 1 < len(exp_list):
                prep_xsT(1 - pbuf, lds_n)
            for sc_ in range(NSC):
                for dh in range(2):
                    bank = 2 + sc_ * 2 + dh
                    mk.wait("s", tc, dnm_done)
                    mk.wait("s", mk.qy, ystg_free)
                    mk.ins("s", lambda E, o=ystg2[:, :],
                           i=pgu[:, bank * 512:(bank + 1) * 512]:
                           E.copy(o, i), inc=sc)
                    dnb_free[sc_ * 2 + dh] = sc.n
                    mk.wait("q", sc)
                    ystg_free = mk.dma(
                        "q",
                        scr_y[e * CAP + sc_ * 128: e * CAP + (sc_ + 1) * 128,
                              dh * 512:(dh + 1) * 512],
                        ystg2[:, :], sem=mk.qy)
        mk.fence()

        mk.phase(6)
        grt = vf(131072, 2 * D)
        mk.wait("v", vc)
        mk.ins("v", lambda E: E.reciprocal(zrec[:, :], zacc[:, :]), inc=vc)
        zr_done = vc.n
        mk.wait("s", vc, zr_done)
        mk.wait("g", mk.qy)
        cxt_free = [0, 0]
        csn_free = [0, 0]
        cgt_free = [0, 0]
        grt_free = [0, 0]
        cot_free = [0, 0]
        for k in range(NT):
            p = k % 2
            mk.wait("q", vc, cxt_free[p])
            mk.dma("q", cxt[:, p * D:(p + 1) * D], x[b, k * 128:(k + 1) * 128, :],
                   sem=mk.qc[p])
            mk.wait("q", vc, csn_free[p])
            ld_done = mk.dma("q", csn[:, p * D:(p + 1) * D],
                             scr_s[k * 128:(k + 1) * 128, :], sem=mk.qc[p])
            mk.wait("v", sc, grt_free[p])
            mk.ins("v", lambda E, o=grt[:, p * D:(p + 1) * D]: E.memset(o, 0.0), inc=vc)
            gz = vc.n
            mk.wait("g", vc, gz)
            if not mk.mute:
                mk.ggr[p].n += 16
            mk.emit("g", lambda E, o=grt[:, p * D:(p + 1) * D],
                    ix=sidx_sb[:, k:k + 1], h=mk.ggr[p].h:
                    E.indirect_dma_start(out=o, out_offset=None,
                                         in_=scr_y[:, :],
                                         in_offset=IOX(ap=ix, axis=0),
                                         bounds_check=breg,
                                         oob_is_err=False).then_inc(h, 16))
            g_ld = mk.ggr[p].n
            mk.wait("v", mk.ggr[p], g_ld)
            mk.ins("v", lambda E, o=grt[:, p * D:(p + 1) * D],
                   a=grt[:, p * D:(p + 1) * D], b_=acc[:, k * D:(k + 1) * D]:
                   E.tensor_tensor(o, a, b_, ALU.add), inc=vc)
            ffn_done = vc.n
            mk.wait("s", vc, ffn_done)
            mk.wait("s", vc, cgt_free[p])
            mk.ins("s", lambda E, o=cgt[:, p * D:(p + 1) * D],
                   i=grt[:, p * D:(p + 1) * D]:
                   E.activation(out=o, in_=i, func=AF.Silu), inc=sc)
            grt_free[p] = sc.n
            mk.wait("v", sc)
            mk.wait("v", mk.qc[p], ld_done)
            mk.wait("v", vc, zr_done)
            mk.wait("v", mk.qo[p], cot_free[p])
            mk.ins("v", lambda E, o=cot[:, p * D:(p + 1) * D],
                   i=csn[:, p * D:(p + 1) * D], z=zrec[:, k:k + 1],
                   g_=cgt[:, p * D:(p + 1) * D]: E.scalar_tensor_tensor(
                   out=o, in0=i, scalar=z, in1=g_, op0=ALU.mult, op1=ALU.mult), inc=vc)
            csn_free[p] = vc.n
            cgt_free[p] = vc.n
            mk.wait("v", vc)
            mk.ins("v", lambda E, o=cot[:, p * D:(p + 1) * D],
                   a=cot[:, p * D:(p + 1) * D], b_=cxt[:, p * D:(p + 1) * D]:
                   E.tensor_tensor(o, a, b_, ALU.add), inc=vc)
            cxt_free[p] = vc.n
            mk.wait("q", vc)
            cot_free[p] = mk.dma("q", out[b, k * 128:(k + 1) * 128, :],
                                 cot[:, p * D:(p + 1) * D], sem=mk.qo[p])
        mk.fence()

    for s_ in mk.final_sems:
        mk.wait("q", s_)
    mk.replay()
    es.close()
    return nc


_IDENT = np.concatenate([
    np.eye(128, dtype=ml_dtypes.bfloat16),
    np.triu(np.ones((128, 128), dtype=ml_dtypes.bfloat16)),
    np.ones((128, 128), dtype=ml_dtypes.bfloat16),
], axis=1)
_CVEC = np.zeros((128, 16), np.float32)
_CVEC[:, 0] = EPS
_CVEC[:, 1] = 1.0
_CVEC[:, 2:10] = np.arange(8, dtype=np.float32)[None, :]


def kernel(L0_attn_x, L0_attn_U, L0_ffn_gate_w, L0_ffn_sh_gate, L0_ffn_sh_up,
           L0_ffn_sh_down, L0_ffn_rt_gate, L0_ffn_rt_up, L0_ffn_rt_down):
    xf = np.ascontiguousarray(np.asarray(L0_attn_x, dtype=np.float32))
    wts = {
        "w_u": np.ascontiguousarray(np.asarray(L0_attn_U, np.float32)),
        "w_gate": np.ascontiguousarray(np.asarray(L0_ffn_gate_w, np.float32)),
        "w_shg": np.ascontiguousarray(np.asarray(L0_ffn_sh_gate, np.float32)),
        "w_shu": np.ascontiguousarray(np.asarray(L0_ffn_sh_up, np.float32)),
        "w_shd": np.ascontiguousarray(np.asarray(L0_ffn_sh_down, np.float32)),
        "w_rtg": np.ascontiguousarray(np.asarray(L0_ffn_rt_gate, np.float32)),
        "w_rtu": np.ascontiguousarray(np.asarray(L0_ffn_rt_up, np.float32)),
        "w_rtd": np.ascontiguousarray(np.asarray(L0_ffn_rt_down, np.float32)),
        "ident": _IDENT,
        "cvec": _CVEC,
    }
    nc = build_program()
    in_maps = []
    for c in range(NCORES):
        m = dict(wts)
        m["x"] = np.ascontiguousarray(xf[c * BPC:(c + 1) * BPC])
        in_maps.append(m)
    res = run_bass_kernel_spmd(nc, in_maps, core_ids=list(range(NCORES)))
    return np.concatenate([r["out"] for r in res.results], axis=0)


# version 14
# speedup vs baseline: 1.1070x; 1.0077x over previous
import numpy as np
import ml_dtypes
from contextlib import ExitStack

import concourse.bass as bass
import concourse.mybir as mybir
from concourse.bass_utils import run_bass_kernel_spmd
from concourse.bass import IndirectOffsetOnAxis as IOX

F32 = mybir.dt.float32
BF16 = mybir.dt.bfloat16
AF = mybir.ActivationFunctionType
ALU = mybir.AluOpType
AX = mybir.AxisListType

NCORES = 8
B_ALL, T, D = 16, 2048, 1024
BPC = B_ALL // NCORES
NT = T // 128
ND = D // 128
IFF = 2048
NE = 8
R = 32
CAP = 384
EPS = 1e-5

ENGS = ("t", "v", "s", "q", "g")


class Sem:
    __slots__ = ("h", "n")

    def __init__(self, h):
        self.h = h
        self.n = 0


class MK:

    def __init__(self, nc, es, dbg=None, max_ph=99):
        self.nc = nc
        self.es = es
        self.code = {e: [] for e in ENGS}
        self.dbg = dbg or {}
        self.max_ph = max_ph
        self.mute = False
        self.tc = self.sem("tc")
        self.vc = self.sem("vc")
        self.sc = self.sem("sc")
        self.qd = self.sem("qd")
        self.gd = self.sem("gd")
        self.fs = self.sem("fs")
        self.qx = [self.sem("qx0"), self.sem("qx1")]
        self.qs = [self.sem("qs0"), self.sem("qs1")]
        self.qc = [self.sem("qc0"), self.sem("qc1")]
        self.qo = [self.sem("qo0"), self.sem("qo1")]
        self.qa = self.sem("qa")
        self.gw = [self.sem("gw0"), self.sem("gw1")]
        self.gwd = self.sem("gwd")
        self.gc = self.sem("gc")
        self.qy = self.sem("qy")
        self.gsc = [self.sem("gsc0"), self.sem("gsc1")]
        self.ggr = [self.sem("ggr0"), self.sem("ggr1")]
        self.qz = self.sem("qz")
        self.qwb = [self.sem("qwb0"), self.sem("qwb1")]
        self.qwbd = self.sem("qwbd")
        self.qxs = [self.sem("qxs0"), self.sem("qxs1"), self.sem("qxs2")]
        self.qgw = [self.sem("qgw0"), self.sem("qgw1")]
        self.qwd = self.sem("qwd")
        self.cv_sh = self.sem("cv_sh")
        self.cv_ex = [self.sem(f"cv_ex{i}") for i in range(NE)]
        self.dma_sems = [self.qd, self.gd, self.qa, self.gwd, self.qy,
                         self.qz, self.qwbd, self.qwd] + \
            self.qwb + self.qxs + \
            self.qx + self.qs + self.qc + self.qo + self.gw + self.gsc + \
            self.ggr + self.qgw
        self.final_sems = self.dma_sems + [self.cv_sh] + self.cv_ex

    def sem(self, name):
        return Sem(self.es.enter_context(self.nc.semaphore(name)))

    def phase(self, n):
        self.mute = n > self.max_ph

    def emit(self, eng, fn):
        if self.mute:
            return
        self.code[eng].append(fn)

    def wait(self, eng, sem, val=None):
        v = sem.n if val is None else val
        if v <= 0:
            return
        self.emit(eng, lambda E, h=sem.h, v=v: E.wait_ge(h, v))

    def ins(self, eng, build, inc=None, k=1):
        if self.mute:
            return
        if inc is None:
            self.emit(eng, lambda E, b=build: b(E))
        else:
            inc.n += k
            self.emit(eng, lambda E, b=build, h=inc.h, k=k: b(E).then_inc(h, k))

    def dma(self, eng_q, out, in_, sem=None, **kw):
        if sem is None:
            sem = self.qd if eng_q == "q" else self.gd
        if self.mute:
            return sem.n
        sem.n += 16
        self.emit(eng_q, lambda E, o=out, i=in_, h=sem.h, kw=kw:
                  E.dma_start(out=o, in_=i, **kw).then_inc(h, 16))
        return sem.n

    def fence(self):
        if self.mute:
            return
        tgts = [(s, s.n) for s in [self.tc, self.vc, self.sc, self.gc] + self.dma_sems]
        self.fs.n += len(ENGS)
        tgt = self.fs.n
        for e in ENGS:
            for s, v in tgts:
                self.wait(e, s, v)
            self.emit(e, lambda E, h=self.fs.h: E.sem_inc(h, 1))
            self.emit(e, lambda E, h=self.fs.h, v=tgt: E.wait_ge(h, v))

    def tap(self, name, sb_ap):
        if name not in self.dbg:
            return
        for s in (self.tc, self.vc, self.sc):
            self.wait("g", s)
        self.dma("g", self.dbg[name][:, :], sb_ap)
        self.fence()

    def replay(self):
        nc = self.nc
        with nc.Block() as block:
            @block.tensor
            def _(E):
                for fn in self.code["t"]:
                    fn(E)

            @block.vector
            def _(E):
                for fn in self.code["v"]:
                    fn(E)

            @block.scalar
            def _(E):
                for fn in self.code["s"]:
                    fn(E)

            @block.sync
            def _(E):
                for fn in self.code["q"]:
                    fn(E)

            @block.gpsimd
            def _(E):
                for fn in self.code["g"]:
                    fn(E)


def build_program(dbg_names=(), max_ph=99, branch_filter=None, detect_races=True, nbat=BPC):
    nc = bass.Bass("TRN2", target_bir_lowering=False, detect_race_conditions=detect_races)
    es = ExitStack()
    U8 = mybir.dt.uint8

    x = nc.dram_tensor("x", [BPC, T, D], F32, kind="ExternalInput").ap()
    w_u = nc.dram_tensor("w_u", [D, R], F32, kind="ExternalInput").ap()
    w_gate = nc.dram_tensor("w_gate", [D, NE], F32, kind="ExternalInput").ap()
    w_shg = nc.dram_tensor("w_shg", [D, IFF], F32, kind="ExternalInput").ap()
    w_shu = nc.dram_tensor("w_shu", [D, IFF], F32, kind="ExternalInput").ap()
    w_shd = nc.dram_tensor("w_shd", [IFF, D], F32, kind="ExternalInput").ap()
    w_rtg = nc.dram_tensor("w_rtg", [NE, D, IFF], F32, kind="ExternalInput").ap()
    w_rtu = nc.dram_tensor("w_rtu", [NE, D, IFF], F32, kind="ExternalInput").ap()
    w_rtd = nc.dram_tensor("w_rtd", [NE, IFF, D], F32, kind="ExternalInput").ap()
    ident_in = nc.dram_tensor("ident", [128, 3 * 128], BF16, kind="ExternalInput").ap()
    cvec_in = nc.dram_tensor("cvec", [128, 16], F32, kind="ExternalInput").ap()
    out = nc.dram_tensor("out", [BPC, T, D], F32, kind="ExternalOutput").ap()
    scr_s = nc.dram_tensor("scr_s", [T, D], F32, kind="Internal").ap()
    scr_xsc = nc.dram_tensor("scr_xsc", [NE * CAP, D], BF16, kind="Internal").ap()
    scr_y = nc.dram_tensor("scr_y", [NE * CAP, D], F32, kind="Internal").ap()
    scr_wg = nc.dram_tensor("scr_wg", [9, D, IFF], BF16, kind="Internal").ap()
    scr_wu = nc.dram_tensor("scr_wu", [9, D, IFF], BF16, kind="Internal").ap()
    scr_wdn = nc.dram_tensor("scr_wdn", [9, IFF, D], BF16, kind="Internal").ap()

    dbg = {}
    for nm, shp, dt in [
        ("dbg_xnt", [128, ND * T], F32), ("dbg_logits", [128, NT * NE], F32),
        ("dbg_score", [128, NT], F32), ("dbg_mask", [128, NT * NE], F32),
        ("dbg_hi", [128, NT * D], F32), ("dbg_lo", [128, NT * D], F32),
        ("dbg_yb", [32, T], F32), ("dbg_n", [1, T], F32),
        ("dbg_z", [128, NT], F32), ("dbg_acc", [128, NT * D], F32),
        ("dbg_xst", [128, ND * T], F32),
    ]:
        if nm in dbg_names:
            dbg[nm] = nc.dram_tensor(nm, shp, dt, kind="ExternalOutput").ap()

    mk = MK(nc, es, dbg, max_ph=max_ph)
    tc, vc, sc, qd, gd = mk.tc, mk.vc, mk.sc, mk.qd, mk.gd
    breg = es.enter_context(nc.gpsimd.register("breg"))
    mk.emit("g", lambda E: E.reg_mov(breg, NE * CAP - 1))

    identb3 = es.enter_context(nc.sbuf_tensor("identb", [128, 3 * 128], BF16))
    ident = identb3[:, 0:128]
    tril_b = identb3[:, 128:256]
    ones_b = identb3[:, 256:384]
    cvec = es.enter_context(nc.sbuf_tensor("cvecb", [128, 16], F32))
    u_bf = es.enter_context(nc.sbuf_tensor("u_bf", [128, ND * R], BF16))
    gw_hi = es.enter_context(nc.sbuf_tensor("gw_hi", [128, ND * NE], BF16))
    gw_lo = es.enter_context(nc.sbuf_tensor("gw_lo", [128, ND * NE], BF16))
    gw_f = es.enter_context(nc.sbuf_tensor("gw_f", [128, ND * NE], F32))
    gw_t = es.enter_context(nc.sbuf_tensor("gw_t", [128, ND * NE], F32))
    logits_sb = es.enter_context(nc.sbuf_tensor("logits_sb", [128, NT * NE], F32))
    tv_sb = es.enter_context(nc.sbuf_tensor("tv_sb", [128, NT], F32))
    score_sb = es.enter_context(nc.sbuf_tensor("score_sb", [128, NT], F32))
    mask_sb = es.enter_context(nc.sbuf_tensor("mask_sb", [128, NT * NE], F32))
    zacc = es.enter_context(nc.sbuf_tensor("zacc", [128, NT], F32))
    zscr = es.enter_context(nc.sbuf_tensor("zscr", [128, NT], F32))
    zrec = es.enter_context(nc.sbuf_tensor("zrec", [128, NT], F32))
    silu_g = [es.enter_context(nc.sbuf_tensor(f"silu_g{p}", [128, 512], F32)) for p in range(4)]
    mcol = es.enter_context(nc.sbuf_tensor("mcol", [128, 4], F32))
    mask_bf = es.enter_context(nc.sbuf_tensor("mask_bf", [128, NT * NE], BF16))
    pos_sb = es.enter_context(nc.sbuf_tensor("pos_sb", [128, NT * NE], F32))
    sidx_sb = es.enter_context(nc.sbuf_tensor("sidx_sb", [128, NT], mybir.dt.uint32))
    tmp8a = es.enter_context(nc.sbuf_tensor("tmp8a", [128, NE], F32))
    tmp8b = es.enter_context(nc.sbuf_tensor("tmp8b", [128, NE], F32))
    ridx_f = es.enter_context(nc.sbuf_tensor("ridx_f", [128, 4], F32))
    zero_tile = es.enter_context(nc.sbuf_tensor("zero_tile", [128, D], BF16))
    ssq = es.enter_context(nc.sbuf_tensor("ssq", [128, 4], F32))
    rstd = es.enter_context(nc.sbuf_tensor("rstd", [128, 4], F32))

    AR_BYTES = 196608
    arena = es.enter_context(nc.sbuf_tensor("arena", [128, AR_BYTES], U8))
    arf = arena.bitcast(F32)
    arb = arena.bitcast(BF16)

    def vf(off, n):
        return arf[:, off // 4: off // 4 + n]

    def vb(off, n):
        return arb[:, off // 2: off // 2 + n]

    dmaj = vb(0, ND * T)
    dmaj3 = dmaj.rearrange("p (c t) -> p c t", c=ND)
    hi_tm = vb(32768, NT * D)
    lo_tm = vb(65536, NT * D)
    xst = vf(98304, 2 * D)
    tm32a = vf(106496, 2 * D)
    tm32b = vf(114688, 2 * D)
    loch = vb(122880, 2 * D)
    auga = vb(126976, T)[0:36]
    augb = vb(131072, T)[0:36]
    ybf = vb(135168, T)[0:32]
    ysq = vf(139264, 512)[0:32]
    nrow = vf(141312, T)[0:1]
    nt1 = vf(149504, T)[0:1]
    nnhi = vb(157696, T)[0:1]
    nnlo = vb(161792, T)[0:1]
    ones_row = vb(165888, T)[0:1]
    qcol = vb(169984, NT * 512)
    sstg = vf(186368, 2 * D)
    xsb = vb(169984, 2 * D)
    acc = vf(32768, NT * D)
    actt = vb(98304, ND * T)
    wgb = vb(131072, 2 * ND * 512)
    wub = vb(147456, 2 * ND * 512)
    wdb = vb(163840, 16 * 1024)
    cxt = vf(98304, 2 * D)
    csn = vf(106496, 2 * D)
    cgt = vf(114688, 2 * D)
    cot = vf(122880, 2 * D)

    psA = es.enter_context(nc.psum_tensor("psA", [128, 4096], F32))
    psAb = psA.bitcast(BF16)
    ptrh = psAb[:, 0:2048]
    ptrl = psAb[:, 2048:4096]
    plg = psA[:, 2048:2048 + NE]
    pyp = psA[:, 2560:3072][0:32]
    pnp = psA[:, 3072:3584][0:1]
    pq = psA[:, 0:2048]
    ps = psA[:, 2048:4096]
    ptx = psAb[:, 0:2048]
    pgu = psA[:, 0:4096]

    mk.dma("q", identb3[:, :], ident_in[:, :])
    mk.dma("q", cvec[:, :], cvec_in[:, :])
    mk.dma("q", gw_f.rearrange("p (c e) -> p c e", c=ND),
           w_gate.rearrange("(c p) e -> p c e", p=128))
    for c in range(ND):
        mk.dma("g", gw_hi[:, c * NE:(c + 1) * NE], w_gate[c * 128:(c + 1) * 128, :])
        mk.dma("g", u_bf[:, c * R:(c + 1) * R], w_u[c * 128:(c + 1) * 128, :])
    mk.dma("g", scr_wg[8], w_shg[:, :], sem=mk.cv_sh, max_dma_last_dim=8192)
    mk.dma("g", scr_wu[8], w_shu[:, :], sem=mk.cv_sh, max_dma_last_dim=8192)
    mk.dma("g", scr_wdn[8], w_shd[:, :], sem=mk.cv_sh, max_dma_last_dim=8192)
    for e_ in range(NE):
        mk.dma("g", scr_wg[e_], w_rtg[e_], sem=mk.cv_ex[e_], max_dma_last_dim=8192)
        mk.dma("g", scr_wu[e_], w_rtu[e_], sem=mk.cv_ex[e_], max_dma_last_dim=8192)
        mk.dma("g", scr_wdn[e_], w_rtd[e_], sem=mk.cv_ex[e_], max_dma_last_dim=8192)
    mk.wait("v", gd)
    mk.wait("v", qd)
    mk.ins("v", lambda E: E.tensor_copy(gw_t[:, :], gw_hi[:, :]), inc=vc)
    mk.wait("v", vc)
    mk.ins("v", lambda E: E.tensor_tensor(gw_t[:, :], gw_f[:, :], gw_t[:, :], ALU.subtract), inc=vc)
    mk.wait("v", vc)
    mk.ins("v", lambda E: E.tensor_copy(gw_lo[:, :], gw_t[:, :]), inc=vc)
    mk.ins("v", lambda E: E.memset(zero_tile[:, :], 0.0), inc=vc)
    zf = vc.n
    mk.wait("q", vc, zf)
    for zi in range((NE * CAP) // 128):
        mk.dma("q", scr_xsc[zi * 128:(zi + 1) * 128, :], zero_tile[:, :], sem=mk.qz)

    branches = [("e", e, w_rtg, w_rtu, w_rtd) for e in range(NE)] + \
               [("sh", None, w_shg, w_shu, w_shd)]
    if branch_filter is not None:
        branches = [br for br in branches
                    if (br[0] == "sh" and "sh" in branch_filter) or
                       (br[0] == "e" and br[1] in branch_filter)]

    for b in range(nbat):
        mk.phase(1)
        xst_free = [0, 0]
        xnf_free = [0, 0]
        cent_free = [0, 0]
        ptrh_free = [0, 0]
        ptrl_free = [0, 0]
        loch_free = [0, 0]
        lg_free = 0
        pend_tv = []

        def emit_tv(kk, lgv):
            mk.wait("v", sc, lgv)
            mk.ins("v", lambda E, o=tv_sb[:, kk:kk + 1],
                   i=logits_sb[:, kk * NE:(kk + 1) * NE]:
                   E.reduce_max(o, i, axis=AX.X), inc=vc)
            mk.wait("v", vc)
            mk.ins("v", lambda E, o=mask_sb[:, kk * NE:(kk + 1) * NE],
                   i=logits_sb[:, kk * NE:(kk + 1) * NE], s_=tv_sb[:, kk:kk + 1]:
                   E.tensor_scalar(out=o, in0=i, scalar1=s_, scalar2=None,
                                   op0=ALU.is_equal), inc=vc)

        def stage1(k):
            p = k % 2
            xs_ = xst[:, p * D:(p + 1) * D]
            cent = tm32a[:, p * D:(p + 1) * D]
            mk.wait("q", vc, xst_free[p])
            x_ld = mk.dma("q", xs_, x[b, k * 128:(k + 1) * 128, :], sem=mk.qx[p])
            mk.wait("v", mk.qx[p], x_ld)
            mk.ins("v", lambda E, o=mcol[:, p:p + 1], i=xs_:
                   E.reduce_sum(o, i, axis=AX.X), inc=vc)
            mk.wait("v", vc)
            mk.ins("v", lambda E, o=mcol[:, p:p + 1]: E.tensor_scalar(
                out=o, in0=o, scalar1=-1.0 / D, scalar2=None, op0=ALU.mult), inc=vc)
            mk.wait("v", vc)
            mk.wait("v", sc, cent_free[p])
            mk.ins("v", lambda E, o=cent, i=xs_, m=mcol[:, p:p + 1]: E.tensor_scalar(
                out=o, in0=i, scalar1=m, scalar2=None, op0=ALU.add), inc=vc)
            xst_free[p] = vc.n
            return vc.n

        def stage2a(k, cent_done):
            p = k % 2
            cent = tm32a[:, p * D:(p + 1) * D]
            xnf = tm32b[:, p * D:(p + 1) * D]
            hi_sl = hi_tm[:, k * D:(k + 1) * D]
            lo_sl = lo_tm[:, k * D:(k + 1) * D]
            mk.wait("s", vc, cent_done)
            mk.wait("s", vc, xnf_free[p])
            mk.ins("s", lambda E, o=xnf, i=cent, a=ssq[:, p:p + 1]: E.activation(
                out=o, in_=i, func=AF.Square, accum_out=a), inc=sc)
            mk.wait("s", sc)
            mk.ins("s", lambda E, o=rstd[:, p:p + 1], i=ssq[:, p:p + 1]: E.activation(
                out=o, in_=i, func=AF.Ln, scale=1.0 / D, bias=cvec[:, 0:1]), inc=sc)
            mk.wait("s", sc)
            mk.ins("s", lambda E, o=rstd[:, p:p + 1], i=rstd[:, p:p + 1]: E.activation(
                out=o, in_=i, func=AF.Exp, scale=-0.5), inc=sc)
            mk.wait("s", sc)
            mk.ins("s", lambda E, o=xnf, i=cent, r=rstd[:, p:p + 1]: E.activation(
                out=o, in_=i, func=AF.Copy, scale=r), inc=sc)
            cent_free[p] = sc.n
            mk.wait("s", sc)
            mk.ins("s", lambda E, o=hi_sl, i=xnf: E.copy(o, i), inc=sc)
            hi_done = sc.n
            mk.wait("v", sc, hi_done)
            mk.ins("v", lambda E, o=lo_sl, a=hi_sl, b_=xnf:
                   E.scalar_tensor_tensor(out=o, in0=a, scalar=-1.0, in1=b_,
                                          op0=ALU.mult, op1=ALU.add), inc=vc)
            xnf_free[p] = vc.n
            hilo_done = vc.n
            mk.wait("t", sc, hi_done)
            mk.wait("t", vc, hilo_done)
            mk.wait("t", sc, ptrh_free[p])
            for c in range(ND):
                mk.ins("t", lambda E,
                       o=ptrh[:, p * 1024 + c * 128: p * 1024 + (c + 1) * 128],
                       i=hi_sl[:, c * 128:(c + 1) * 128], d=ident[:, :]:
                       E.transpose(o, i, d), inc=(tc if c == ND - 1 else None))
            trh_done = tc.n
            mk.wait("t", vc, ptrl_free[p])
            for c in range(ND):
                mk.ins("t", lambda E,
                       o=ptrl[:, p * 1024 + c * 128: p * 1024 + (c + 1) * 128],
                       i=lo_sl[:, c * 128:(c + 1) * 128], d=ident[:, :]:
                       E.transpose(o, i, d), inc=(tc if c == ND - 1 else None))
            trl_done = tc.n
            return (k, trh_done, trl_done)

        def stage2b(k, trh_done, trl_done):
            p = k % 2
            mk.wait("s", tc, trh_done)
            mk.ins("s", lambda E, o=dmaj3[:, :, k * 128:(k + 1) * 128],
                   i=ptrh.rearrange("p (q c t) -> p q c t", q=2, c=ND)[:, p]:
                   E.copy(o, i), inc=sc)
            ptrh_free[p] = sc.n
            hi_evac = sc.n
            mk.wait("v", tc, trl_done)
            mk.wait("v", tc, loch_free[p])
            mk.ins("v", lambda E, o=loch[:, p * 1024:(p + 1) * 1024],
                   i=ptrl[:, p * 1024:(p + 1) * 1024]: E.tensor_copy(o, i), inc=vc)
            ptrl_free[p] = vc.n
            lo_evac = vc.n
            mk.wait("t", sc, hi_evac)
            mk.wait("t", vc, lo_evac)
            mk.wait("t", sc, lg_free2[0])
            nmm = 3 * ND
            j = 0
            for pass_i in range(3):
                wsb = gw_hi if pass_i != 1 else gw_lo
                for c in range(ND):
                    if pass_i < 2:
                        lh = dmaj[:, c * T + k * 128: c * T + (k + 1) * 128]
                    else:
                        lh = loch[:, p * 1024 + c * 128: p * 1024 + (c + 1) * 128]
                    mk.ins("t", lambda E, o=plg[:, :], l=lh,
                           r=wsb[:, c * NE:(c + 1) * NE],
                           st=(j == 0), sp=(j == nmm - 1):
                           E.matmul(o, l, r, start=st, stop=sp),
                           inc=(tc if j == nmm - 1 else None))
                    j += 1
            loch_free[p] = tc.n
            mk.wait("s", tc)
            mk.ins("s", lambda E, o=logits_sb[:, k * NE:(k + 1) * NE],
                   i=plg[:, :]: E.copy(o, i), inc=sc)
            lg_free2[0] = sc.n
            pend_tv.append((k, sc.n))

        lg_free2 = [0]
        prev = None
        prevb = None
        for k in range(NT):
            cd = stage1(k)
            if len(pend_tv) >= 2:
                emit_tv(*pend_tv.pop(0))
            if prev is not None:
                nb = stage2a(*prev)
                if prevb is not None:
                    stage2b(*prevb)
                prevb = nb
            prev = (k, cd)
        nb = stage2a(*prev)
        stage2b(*prevb)
        stage2b(*nb)
        for pt in pend_tv:
            emit_tv(*pt)
        pend_tv = []
        mk.tap("dbg_hi", hi_tm[:, :])
        mk.tap("dbg_lo", lo_tm[:, :])
        mk.tap("dbg_xnt", dmaj[:, :])
        mk.tap("dbg_logits", logits_sb[:, :])
        mk.tap("dbg_mask", mask_sb[:, :])
        mk.fence()

        mk.phase(2)
        mk.ins("v", lambda E: E.memset(ones_row[:, :], 1.0), inc=vc)
        for js in range(4):
            for c in range(ND):
                mk.ins("t", lambda E, o=pyp[:, :], l=u_bf[:, c * R:(c + 1) * R],
                       r=dmaj[:, c * T + js * 512: c * T + js * 512 + 512],
                       st=(c == 0), sp=(c == ND - 1):
                       E.matmul(o, l, r, start=st, stop=sp),
                       inc=(tc if c == ND - 1 else None))
            mk.wait("s", tc)
            mk.ins("s", lambda E, o=ybf[:, js * 512:(js + 1) * 512],
                   i=pyp[:, :]: E.copy(o, i), inc=sc)
            mk.wait("s", sc)
            mk.ins("s", lambda E, o=ysq[:, :],
                   i=ybf[:, js * 512:(js + 1) * 512]:
                   E.activation(out=o, in_=i, func=AF.Square), inc=sc)
            mk.wait("t", sc)
            mk.ins("t", lambda E, o=pnp[:, :], l=cvec[0:32, 1:2],
                   r=ysq[:, :]:
                   E.matmul(o, l, r, start=True, stop=True), inc=tc)
            mk.wait("s", tc)
            mk.ins("s", lambda E, o=nrow[:, js * 512:(js + 1) * 512],
                   i=pnp[:, :]: E.copy(o, i), inc=sc)
            mk.wait("t", sc)
        mk.wait("v", sc)
        mk.ins("v", lambda E: E.tensor_copy(nnhi[:, :], nrow[:, :]), inc=vc)
        mk.wait("v", vc)
        mk.ins("v", lambda E: E.tensor_copy(nt1[:, :], nnhi[:, :]), inc=vc)
        mk.wait("v", vc)
        mk.ins("v", lambda E: E.tensor_scalar(out=nnhi[:, :], in0=nnhi[:, :],
               scalar1=-1.0, scalar2=None, op0=ALU.mult), inc=vc)
        mk.wait("v", vc)
        mk.ins("v", lambda E: E.tensor_tensor(nt1[:, :], nrow[:, :], nt1[:, :],
                                              ALU.subtract), inc=vc)
        mk.wait("v", vc)
        mk.ins("v", lambda E: E.tensor_scalar(out=nnlo[:, :], in0=nt1[:, :],
               scalar1=-1.0, scalar2=None, op0=ALU.mult), inc=vc)
        mk.wait("v", vc)
        mk.ins("v", lambda E: E.tensor_scalar(out=auga[0:32, :], in0=ybf[:, :],
               scalar1=2.0, scalar2=None, op0=ALU.mult), inc=vc)
        mk.wait("v", vc)
        mk.ins("v", lambda E: E.tensor_copy(augb[0:32, :], ybf[:, :]), inc=vc)
        mk.wait("q", vc)
        mk.dma("q", auga[32:33, :], nnhi[:, :], sem=mk.qa)
        mk.dma("q", auga[33:34, :], nnlo[:, :], sem=mk.qa)
        mk.dma("q", auga[34:35, :], ones_row[:, :], sem=mk.qa)
        mk.dma("q", auga[35:36, :], ones_row[:, :], sem=mk.qa)
        mk.dma("q", augb[32:33, :], ones_row[:, :], sem=mk.qa)
        mk.dma("q", augb[33:34, :], ones_row[:, :], sem=mk.qa)
        mk.dma("q", augb[34:35, :], nnhi[:, :], sem=mk.qa)
        aug_ld = mk.dma("q", augb[35:36, :], nnlo[:, :], sem=mk.qa)
        mk.tap("dbg_yb", ybf[:, :])
        mk.tap("dbg_n", nrow[:, :])
        mk.wait("t", mk.qa, aug_ld)
        ppos = psA[:, 3584:3584 + NE]
        mk.wait("v", vc)
        mk.ins("v", lambda E: E.tensor_copy(mask_bf[:, :], mask_sb[:, :]), inc=vc)
        mbf_done = vc.n
        pos_ev = 0
        for I in range(NT):
            mk.wait("t", vc, mbf_done)
            mk.wait("t", sc, pos_ev)
            for Jj in range(I + 1):
                lh = ones_b if Jj < I else tril_b
                mk.ins("t", lambda E, o=ppos[:, :], l=lh,
                       r=mask_bf[:, Jj * NE:(Jj + 1) * NE],
                       st=(Jj == 0), sp=(Jj == I):
                       E.matmul(o, l, r, start=st, stop=sp),
                       inc=(tc if Jj == I else None))
            mk.wait("s", tc)
            mk.ins("s", lambda E, o=pos_sb[:, I * NE:(I + 1) * NE], i=ppos[:, :]:
                   E.copy(o, i), inc=sc)
            pos_ev = sc.n
        mk.wait("v", sc, pos_ev)
        for I in range(NT):
            mk.ins("v", lambda E, o=tmp8a[:, :], a=mask_sb[:, I * NE:(I + 1) * NE],
                   b_=pos_sb[:, I * NE:(I + 1) * NE]:
                   E.tensor_tensor(o, a, b_, ALU.mult), inc=vc)
            mk.wait("v", vc)
            mk.ins("v", lambda E, o=ridx_f[:, 0:1], i=tmp8a[:, :]:
                   E.reduce_sum(o, i, axis=AX.X), inc=vc)
            mk.ins("v", lambda E, o=tmp8b[:, :], a=mask_sb[:, I * NE:(I + 1) * NE],
                   b_=cvec[:, 2:10]:
                   E.tensor_tensor(o, a, b_, ALU.mult), inc=vc)
            mk.wait("v", vc)
            mk.ins("v", lambda E, o=ridx_f[:, 1:2], i=tmp8b[:, :]:
                   E.reduce_sum(o, i, axis=AX.X), inc=vc)
            mk.wait("v", vc)
            mk.ins("v", lambda E, o=ridx_f[:, 0:1]: E.tensor_scalar(
                out=o, in0=o, scalar1=-1.0, scalar2=None, op0=ALU.add), inc=vc)
            mk.wait("v", vc)
            mk.ins("v", lambda E, o=ridx_f[:, 2:3], i=ridx_f[:, 1:2],
                   s2=ridx_f[:, 0:1]: E.scalar_tensor_tensor(
                   out=o, in0=i, scalar=float(CAP), in1=s2,
                   op0=ALU.mult, op1=ALU.add), inc=vc)
            mk.wait("v", vc)
            mk.ins("v", lambda E, o=ridx_f[:, 3:4], i=ridx_f[:, 0:1]:
                   E.tensor_scalar(out=o, in0=i, scalar1=float(CAP), scalar2=None,
                                   op0=ALU.is_ge), inc=vc)
            mk.wait("v", vc)
            mk.ins("v", lambda E, o=ridx_f[:, 2:3], i=ridx_f[:, 3:4],
                   s2=ridx_f[:, 2:3]: E.scalar_tensor_tensor(
                   out=o, in0=i, scalar=1.0e6, in1=s2,
                   op0=ALU.mult, op1=ALU.add), inc=vc)
            mk.wait("v", vc)
            mk.ins("v", lambda E, o=sidx_sb[:, I:I + 1], i=ridx_f[:, 2:3]:
                   E.tensor_copy(o, i), inc=vc)
            mk.wait("v", vc)
        mk.fence()

        mk.phase(3)
        mk.wait("v", vc)
        mk.ins("v", lambda E: E.memset(zacc[:, :], 0.0), inc=vc)
        mk.wait("v", vc)
        pq_free = [0, 0, 0, 0]
        ps_free = [0, 0, 0, 0]
        qcol_free = 0
        sstg_free = [0, 0]
        zprev = 0
        for gi in range(4):
            mk.wait("s", tc, qcol_free)
            mk.wait("s", vc, zprev)
            for J in range(NT):
                bank = J % 4
                mk.wait("t", sc, pq_free[bank])
                mk.ins("t", lambda E, o=pq[:, bank * 512:(bank + 1) * 512],
                       l=auga[:, J * 128:(J + 1) * 128],
                       r=augb[:, gi * 512:(gi + 1) * 512]:
                       E.matmul(o, l, r, start=True, stop=True), inc=tc)
                mk.wait("s", tc)
                mk.ins("s", lambda E, o=qcol[:, J * 512:(J + 1) * 512],
                       i=pq[:, bank * 512:(bank + 1) * 512],
                       a=zscr[:, J:J + 1]: E.activation(
                       out=o, in_=i, func=AF.Exp, accum_out=a), inc=sc)
                pq_free[bank] = sc.n
            exp_done = sc.n
            mk.wait("v", sc, exp_done)
            mk.ins("v", lambda E: E.tensor_tensor(zacc[:, :], zacc[:, :],
                   zscr[:, :], ALU.add), inc=vc)
            zprev = vc.n
            mk.wait("t", sc, exp_done)
            for il in range(4):
                I = gi * 4 + il
                bk0, bk1 = 2 * (il % 2), 2 * (il % 2) + 1
                for dh, bank in ((0, bk0), (1, bk1)):
                    mk.wait("t", vc, ps_free[bank])
                    for J in range(NT):
                        mk.ins("t", lambda E,
                               o=ps[:, bank * 512:(bank + 1) * 512],
                               l=qcol[:, J * 512 + il * 128: J * 512 + (il + 1) * 128],
                               r=hi_tm[:, J * D + dh * 512: J * D + dh * 512 + 512],
                               st=(J == 0), sp=(J == NT - 1):
                               E.matmul(o, l, r, start=st, stop=sp),
                               inc=(tc if J == NT - 1 else None))
                smm_done = tc.n
                sp_ = il % 2
                mk.wait("v", tc, smm_done)
                mk.wait("v", mk.qs[sp_], sstg_free[sp_])
                mk.ins("v", lambda E, o=sstg[:, sp_ * D: sp_ * D + 512],
                       i=ps[:, bk0 * 512:(bk0 + 1) * 512]:
                       E.tensor_copy(o, i), inc=vc)
                mk.wait("v", vc)
                mk.ins("v", lambda E, o=sstg[:, sp_ * D + 512: (sp_ + 1) * D],
                       i=ps[:, bk1 * 512:(bk1 + 1) * 512]:
                       E.tensor_copy(o, i), inc=vc)
                ps_free[bk0] = vc.n
                ps_free[bk1] = vc.n
                mk.wait("q", vc)
                sstg_free[sp_] = mk.dma(
                    "q", scr_s[I * 128:(I + 1) * 128, :],
                    sstg[:, sp_ * D:(sp_ + 1) * D], sem=mk.qs[sp_])
            qcol_free = tc.n
        mk.tap("dbg_z", zacc[:, :])
        mk.fence()

        mk.phase(4)
        mk.wait("s", vc)
        mk.ins("s", lambda E: E.activation(out=score_sb[:, :], in_=tv_sb[:, :],
                                           func=AF.Sigmoid), inc=sc)
        sc_score = sc.n
        if b == 0:
            mk.wait("g", mk.qz)
        xsb_free = [0, 0]
        for k in range(NT):
            p = k % 2
            tmp = tm32a[:, p * D:(p + 1) * D]
            mk.wait("v", sc, sc_score)
            mk.ins("v", lambda E, o=tmp, a=hi_tm[:, k * D:(k + 1) * D],
                   b_=lo_tm[:, k * D:(k + 1) * D]:
                   E.tensor_tensor(o, a, b_, ALU.add), inc=vc)
            mk.wait("v", vc)
            mk.wait("v", mk.gsc[p], xsb_free[p])
            mk.ins("v", lambda E, o=xsb[:, p * D:(p + 1) * D], i=tmp,
                   s_=score_sb[:, k:k + 1]: E.tensor_scalar(
                   out=o, in0=i, scalar1=s_, scalar2=None, op0=ALU.mult), inc=vc)
            xs_done = vc.n
            mk.wait("g", vc, xs_done)
            if not mk.mute:
                mk.gsc[p].n += 16
            mk.emit("g", lambda E, i=xsb[:, p * D:(p + 1) * D],
                    ix=sidx_sb[:, k:k + 1], h=mk.gsc[p].h:
                    E.indirect_dma_start(out=scr_xsc[:, :],
                                         out_offset=IOX(ap=ix, axis=0),
                                         in_=i, in_offset=None,
                                         bounds_check=breg,
                                         oob_is_err=False).then_inc(h, 16))
            xsb_free[p] = mk.gsc[p].n
        mk.fence()

        mk.phase(5)
        actt_sh = vb(98304, ND * T)
        xsT_e3 = [vb(98304 + p * 6144, ND * CAP).rearrange("p_ (c s) -> p_ c s", c=ND)
                  for p in range(2)]
        acte = vb(110592, NT * CAP)
        wg_free = [0, 0]
        wgwb_free = [0, 0]
        wd_free = 0
        wdwb_free = 0
        dn_free = [0, 0, 0, 0]
        gbank_sfree = [0] * 4
        ubank_vfree = [0] * 4
        silu_free = [0, 0, 0, 0]
        spi = 0

        def load_wgu(br_i, i0, pw):
            for c in range(ND):
                mk.dma("q", wgb[:, (pw * ND + c) * 512:(pw * ND + c + 1) * 512],
                       scr_wg[br_i, c * 128:(c + 1) * 128, i0:i0 + 512], sem=mk.qgw[pw])
                mk.dma("s", wub[:, (pw * ND + c) * 512:(pw * ND + c + 1) * 512],
                       scr_wu[br_i, c * 128:(c + 1) * 128, i0:i0 + 512], sem=mk.qgw[pw])
            return (pw, mk.qgw[pw].n)

        def load_wd(br_i, ic_list, dst_off):
            for j, ic in enumerate(ic_list):
                mk.dma("s" if j % 2 else "q",
                       wdb[:, dst_off + j * 1024: dst_off + (j + 1) * 1024],
                       scr_wdn[br_i, ic * 128:(ic + 1) * 128, :], sem=mk.qwd)
            return mk.qwd.n

        sh_i = 8
        do_shared = branch_filter is None or "sh" in branch_filter
        if do_shared:
            if b == 0:
                mk.wait("q", mk.cv_sh)
                mk.wait("s", mk.cv_sh)
            for half in range(2):
                wg_loaded = [None, None]
                for sp_i in range(2):
                    pw = spi % 2
                    mk.wait("q", tc, wg_free[pw])
                    mk.wait("s", tc, wg_free[pw])
                    i0 = half * 1024 + sp_i * 512
                    wg_loaded[sp_i] = load_wgu(sh_i, i0, pw)
                    spi += 1
                mk.wait("q", tc, wd_free)
                mk.wait("s", tc, wd_free)
                ics = list(range(half * 8, half * 8 + 8))
                wd_ld = load_wd(sh_i, ics, 0)
                for sp_i in range(2):
                    pw, _wgl = wg_loaded[sp_i]
                    mk.wait("t", mk.qgw[pw], _wgl)
                    for il in range(4):
                        ic = sp_i * 4 + il
                        for t4 in range(4):
                            mk.wait("t", sc, gbank_sfree[t4])
                            mk.wait("t", vc, dn_free[t4])
                        for c in range(ND):
                            for t4 in range(4):
                                mk.ins("t", lambda E,
                                       o=pgu[:, t4 * 512:(t4 + 1) * 512],
                                       l=wgb[:, (pw * ND + c) * 512 + il * 128:
                                              (pw * ND + c) * 512 + (il + 1) * 128],
                                       r=dmaj[:, c * T + t4 * 512: c * T + (t4 + 1) * 512],
                                       st=(c == 0), sp=(c == ND - 1):
                                       E.matmul(o, l, r, start=st, stop=sp),
                                       inc=(tc if (c == ND - 1 and t4 == 3) else None))
                        gmm_done = tc.n
                        for t4 in range(4):
                            mk.wait("t", vc, ubank_vfree[t4])
                        for c in range(ND):
                            for t4 in range(4):
                                mk.ins("t", lambda E,
                                       o=pgu[:, (4 + t4) * 512:(5 + t4) * 512],
                                       l=wub[:, (pw * ND + c) * 512 + il * 128:
                                              (pw * ND + c) * 512 + (il + 1) * 128],
                                       r=dmaj[:, c * T + t4 * 512: c * T + (t4 + 1) * 512],
                                       st=(c == 0), sp=(c == ND - 1):
                                       E.matmul(o, l, r, start=st, stop=sp),
                                       inc=(tc if (c == ND - 1 and t4 == 3) else None))
                        umm_done = tc.n
                        for t4 in range(4):
                            mk.wait("s", tc, gmm_done)
                            mk.wait("s", vc, silu_free[t4])
                            mk.ins("s", lambda E, o=silu_g[t4][:, :],
                                   i=pgu[:, t4 * 512:(t4 + 1) * 512]:
                                   E.activation(out=o, in_=i, func=AF.Silu), inc=sc)
                            gbank_sfree[t4] = sc.n
                            mk.wait("v", sc)
                            mk.wait("v", tc, umm_done)
                            mk.ins("v", lambda E,
                                   o=actt_sh[:, ic * T + t4 * 512: ic * T + (t4 + 1) * 512],
                                   i=pgu[:, (4 + t4) * 512:(5 + t4) * 512],
                                   s_=silu_g[t4][:, :]:
                                   E.tensor_tensor(o, i, s_, ALU.mult), inc=vc)
                            ubank_vfree[t4] = vc.n
                            silu_free[t4] = vc.n
                    wg_free[pw] = tc.n
                act_done = vc.n
                mk.wait("t", mk.qwd, wd_ld)
                mk.wait("t", vc, act_done)
                for tch in range(NT):
                    for dh in range(2):
                        bank = 2 * (tch % 2) + dh
                        mk.wait("t", vc, dn_free[bank])
                        mk.wait("t", sc, gbank_sfree[bank])
                        for j in range(8):
                            mk.ins("t", lambda E,
                                   o=pgu[:, bank * 512:(bank + 1) * 512],
                                   l=actt_sh[:, j * T + tch * 128:
                                             j * T + (tch + 1) * 128],
                                   r=wdb[:, j * 1024 + dh * 512: j * 1024 + dh * 512 + 512],
                                   st=(j == 0), sp=(j == 7):
                                   E.matmul(o, l, r, start=st, stop=sp),
                                   inc=(tc if j == 7 else None))
                        dn_done = tc.n
                        a_sl = acc[:, tch * D + dh * 512: tch * D + dh * 512 + 512]
                        mk.wait("v", tc, dn_done)
                        if half == 0:
                            mk.ins("v", lambda E, o=a_sl,
                                   i=pgu[:, bank * 512:(bank + 1) * 512]:
                                   E.tensor_copy(o, i), inc=vc)
                        else:
                            mk.ins("v", lambda E, o=a_sl,
                                   i=pgu[:, bank * 512:(bank + 1) * 512]:
                                   E.tensor_tensor(o, o, i, ALU.add), inc=vc)
                        dn_free[bank] = vc.n
                wd_free = tc.n
        else:
            mk.wait("v", vc)
            mk.ins("v", lambda E: E.memset(acc[:, :], 0.0), inc=vc)
            mk.wait("v", vc)

        xstg3 = vb(0, 3 * 1024)
        ystg2 = vf(8192, 512)
        NSC = CAP // 128
        exp_list = [e for e in range(NE)
                    if branch_filter is None or e in branch_filter]
        mk.wait("q", mk.gsc[0])
        mk.wait("q", mk.gsc[1])
        dmaj_free = tc.n
        xstg_free = [dmaj_free] * 3
        ptr_free = [0, 0]
        xsT_free = [0, 0]
        xsT_done = [0, 0]
        acte_free = 0
        ystg_free = 0
        dnb_free = [0] * 6
        ptr_e = psAb[:, 0:2048]

        def issue_xstg_loads(e):
            lds = []
            for sc_ in range(NSC):
                mk.wait("q", tc, xstg_free[sc_])
                lds.append(mk.dma(
                    "q", xstg3[:, sc_ * D: sc_ * D + D],
                    scr_xsc[e * CAP + sc_ * 128: e * CAP + (sc_ + 1) * 128, :],
                    sem=mk.qxs[sc_]))
            return lds

        def prep_xsT(pbuf, lds):
            for sc_ in range(NSC):
                pxp = sc_ % 2
                mk.wait("t", mk.qxs[sc_], lds[sc_])
                mk.wait("t", sc, ptr_free[pxp])
                mk.wait("t", sc, gbank_sfree[pxp])
                mk.wait("t", vc, ubank_vfree[pxp])
                mk.wait("t", vc, dn_free[pxp])
                mk.wait("t", tc, xsT_free[pbuf])
                for c in range(ND):
                    mk.ins("t", lambda E,
                           o=ptr_e[:, pxp * 1024 + c * 128: pxp * 1024 + (c + 1) * 128],
                           i=xstg3[:, sc_ * D + c * 128: sc_ * D + (c + 1) * 128],
                           d=ident[:, :]: E.transpose(o, i, d),
                           inc=(tc if c == ND - 1 else None))
                xstg_free[sc_] = tc.n
                mk.wait("s", tc)
                mk.wait("s", tc, xsT_free[pbuf])
                mk.ins("s", lambda E,
                       o=xsT_e3[pbuf][:, :, sc_ * 128:(sc_ + 1) * 128],
                       i=ptr_e.rearrange("p (q c s) -> p q c s", q=2, c=ND)[:, pxp]:
                       E.copy(o, i), inc=sc)
                ptr_free[pxp] = sc.n
            xsT_done[pbuf] = sc.n

        pend_w = None
        for idx, e in enumerate(exp_list):
            pbuf = idx % 2
            if idx == 0:
                lds0 = issue_xstg_loads(e)
                prep_xsT(pbuf, lds0)
            wg_loaded = [None, None, None, None]

            def load_span(sp_i, ee=None):
                nonlocal spi
                ee = e if ee is None else ee
                pw = spi % 2
                mk.wait("q", tc, wg_free[pw])
                mk.wait("s", tc, wg_free[pw])
                wg_loaded[sp_i] = load_wgu(ee, sp_i * 512, pw)
                spi += 1

            if pend_w is None:
                if b == 0:
                    mk.wait("q", mk.cv_ex[e])
                    mk.wait("s", mk.cv_ex[e])
                load_span(0)
                load_span(1)
            else:
                wg_loaded[0], wg_loaded[1] = pend_w
            mk.wait("q", tc, wd_free)
            mk.wait("s", tc, wd_free)
            wd_ld = load_wd(e, list(range(16)), 0)
            for sp_i in range(4):
                pw, _wgl = wg_loaded[sp_i]
                mk.wait("t", mk.qgw[pw], _wgl)
                mk.wait("t", sc, xsT_done[pbuf])
                for il in range(4):
                    ic = sp_i * 4 + il
                    gb = 2 * (il % 2)
                    ub = 2 * (il % 2) + 1
                    mk.wait("t", sc, gbank_sfree[gb])
                    mk.wait("t", vc, ubank_vfree[ub])
                    mk.wait("t", vc, dn_free[gb])
                    mk.wait("t", vc, dn_free[ub])
                    if gb >= 2:
                        mk.wait("t", sc, dnb_free[gb - 2])
                    if ub >= 2:
                        mk.wait("t", sc, dnb_free[ub - 2])
                    for c in range(ND):
                        mk.ins("t", lambda E,
                               o=pgu[:, gb * 512: gb * 512 + CAP],
                               l=wgb[:, (pw * ND + c) * 512 + il * 128:
                                      (pw * ND + c) * 512 + (il + 1) * 128],
                               r=xsT_e3[pbuf].rearrange("p c s -> p (c s)")[:, c * CAP:(c + 1) * CAP],
                               st=(c == 0), sp=(c == ND - 1):
                               E.matmul(o, l, r, start=st, stop=sp),
                               inc=(tc if c == ND - 1 else None))
                    gmm_done = tc.n
                    for c in range(ND):
                        mk.ins("t", lambda E,
                               o=pgu[:, ub * 512: ub * 512 + CAP],
                               l=wub[:, (pw * ND + c) * 512 + il * 128:
                                      (pw * ND + c) * 512 + (il + 1) * 128],
                               r=xsT_e3[pbuf].rearrange("p c s -> p (c s)")[:, c * CAP:(c + 1) * CAP],
                               st=(c == 0), sp=(c == ND - 1):
                               E.matmul(o, l, r, start=st, stop=sp),
                               inc=(tc if c == ND - 1 else None))
                    umm_done = tc.n
                    sgi = il % 2
                    mk.wait("s", tc, gmm_done)
                    mk.wait("s", vc, silu_free[sgi])
                    mk.ins("s", lambda E, o=silu_g[sgi][:, 0:CAP],
                           i=pgu[:, gb * 512: gb * 512 + CAP]:
                           E.activation(out=o, in_=i, func=AF.Silu), inc=sc)
                    gbank_sfree[gb] = sc.n
                    mk.wait("v", sc)
                    mk.wait("v", tc, umm_done)
                    mk.wait("v", tc, acte_free)
                    mk.ins("v", lambda E,
                           o=acte[:, ic * CAP:(ic + 1) * CAP],
                           i=pgu[:, ub * 512: ub * 512 + CAP],
                           s_=silu_g[sgi][:, 0:CAP]:
                           E.tensor_tensor(o, i, s_, ALU.mult), inc=vc)
                    ubank_vfree[ub] = vc.n
                    silu_free[sgi] = vc.n
                wg_free[pw] = tc.n
                if sp_i + 2 < 4:
                    load_span(sp_i + 2)
            act_done = vc.n
            xsT_free[pbuf] = tc.n
            if idx + 1 < len(exp_list):
                e_next = exp_list[idx + 1]
                if b == 0:
                    mk.wait("q", mk.cv_ex[e_next])
                    mk.wait("s", mk.cv_ex[e_next])
                load_span(0, ee=e_next)
                load_span(1, ee=e_next)
                pend_w = (wg_loaded[0], wg_loaded[1])
                lds_n = issue_xstg_loads(e_next)
            mk.wait("t", mk.qwd, wd_ld)
            mk.wait("t", vc, act_done)
            for bnk in range(6):
                mk.wait("t", sc, dnb_free[bnk])
            for ic in range(16):
                for sc_ in range(NSC):
                    for dh in range(2):
                        bank = 2 + sc_ * 2 + dh
                        mk.ins("t", lambda E,
                               o=pgu[:, bank * 512:(bank + 1) * 512],
                               l=acte[:, ic * CAP + sc_ * 128: ic * CAP + (sc_ + 1) * 128],
                               r=wdb[:, ic * 1024 + dh * 512: ic * 1024 + dh * 512 + 512],
                               st=(ic == 0), sp=(ic == 15):
                               E.matmul(o, l, r, start=st, stop=sp),
                               inc=(tc if (ic == 15 and sc_ == NSC - 1 and dh == 1) else None))
            dnm_done = tc.n
            acte_free = tc.n
            wd_free = tc.n
            if idx + 1 < len(exp_list):
                prep_xsT(1 - pbuf, lds_n)
            for sc_ in range(NSC):
                for dh in range(2):
                    bank = 2 + sc_ * 2 + dh
                    mk.wait("s", tc, dnm_done)
                    mk.wait("s", mk.qy, ystg_free)
                    mk.ins("s", lambda E, o=ystg2[:, :],
                           i=pgu[:, bank * 512:(bank + 1) * 512]:
                           E.copy(o, i), inc=sc)
                    dnb_free[sc_ * 2 + dh] = sc.n
                    mk.wait("q", sc)
                    ystg_free = mk.dma(
                        "q",
                        scr_y[e * CAP + sc_ * 128: e * CAP + (sc_ + 1) * 128,
                              dh * 512:(dh + 1) * 512],
                        ystg2[:, :], sem=mk.qy)
        mk.fence()

        mk.phase(6)
        grt = vf(131072, 2 * D)
        mk.wait("v", vc)
        mk.ins("v", lambda E: E.reciprocal(zrec[:, :], zacc[:, :]), inc=vc)
        zr_done = vc.n
        mk.wait("s", vc, zr_done)
        mk.wait("g", mk.qy)
        cxt_free = [0, 0]
        csn_free = [0, 0]
        cgt_free = [0, 0]
        grt_free = [0, 0]
        cot_free = [0, 0]
        for k in range(NT):
            p = k % 2
            mk.wait("q", vc, cxt_free[p])
            mk.dma("q", cxt[:, p * D:(p + 1) * D], x[b, k * 128:(k + 1) * 128, :],
                   sem=mk.qc[p])
            mk.wait("q", vc, csn_free[p])
            ld_done = mk.dma("q", csn[:, p * D:(p + 1) * D],
                             scr_s[k * 128:(k + 1) * 128, :], sem=mk.qc[p])
            mk.wait("v", sc, grt_free[p])
            mk.ins("v", lambda E, o=grt[:, p * D:(p + 1) * D]: E.memset(o, 0.0), inc=vc)
            gz = vc.n
            mk.wait("g", vc, gz)
            if not mk.mute:
                mk.ggr[p].n += 16
            mk.emit("g", lambda E, o=grt[:, p * D:(p + 1) * D],
                    ix=sidx_sb[:, k:k + 1], h=mk.ggr[p].h:
                    E.indirect_dma_start(out=o, out_offset=None,
                                         in_=scr_y[:, :],
                                         in_offset=IOX(ap=ix, axis=0),
                                         bounds_check=breg,
                                         oob_is_err=False).then_inc(h, 16))
            g_ld = mk.ggr[p].n
            mk.wait("v", mk.ggr[p], g_ld)
            mk.ins("v", lambda E, o=grt[:, p * D:(p + 1) * D],
                   a=grt[:, p * D:(p + 1) * D], b_=acc[:, k * D:(k + 1) * D]:
                   E.tensor_tensor(o, a, b_, ALU.add), inc=vc)
            ffn_done = vc.n
            mk.wait("s", vc, ffn_done)
            mk.wait("s", vc, cgt_free[p])
            mk.ins("s", lambda E, o=cgt[:, p * D:(p + 1) * D],
                   i=grt[:, p * D:(p + 1) * D]:
                   E.activation(out=o, in_=i, func=AF.Silu), inc=sc)
            grt_free[p] = sc.n
            mk.wait("v", sc)
            mk.wait("v", mk.qc[p], ld_done)
            mk.wait("v", vc, zr_done)
            mk.wait("v", mk.qo[p], cot_free[p])
            mk.ins("v", lambda E, o=cot[:, p * D:(p + 1) * D],
                   i=csn[:, p * D:(p + 1) * D], z=zrec[:, k:k + 1],
                   g_=cgt[:, p * D:(p + 1) * D]: E.scalar_tensor_tensor(
                   out=o, in0=i, scalar=z, in1=g_, op0=ALU.mult, op1=ALU.mult), inc=vc)
            csn_free[p] = vc.n
            cgt_free[p] = vc.n
            mk.wait("v", vc)
            mk.ins("v", lambda E, o=cot[:, p * D:(p + 1) * D],
                   a=cot[:, p * D:(p + 1) * D], b_=cxt[:, p * D:(p + 1) * D]:
                   E.tensor_tensor(o, a, b_, ALU.add), inc=vc)
            cxt_free[p] = vc.n
            mk.wait("q", vc)
            cot_free[p] = mk.dma("q", out[b, k * 128:(k + 1) * 128, :],
                                 cot[:, p * D:(p + 1) * D], sem=mk.qo[p])
        mk.fence()

    for s_ in mk.final_sems:
        mk.wait("q", s_)
    mk.replay()
    es.close()
    return nc


_IDENT = np.concatenate([
    np.eye(128, dtype=ml_dtypes.bfloat16),
    np.triu(np.ones((128, 128), dtype=ml_dtypes.bfloat16)),
    np.ones((128, 128), dtype=ml_dtypes.bfloat16),
], axis=1)
_CVEC = np.zeros((128, 16), np.float32)
_CVEC[:, 0] = EPS
_CVEC[:, 1] = 1.0
_CVEC[:, 2:10] = np.arange(8, dtype=np.float32)[None, :]


def kernel(L0_attn_x, L0_attn_U, L0_ffn_gate_w, L0_ffn_sh_gate, L0_ffn_sh_up,
           L0_ffn_sh_down, L0_ffn_rt_gate, L0_ffn_rt_up, L0_ffn_rt_down):
    xf = np.ascontiguousarray(np.asarray(L0_attn_x, dtype=np.float32))
    wts = {
        "w_u": np.ascontiguousarray(np.asarray(L0_attn_U, np.float32)),
        "w_gate": np.ascontiguousarray(np.asarray(L0_ffn_gate_w, np.float32)),
        "w_shg": np.ascontiguousarray(np.asarray(L0_ffn_sh_gate, np.float32)),
        "w_shu": np.ascontiguousarray(np.asarray(L0_ffn_sh_up, np.float32)),
        "w_shd": np.ascontiguousarray(np.asarray(L0_ffn_sh_down, np.float32)),
        "w_rtg": np.ascontiguousarray(np.asarray(L0_ffn_rt_gate, np.float32)),
        "w_rtu": np.ascontiguousarray(np.asarray(L0_ffn_rt_up, np.float32)),
        "w_rtd": np.ascontiguousarray(np.asarray(L0_ffn_rt_down, np.float32)),
        "ident": _IDENT,
        "cvec": _CVEC,
    }
    nc = build_program()
    in_maps = []
    for c in range(NCORES):
        m = dict(wts)
        m["x"] = np.ascontiguousarray(xf[c * BPC:(c + 1) * BPC])
        in_maps.append(m)
    res = run_bass_kernel_spmd(nc, in_maps, core_ids=list(range(NCORES)))
    return np.concatenate([r["out"] for r in res.results], axis=0)


# version 15
# speedup vs baseline: 1.1775x; 1.0637x over previous
import numpy as np
import ml_dtypes
from contextlib import ExitStack

import concourse.bass as bass
import concourse.mybir as mybir
from concourse.bass_utils import run_bass_kernel_spmd
from concourse.bass import IndirectOffsetOnAxis as IOX

F32 = mybir.dt.float32
BF16 = mybir.dt.bfloat16
AF = mybir.ActivationFunctionType
ALU = mybir.AluOpType
AX = mybir.AxisListType

NCORES = 8
B_ALL, T, D = 16, 2048, 1024
BPC = B_ALL // NCORES
NT = T // 128
ND = D // 128
IFF = 2048
NE = 8
R = 32
CAP = 384
EPS = 1e-5

ENGS = ("t", "v", "s", "q", "g")


class Sem:
    __slots__ = ("h", "n")

    def __init__(self, h):
        self.h = h
        self.n = 0


class MK:

    def __init__(self, nc, es, dbg=None, max_ph=99):
        self.nc = nc
        self.es = es
        self.code = {e: [] for e in ENGS}
        self.dbg = dbg or {}
        self.max_ph = max_ph
        self.mute = False
        self.tc = self.sem("tc")
        self.vc = self.sem("vc")
        self.sc = self.sem("sc")
        self.qd = self.sem("qd")
        self.gd = self.sem("gd")
        self.fs = self.sem("fs")
        self.qx = [self.sem("qx0"), self.sem("qx1")]
        self.qs = [self.sem("qs0"), self.sem("qs1")]
        self.qc = [self.sem("qc0"), self.sem("qc1")]
        self.qo = [self.sem("qo0"), self.sem("qo1")]
        self.qa = self.sem("qa")
        self.gw = [self.sem("gw0"), self.sem("gw1")]
        self.gwd = self.sem("gwd")
        self.gc = self.sem("gc")
        self.qy = self.sem("qy")
        self.gsc = [self.sem("gsc0"), self.sem("gsc1")]
        self.ggr = [self.sem("ggr0"), self.sem("ggr1")]
        self.qz = self.sem("qz")
        self.qwb = [self.sem("qwb0"), self.sem("qwb1")]
        self.qwbd = self.sem("qwbd")
        self.qxs = [self.sem("qxs0"), self.sem("qxs1"), self.sem("qxs2")]
        self.qgw = [self.sem("qgw0"), self.sem("qgw1")]
        self.qwd = self.sem("qwd")
        self.cv_sh = self.sem("cv_sh")
        self.cv_ex = [self.sem(f"cv_ex{i}") for i in range(NE)]
        self.dma_sems = [self.qd, self.gd, self.qa, self.gwd, self.qy,
                         self.qz, self.qwbd, self.qwd] + \
            self.qwb + self.qxs + \
            self.qx + self.qs + self.qc + self.qo + self.gw + self.gsc + \
            self.ggr + self.qgw
        self.final_sems = self.dma_sems + [self.cv_sh] + self.cv_ex

    def sem(self, name):
        return Sem(self.es.enter_context(self.nc.semaphore(name)))

    def phase(self, n):
        self.mute = n > self.max_ph

    def emit(self, eng, fn):
        if self.mute:
            return
        self.code[eng].append(fn)

    def wait(self, eng, sem, val=None):
        v = sem.n if val is None else val
        if v <= 0:
            return
        self.emit(eng, lambda E, h=sem.h, v=v: E.wait_ge(h, v))

    def ins(self, eng, build, inc=None, k=1):
        if self.mute:
            return
        if inc is None:
            self.emit(eng, lambda E, b=build: b(E))
        else:
            inc.n += k
            self.emit(eng, lambda E, b=build, h=inc.h, k=k: b(E).then_inc(h, k))

    def dma(self, eng_q, out, in_, sem=None, **kw):
        if sem is None:
            sem = self.qd if eng_q == "q" else self.gd
        if self.mute:
            return sem.n
        sem.n += 16
        self.emit(eng_q, lambda E, o=out, i=in_, h=sem.h, kw=kw:
                  E.dma_start(out=o, in_=i, **kw).then_inc(h, 16))
        return sem.n

    def fence(self):
        if self.mute:
            return
        tgts = [(s, s.n) for s in [self.tc, self.vc, self.sc, self.gc] + self.dma_sems]
        self.fs.n += len(ENGS)
        tgt = self.fs.n
        for e in ENGS:
            for s, v in tgts:
                self.wait(e, s, v)
            self.emit(e, lambda E, h=self.fs.h: E.sem_inc(h, 1))
            self.emit(e, lambda E, h=self.fs.h, v=tgt: E.wait_ge(h, v))

    def tap(self, name, sb_ap):
        if name not in self.dbg:
            return
        for s in (self.tc, self.vc, self.sc):
            self.wait("g", s)
        self.dma("g", self.dbg[name][:, :], sb_ap)
        self.fence()

    def replay(self):
        nc = self.nc
        with nc.Block() as block:
            @block.tensor
            def _(E):
                for fn in self.code["t"]:
                    fn(E)

            @block.vector
            def _(E):
                for fn in self.code["v"]:
                    fn(E)

            @block.scalar
            def _(E):
                for fn in self.code["s"]:
                    fn(E)

            @block.sync
            def _(E):
                for fn in self.code["q"]:
                    fn(E)

            @block.gpsimd
            def _(E):
                for fn in self.code["g"]:
                    fn(E)


def build_program(dbg_names=(), max_ph=99, branch_filter=None, detect_races=True, nbat=BPC):
    nc = bass.Bass("TRN2", target_bir_lowering=False, detect_race_conditions=detect_races)
    es = ExitStack()
    U8 = mybir.dt.uint8

    x = nc.dram_tensor("x", [BPC, T, D], F32, kind="ExternalInput").ap()
    w_u = nc.dram_tensor("w_u", [D, R], F32, kind="ExternalInput").ap()
    w_gate = nc.dram_tensor("w_gate", [D, NE], F32, kind="ExternalInput").ap()
    w_shg = nc.dram_tensor("w_shg", [D, IFF], F32, kind="ExternalInput").ap()
    w_shu = nc.dram_tensor("w_shu", [D, IFF], F32, kind="ExternalInput").ap()
    w_shd = nc.dram_tensor("w_shd", [IFF, D], F32, kind="ExternalInput").ap()
    w_rtg = nc.dram_tensor("w_rtg", [NE, D, IFF], F32, kind="ExternalInput").ap()
    w_rtu = nc.dram_tensor("w_rtu", [NE, D, IFF], F32, kind="ExternalInput").ap()
    w_rtd = nc.dram_tensor("w_rtd", [NE, IFF, D], F32, kind="ExternalInput").ap()
    ident_in = nc.dram_tensor("ident", [128, 3 * 128], BF16, kind="ExternalInput").ap()
    cvec_in = nc.dram_tensor("cvec", [128, 16], F32, kind="ExternalInput").ap()
    out = nc.dram_tensor("out", [BPC, T, D], F32, kind="ExternalOutput").ap()
    scr_s = nc.dram_tensor("scr_s", [T, D], F32, kind="Internal").ap()
    scr_xsc = nc.dram_tensor("scr_xsc", [NE * CAP, D], BF16, kind="Internal").ap()
    scr_y = nc.dram_tensor("scr_y", [NE * CAP, D], F32, kind="Internal").ap()
    scr_wg = nc.dram_tensor("scr_wg", [9, D, IFF], BF16, kind="Internal").ap()
    scr_wu = nc.dram_tensor("scr_wu", [9, D, IFF], BF16, kind="Internal").ap()
    scr_wdn = nc.dram_tensor("scr_wdn", [9, IFF, D], BF16, kind="Internal").ap()

    dbg = {}
    for nm, shp, dt in [
        ("dbg_xnt", [128, ND * T], F32), ("dbg_logits", [128, NT * NE], F32),
        ("dbg_score", [128, NT], F32), ("dbg_mask", [128, NT * NE], F32),
        ("dbg_hi", [128, NT * D], F32), ("dbg_lo", [128, NT * D], F32),
        ("dbg_yb", [32, T], F32), ("dbg_n", [1, T], F32),
        ("dbg_z", [128, NT], F32), ("dbg_acc", [128, NT * D], F32),
        ("dbg_xst", [128, ND * T], F32),
    ]:
        if nm in dbg_names:
            dbg[nm] = nc.dram_tensor(nm, shp, dt, kind="ExternalOutput").ap()

    mk = MK(nc, es, dbg, max_ph=max_ph)
    tc, vc, sc, qd, gd = mk.tc, mk.vc, mk.sc, mk.qd, mk.gd
    breg = es.enter_context(nc.gpsimd.register("breg"))
    mk.emit("g", lambda E: E.reg_mov(breg, NE * CAP - 1))

    identb3 = es.enter_context(nc.sbuf_tensor("identb", [128, 3 * 128], BF16))
    ident = identb3[:, 0:128]
    tril_b = identb3[:, 128:256]
    ones_b = identb3[:, 256:384]
    cvec = es.enter_context(nc.sbuf_tensor("cvecb", [128, 16], F32))
    u_bf = es.enter_context(nc.sbuf_tensor("u_bf", [128, ND * R], BF16))
    gw_hi = es.enter_context(nc.sbuf_tensor("gw_hi", [128, ND * NE], BF16))
    gw_lo = es.enter_context(nc.sbuf_tensor("gw_lo", [128, ND * NE], BF16))
    gw_f = es.enter_context(nc.sbuf_tensor("gw_f", [128, ND * NE], F32))
    gw_t = es.enter_context(nc.sbuf_tensor("gw_t", [128, ND * NE], F32))
    logits_sb = es.enter_context(nc.sbuf_tensor("logits_sb", [128, NT * NE], F32))
    tv_sb = es.enter_context(nc.sbuf_tensor("tv_sb", [128, NT], F32))
    score_sb = es.enter_context(nc.sbuf_tensor("score_sb", [128, NT], F32))
    mask_sb = es.enter_context(nc.sbuf_tensor("mask_sb", [128, NT * NE], F32))
    zacc = es.enter_context(nc.sbuf_tensor("zacc", [128, NT], F32))
    zscr = es.enter_context(nc.sbuf_tensor("zscr", [128, NT], F32))
    zrec = es.enter_context(nc.sbuf_tensor("zrec", [128, NT], F32))
    silu_g = [es.enter_context(nc.sbuf_tensor(f"silu_g{p}", [128, 512], F32)) for p in range(4)]
    mcol = es.enter_context(nc.sbuf_tensor("mcol", [128, 4], F32))
    mask_bf = es.enter_context(nc.sbuf_tensor("mask_bf", [128, NT * NE], BF16))
    pos_sb = es.enter_context(nc.sbuf_tensor("pos_sb", [128, NT * NE], F32))
    sidx_sb = es.enter_context(nc.sbuf_tensor("sidx_sb", [128, NT], mybir.dt.uint32))
    tmp8a = es.enter_context(nc.sbuf_tensor("tmp8a", [128, NE], F32))
    tmp8b = es.enter_context(nc.sbuf_tensor("tmp8b", [128, NE], F32))
    ridx_f = es.enter_context(nc.sbuf_tensor("ridx_f", [128, 4], F32))
    zero_tile = es.enter_context(nc.sbuf_tensor("zero_tile", [128, D], BF16))
    ssq = es.enter_context(nc.sbuf_tensor("ssq", [128, 4], F32))
    rstd = es.enter_context(nc.sbuf_tensor("rstd", [128, 4], F32))

    AR_BYTES = 196608
    arena = es.enter_context(nc.sbuf_tensor("arena", [128, AR_BYTES], U8))
    arf = arena.bitcast(F32)
    arb = arena.bitcast(BF16)

    def vf(off, n):
        return arf[:, off // 4: off // 4 + n]

    def vb(off, n):
        return arb[:, off // 2: off // 2 + n]

    dmaj = vb(0, ND * T)
    dmaj3 = dmaj.rearrange("p (c t) -> p c t", c=ND)
    hi_tm = vb(32768, NT * D)
    lo_tm = vb(65536, NT * D)
    xst = vf(98304, 2 * D)
    tm32a = vf(106496, 2 * D)
    tm32b = vf(114688, 2 * D)
    loch = vb(122880, 2 * D)
    auga = vb(126976, T)[0:36]
    augb = vb(131072, T)[0:36]
    ybf = vb(135168, T)[0:32]
    ysq = vf(139264, 512)[0:32]
    nrow = vf(141312, T)[0:1]
    nt1 = vf(149504, T)[0:1]
    nnhi = vb(157696, T)[0:1]
    nnlo = vb(161792, T)[0:1]
    ones_row = vb(165888, T)[0:1]
    qcol = vb(169984, NT * 512)
    sstg = vf(186368, 2 * D)
    xsb = vb(169984, 2 * D)
    acc = vf(32768, NT * D)
    actt = vb(98304, ND * T)
    wgb = vb(131072, 2 * ND * 512)
    wub = vb(147456, 2 * ND * 512)
    wdb = vb(163840, 16 * 1024)
    cxt = vf(98304, 2 * D)
    csn = vf(106496, 2 * D)
    cgt = vf(114688, 2 * D)
    cot = vf(122880, 2 * D)

    psA = es.enter_context(nc.psum_tensor("psA", [128, 4096], F32))
    psAb = psA.bitcast(BF16)
    ptrh = psAb[:, 0:2048]
    ptrl = psAb[:, 2048:4096]
    plg = psA[:, 2048:2048 + NE]
    pyp = psA[:, 2560:3072][0:32]
    pnp = psA[:, 3072:3584][0:1]
    pq = psA[:, 0:2048]
    ps = psA[:, 2048:4096]
    ptx = psAb[:, 0:2048]
    pgu = psA[:, 0:4096]

    mk.dma("q", identb3[:, :], ident_in[:, :])
    mk.dma("q", cvec[:, :], cvec_in[:, :])
    mk.dma("q", gw_f.rearrange("p (c e) -> p c e", c=ND),
           w_gate.rearrange("(c p) e -> p c e", p=128))
    for c in range(ND):
        mk.dma("g", gw_hi[:, c * NE:(c + 1) * NE], w_gate[c * 128:(c + 1) * 128, :])
        mk.dma("g", u_bf[:, c * R:(c + 1) * R], w_u[c * 128:(c + 1) * 128, :])
    mk.dma("g", scr_wg[8], w_shg[:, :], sem=mk.cv_sh, max_dma_last_dim=8192)
    mk.dma("g", scr_wu[8], w_shu[:, :], sem=mk.cv_sh, max_dma_last_dim=8192)
    mk.dma("g", scr_wdn[8], w_shd[:, :], sem=mk.cv_sh, max_dma_last_dim=8192)
    for e_ in range(NE):
        mk.dma("g", scr_wg[e_], w_rtg[e_], sem=mk.cv_ex[e_], max_dma_last_dim=8192)
        mk.dma("g", scr_wu[e_], w_rtu[e_], sem=mk.cv_ex[e_], max_dma_last_dim=8192)
        mk.dma("g", scr_wdn[e_], w_rtd[e_], sem=mk.cv_ex[e_], max_dma_last_dim=8192)
    mk.wait("v", gd)
    mk.wait("v", qd)
    mk.ins("v", lambda E: E.tensor_copy(gw_t[:, :], gw_hi[:, :]), inc=vc)
    mk.wait("v", vc)
    mk.ins("v", lambda E: E.tensor_tensor(gw_t[:, :], gw_f[:, :], gw_t[:, :], ALU.subtract), inc=vc)
    mk.wait("v", vc)
    mk.ins("v", lambda E: E.tensor_copy(gw_lo[:, :], gw_t[:, :]), inc=vc)
    mk.ins("v", lambda E: E.memset(zero_tile[:, :], 0.0), inc=vc)
    zf = vc.n
    mk.wait("q", vc, zf)
    for zi in range((NE * CAP) // 128):
        mk.dma("q", scr_xsc[zi * 128:(zi + 1) * 128, :], zero_tile[:, :], sem=mk.qz)

    branches = [("e", e, w_rtg, w_rtu, w_rtd) for e in range(NE)] + \
               [("sh", None, w_shg, w_shu, w_shd)]
    if branch_filter is not None:
        branches = [br for br in branches
                    if (br[0] == "sh" and "sh" in branch_filter) or
                       (br[0] == "e" and br[1] in branch_filter)]

    for b in range(nbat):
        mk.phase(1)
        xst_free = [0, 0]
        xnf_free = [0, 0]
        cent_free = [0, 0]
        ptrh_free = [0, 0]
        ptrl_free = [0, 0]
        loch_free = [0, 0]
        lg_free = 0
        pend_tv = []

        def emit_tv(kk, lgv):
            if lgv < 0:
                mk.wait("v", vc, -lgv)
            else:
                mk.wait("v", sc, lgv)
            mk.ins("v", lambda E, o=tv_sb[:, kk:kk + 1],
                   i=logits_sb[:, kk * NE:(kk + 1) * NE]:
                   E.reduce_max(o, i, axis=AX.X), inc=vc)
            mk.wait("v", vc)
            mk.ins("v", lambda E, o=mask_sb[:, kk * NE:(kk + 1) * NE],
                   i=logits_sb[:, kk * NE:(kk + 1) * NE], s_=tv_sb[:, kk:kk + 1]:
                   E.tensor_scalar(out=o, in0=i, scalar1=s_, scalar2=None,
                                   op0=ALU.is_equal), inc=vc)

        def stage1(k):
            p = k % 2
            xs_ = xst[:, p * D:(p + 1) * D]
            cent = tm32a[:, p * D:(p + 1) * D]
            mk.wait("q", vc, xst_free[p])
            x_ld = mk.dma("q", xs_, x[b, k * 128:(k + 1) * 128, :], sem=mk.qx[p])
            mk.wait("v", mk.qx[p], x_ld)
            mk.ins("v", lambda E, o=mcol[:, p:p + 1], i=xs_:
                   E.reduce_sum(o, i, axis=AX.X), inc=vc)
            mk.wait("v", vc)
            mk.ins("v", lambda E, o=mcol[:, p:p + 1]: E.tensor_scalar(
                out=o, in0=o, scalar1=-1.0 / D, scalar2=None, op0=ALU.mult), inc=vc)
            mk.wait("v", vc)
            mk.wait("v", sc, cent_free[p])
            mk.ins("v", lambda E, o=cent, i=xs_, m=mcol[:, p:p + 1]: E.tensor_scalar(
                out=o, in0=i, scalar1=m, scalar2=None, op0=ALU.add), inc=vc)
            xst_free[p] = vc.n
            return vc.n

        def stage2a(k, cent_done):
            p = k % 2
            cent = tm32a[:, p * D:(p + 1) * D]
            xnf = tm32b[:, p * D:(p + 1) * D]
            hi_sl = hi_tm[:, k * D:(k + 1) * D]
            lo_sl = lo_tm[:, k * D:(k + 1) * D]
            mk.wait("s", vc, cent_done)
            mk.wait("s", vc, xnf_free[p])
            mk.ins("s", lambda E, o=xnf, i=cent, a=ssq[:, p:p + 1]: E.activation(
                out=o, in_=i, func=AF.Square, accum_out=a), inc=sc)
            mk.wait("s", sc)
            mk.ins("s", lambda E, o=rstd[:, p:p + 1], i=ssq[:, p:p + 1]: E.activation(
                out=o, in_=i, func=AF.Ln, scale=1.0 / D, bias=cvec[:, 0:1]), inc=sc)
            mk.wait("s", sc)
            mk.ins("s", lambda E, o=rstd[:, p:p + 1], i=rstd[:, p:p + 1]: E.activation(
                out=o, in_=i, func=AF.Exp, scale=-0.5), inc=sc)
            mk.wait("s", sc)
            mk.ins("s", lambda E, o=xnf, i=cent, r=rstd[:, p:p + 1]: E.activation(
                out=o, in_=i, func=AF.Copy, scale=r), inc=sc)
            cent_free[p] = sc.n
            mk.wait("s", sc)
            mk.ins("s", lambda E, o=hi_sl, i=xnf: E.copy(o, i), inc=sc)
            hi_done = sc.n
            mk.wait("v", sc, hi_done)
            mk.ins("v", lambda E, o=lo_sl, a=hi_sl, b_=xnf:
                   E.scalar_tensor_tensor(out=o, in0=a, scalar=-1.0, in1=b_,
                                          op0=ALU.mult, op1=ALU.add), inc=vc)
            xnf_free[p] = vc.n
            hilo_done = vc.n
            mk.wait("t", sc, hi_done)
            mk.wait("t", vc, hilo_done)
            mk.wait("t", vc, ptrh_free[p])
            for c in range(ND):
                mk.ins("t", lambda E,
                       o=ptrh[:, p * 1024 + c * 128: p * 1024 + (c + 1) * 128],
                       i=hi_sl[:, c * 128:(c + 1) * 128], d=ident[:, :]:
                       E.transpose(o, i, d), inc=(tc if c == ND - 1 else None))
            trh_done = tc.n
            mk.wait("t", vc, ptrl_free[p])
            for c in range(ND):
                mk.ins("t", lambda E,
                       o=ptrl[:, p * 1024 + c * 128: p * 1024 + (c + 1) * 128],
                       i=lo_sl[:, c * 128:(c + 1) * 128], d=ident[:, :]:
                       E.transpose(o, i, d), inc=(tc if c == ND - 1 else None))
            trl_done = tc.n
            return (k, trh_done, trl_done)

        def stage2b(k, trh_done, trl_done):
            p = k % 2
            mk.wait("v", tc, trh_done)
            mk.ins("v", lambda E, o=dmaj3[:, :, k * 128:(k + 1) * 128],
                   i=ptrh.rearrange("p (q c t) -> p q c t", q=2, c=ND)[:, p]:
                   E.tensor_copy(o, i), inc=vc)
            ptrh_free[p] = vc.n
            hi_evac = vc.n
            mk.wait("v", tc, trl_done)
            mk.wait("v", tc, loch_free[p])
            mk.ins("v", lambda E, o=loch[:, p * 1024:(p + 1) * 1024],
                   i=ptrl[:, p * 1024:(p + 1) * 1024]: E.tensor_copy(o, i), inc=vc)
            ptrl_free[p] = vc.n
            lo_evac = vc.n
            mk.wait("t", vc, hi_evac)
            mk.wait("t", vc, lo_evac)
            mk.wait("t", vc, lg_free2[0])
            nmm = 3 * ND
            j = 0
            for pass_i in range(3):
                wsb = gw_hi if pass_i != 1 else gw_lo
                for c in range(ND):
                    if pass_i < 2:
                        lh = dmaj[:, c * T + k * 128: c * T + (k + 1) * 128]
                    else:
                        lh = loch[:, p * 1024 + c * 128: p * 1024 + (c + 1) * 128]
                    mk.ins("t", lambda E, o=plg[:, :], l=lh,
                           r=wsb[:, c * NE:(c + 1) * NE],
                           st=(j == 0), sp=(j == nmm - 1):
                           E.matmul(o, l, r, start=st, stop=sp),
                           inc=(tc if j == nmm - 1 else None))
                    j += 1
            loch_free[p] = tc.n
            mk.wait("v", tc)
            mk.ins("v", lambda E, o=logits_sb[:, k * NE:(k + 1) * NE],
                   i=plg[:, :]: E.tensor_copy(o, i), inc=vc)
            lg_free2[0] = vc.n
            pend_tv.append((k, -vc.n))

        lg_free2 = [0]
        prev = None
        prevb = None
        for k in range(NT):
            cd = stage1(k)
            if len(pend_tv) >= 2:
                emit_tv(*pend_tv.pop(0))
            if prev is not None:
                nb = stage2a(*prev)
                if prevb is not None:
                    stage2b(*prevb)
                prevb = nb
            prev = (k, cd)
        nb = stage2a(*prev)
        stage2b(*prevb)
        stage2b(*nb)
        for pt in pend_tv:
            emit_tv(*pt)
        pend_tv = []
        mk.tap("dbg_hi", hi_tm[:, :])
        mk.tap("dbg_lo", lo_tm[:, :])
        mk.tap("dbg_xnt", dmaj[:, :])
        mk.tap("dbg_logits", logits_sb[:, :])
        mk.tap("dbg_mask", mask_sb[:, :])
        mk.fence()

        mk.phase(2)
        mk.ins("v", lambda E: E.memset(ones_row[:, :], 1.0), inc=vc)
        for js in range(4):
            for c in range(ND):
                mk.ins("t", lambda E, o=pyp[:, :], l=u_bf[:, c * R:(c + 1) * R],
                       r=dmaj[:, c * T + js * 512: c * T + js * 512 + 512],
                       st=(c == 0), sp=(c == ND - 1):
                       E.matmul(o, l, r, start=st, stop=sp),
                       inc=(tc if c == ND - 1 else None))
            mk.wait("s", tc)
            mk.ins("s", lambda E, o=ybf[:, js * 512:(js + 1) * 512],
                   i=pyp[:, :]: E.copy(o, i), inc=sc)
            mk.wait("s", sc)
            mk.ins("s", lambda E, o=ysq[:, :],
                   i=ybf[:, js * 512:(js + 1) * 512]:
                   E.activation(out=o, in_=i, func=AF.Square), inc=sc)
            mk.wait("t", sc)
            mk.ins("t", lambda E, o=pnp[:, :], l=cvec[0:32, 1:2],
                   r=ysq[:, :]:
                   E.matmul(o, l, r, start=True, stop=True), inc=tc)
            mk.wait("s", tc)
            mk.ins("s", lambda E, o=nrow[:, js * 512:(js + 1) * 512],
                   i=pnp[:, :]: E.copy(o, i), inc=sc)
            mk.wait("t", sc)
        mk.wait("v", sc)
        mk.ins("v", lambda E: E.tensor_copy(nnhi[:, :], nrow[:, :]), inc=vc)
        mk.wait("v", vc)
        mk.ins("v", lambda E: E.tensor_copy(nt1[:, :], nnhi[:, :]), inc=vc)
        mk.wait("v", vc)
        mk.ins("v", lambda E: E.tensor_scalar(out=nnhi[:, :], in0=nnhi[:, :],
               scalar1=-1.0, scalar2=None, op0=ALU.mult), inc=vc)
        mk.wait("v", vc)
        mk.ins("v", lambda E: E.tensor_tensor(nt1[:, :], nrow[:, :], nt1[:, :],
                                              ALU.subtract), inc=vc)
        mk.wait("v", vc)
        mk.ins("v", lambda E: E.tensor_scalar(out=nnlo[:, :], in0=nt1[:, :],
               scalar1=-1.0, scalar2=None, op0=ALU.mult), inc=vc)
        mk.wait("v", vc)
        mk.ins("v", lambda E: E.tensor_scalar(out=auga[0:32, :], in0=ybf[:, :],
               scalar1=2.0, scalar2=None, op0=ALU.mult), inc=vc)
        mk.wait("v", vc)
        mk.ins("v", lambda E: E.tensor_copy(augb[0:32, :], ybf[:, :]), inc=vc)
        mk.wait("q", vc)
        mk.dma("q", auga[32:33, :], nnhi[:, :], sem=mk.qa)
        mk.dma("q", auga[33:34, :], nnlo[:, :], sem=mk.qa)
        mk.dma("q", auga[34:35, :], ones_row[:, :], sem=mk.qa)
        mk.dma("q", auga[35:36, :], ones_row[:, :], sem=mk.qa)
        mk.dma("q", augb[32:33, :], ones_row[:, :], sem=mk.qa)
        mk.dma("q", augb[33:34, :], ones_row[:, :], sem=mk.qa)
        mk.dma("q", augb[34:35, :], nnhi[:, :], sem=mk.qa)
        aug_ld = mk.dma("q", augb[35:36, :], nnlo[:, :], sem=mk.qa)
        mk.tap("dbg_yb", ybf[:, :])
        mk.tap("dbg_n", nrow[:, :])
        mk.wait("t", mk.qa, aug_ld)
        ppos = psA[:, 3584:3584 + NE]
        mk.wait("v", vc)
        mk.ins("v", lambda E: E.tensor_copy(mask_bf[:, :], mask_sb[:, :]), inc=vc)
        mbf_done = vc.n
        pos_ev = 0
        for I in range(NT):
            mk.wait("t", vc, mbf_done)
            mk.wait("t", sc, pos_ev)
            for Jj in range(I + 1):
                lh = ones_b if Jj < I else tril_b
                mk.ins("t", lambda E, o=ppos[:, :], l=lh,
                       r=mask_bf[:, Jj * NE:(Jj + 1) * NE],
                       st=(Jj == 0), sp=(Jj == I):
                       E.matmul(o, l, r, start=st, stop=sp),
                       inc=(tc if Jj == I else None))
            mk.wait("s", tc)
            mk.ins("s", lambda E, o=pos_sb[:, I * NE:(I + 1) * NE], i=ppos[:, :]:
                   E.copy(o, i), inc=sc)
            pos_ev = sc.n
        mk.wait("v", sc, pos_ev)
        for I in range(NT):
            mk.ins("v", lambda E, o=tmp8a[:, :], a=mask_sb[:, I * NE:(I + 1) * NE],
                   b_=pos_sb[:, I * NE:(I + 1) * NE]:
                   E.tensor_tensor(o, a, b_, ALU.mult), inc=vc)
            mk.wait("v", vc)
            mk.ins("v", lambda E, o=ridx_f[:, 0:1], i=tmp8a[:, :]:
                   E.reduce_sum(o, i, axis=AX.X), inc=vc)
            mk.ins("v", lambda E, o=tmp8b[:, :], a=mask_sb[:, I * NE:(I + 1) * NE],
                   b_=cvec[:, 2:10]:
                   E.tensor_tensor(o, a, b_, ALU.mult), inc=vc)
            mk.wait("v", vc)
            mk.ins("v", lambda E, o=ridx_f[:, 1:2], i=tmp8b[:, :]:
                   E.reduce_sum(o, i, axis=AX.X), inc=vc)
            mk.wait("v", vc)
            mk.ins("v", lambda E, o=ridx_f[:, 0:1]: E.tensor_scalar(
                out=o, in0=o, scalar1=-1.0, scalar2=None, op0=ALU.add), inc=vc)
            mk.wait("v", vc)
            mk.ins("v", lambda E, o=ridx_f[:, 2:3], i=ridx_f[:, 1:2],
                   s2=ridx_f[:, 0:1]: E.scalar_tensor_tensor(
                   out=o, in0=i, scalar=float(CAP), in1=s2,
                   op0=ALU.mult, op1=ALU.add), inc=vc)
            mk.wait("v", vc)
            mk.ins("v", lambda E, o=ridx_f[:, 3:4], i=ridx_f[:, 0:1]:
                   E.tensor_scalar(out=o, in0=i, scalar1=float(CAP), scalar2=None,
                                   op0=ALU.is_ge), inc=vc)
            mk.wait("v", vc)
            mk.ins("v", lambda E, o=ridx_f[:, 2:3], i=ridx_f[:, 3:4],
                   s2=ridx_f[:, 2:3]: E.scalar_tensor_tensor(
                   out=o, in0=i, scalar=1.0e6, in1=s2,
                   op0=ALU.mult, op1=ALU.add), inc=vc)
            mk.wait("v", vc)
            mk.ins("v", lambda E, o=sidx_sb[:, I:I + 1], i=ridx_f[:, 2:3]:
                   E.tensor_copy(o, i), inc=vc)
            mk.wait("v", vc)
        mk.fence()

        mk.phase(3)
        mk.wait("v", vc)
        mk.ins("v", lambda E: E.memset(zacc[:, :], 0.0), inc=vc)
        mk.wait("v", vc)
        pq_free = [0, 0, 0, 0]
        ps_free = [0, 0, 0, 0]
        qcol_free = 0
        sstg_free = [0, 0]
        zprev = 0
        for gi in range(4):
            mk.wait("s", tc, qcol_free)
            mk.wait("s", vc, zprev)
            for J in range(NT):
                bank = J % 4
                mk.wait("t", sc, pq_free[bank])
                mk.ins("t", lambda E, o=pq[:, bank * 512:(bank + 1) * 512],
                       l=auga[:, J * 128:(J + 1) * 128],
                       r=augb[:, gi * 512:(gi + 1) * 512]:
                       E.matmul(o, l, r, start=True, stop=True), inc=tc)
                mk.wait("s", tc)
                mk.ins("s", lambda E, o=qcol[:, J * 512:(J + 1) * 512],
                       i=pq[:, bank * 512:(bank + 1) * 512],
                       a=zscr[:, J:J + 1]: E.activation(
                       out=o, in_=i, func=AF.Exp, accum_out=a), inc=sc)
                pq_free[bank] = sc.n
            exp_done = sc.n
            mk.wait("v", sc, exp_done)
            mk.ins("v", lambda E: E.tensor_tensor(zacc[:, :], zacc[:, :],
                   zscr[:, :], ALU.add), inc=vc)
            zprev = vc.n
            mk.wait("t", sc, exp_done)
            for il in range(4):
                I = gi * 4 + il
                bk0, bk1 = 2 * (il % 2), 2 * (il % 2) + 1
                for dh, bank in ((0, bk0), (1, bk1)):
                    mk.wait("t", vc, ps_free[bank])
                    for J in range(NT):
                        mk.ins("t", lambda E,
                               o=ps[:, bank * 512:(bank + 1) * 512],
                               l=qcol[:, J * 512 + il * 128: J * 512 + (il + 1) * 128],
                               r=hi_tm[:, J * D + dh * 512: J * D + dh * 512 + 512],
                               st=(J == 0), sp=(J == NT - 1):
                               E.matmul(o, l, r, start=st, stop=sp),
                               inc=(tc if J == NT - 1 else None))
                smm_done = tc.n
                sp_ = il % 2
                mk.wait("v", tc, smm_done)
                mk.wait("v", mk.qs[sp_], sstg_free[sp_])
                mk.ins("v", lambda E, o=sstg[:, sp_ * D: sp_ * D + 512],
                       i=ps[:, bk0 * 512:(bk0 + 1) * 512]:
                       E.tensor_copy(o, i), inc=vc)
                mk.wait("v", vc)
                mk.ins("v", lambda E, o=sstg[:, sp_ * D + 512: (sp_ + 1) * D],
                       i=ps[:, bk1 * 512:(bk1 + 1) * 512]:
                       E.tensor_copy(o, i), inc=vc)
                ps_free[bk0] = vc.n
                ps_free[bk1] = vc.n
                mk.wait("q", vc)
                sstg_free[sp_] = mk.dma(
                    "q", scr_s[I * 128:(I + 1) * 128, :],
                    sstg[:, sp_ * D:(sp_ + 1) * D], sem=mk.qs[sp_])
            qcol_free = tc.n
        mk.tap("dbg_z", zacc[:, :])
        mk.fence()

        mk.phase(4)
        mk.wait("s", vc)
        mk.ins("s", lambda E: E.activation(out=score_sb[:, :], in_=tv_sb[:, :],
                                           func=AF.Sigmoid), inc=sc)
        sc_score = sc.n
        if b == 0:
            mk.wait("g", mk.qz)
        xsb_free = [0, 0]
        for k in range(NT):
            p = k % 2
            tmp = tm32a[:, p * D:(p + 1) * D]
            mk.wait("v", sc, sc_score)
            mk.ins("v", lambda E, o=tmp, a=hi_tm[:, k * D:(k + 1) * D],
                   b_=lo_tm[:, k * D:(k + 1) * D]:
                   E.tensor_tensor(o, a, b_, ALU.add), inc=vc)
            mk.wait("v", vc)
            mk.wait("v", mk.gsc[p], xsb_free[p])
            mk.ins("v", lambda E, o=xsb[:, p * D:(p + 1) * D], i=tmp,
                   s_=score_sb[:, k:k + 1]: E.tensor_scalar(
                   out=o, in0=i, scalar1=s_, scalar2=None, op0=ALU.mult), inc=vc)
            xs_done = vc.n
            mk.wait("g", vc, xs_done)
            if not mk.mute:
                mk.gsc[p].n += 16
            mk.emit("g", lambda E, i=xsb[:, p * D:(p + 1) * D],
                    ix=sidx_sb[:, k:k + 1], h=mk.gsc[p].h:
                    E.indirect_dma_start(out=scr_xsc[:, :],
                                         out_offset=IOX(ap=ix, axis=0),
                                         in_=i, in_offset=None,
                                         bounds_check=breg,
                                         oob_is_err=False).then_inc(h, 16))
            xsb_free[p] = mk.gsc[p].n
        mk.fence()

        mk.phase(5)
        actt_sh = vb(98304, ND * T)
        xsT_e3 = [vb(98304 + p * 6144, ND * CAP).rearrange("p_ (c s) -> p_ c s", c=ND)
                  for p in range(2)]
        acte = vb(110592, NT * CAP)
        wg_free = [0, 0]
        wgwb_free = [0, 0]
        wd_free = 0
        wdwb_free = 0
        dn_free = [0, 0, 0, 0]
        gbank_sfree = [0] * 4
        ubank_vfree = [0] * 4
        silu_free = [0, 0, 0, 0]
        spi = 0

        def load_wgu(br_i, i0, pw):
            for c in range(ND):
                mk.dma("q", wgb[:, (pw * ND + c) * 512:(pw * ND + c + 1) * 512],
                       scr_wg[br_i, c * 128:(c + 1) * 128, i0:i0 + 512], sem=mk.qgw[pw])
                mk.dma("s", wub[:, (pw * ND + c) * 512:(pw * ND + c + 1) * 512],
                       scr_wu[br_i, c * 128:(c + 1) * 128, i0:i0 + 512], sem=mk.qgw[pw])
            return (pw, mk.qgw[pw].n)

        def load_wd(br_i, ic_list, dst_off):
            for j, ic in enumerate(ic_list):
                mk.dma("s" if j % 2 else "q",
                       wdb[:, dst_off + j * 1024: dst_off + (j + 1) * 1024],
                       scr_wdn[br_i, ic * 128:(ic + 1) * 128, :], sem=mk.qwd)
            return mk.qwd.n

        sh_i = 8
        do_shared = branch_filter is None or "sh" in branch_filter
        if do_shared:
            if b == 0:
                mk.wait("q", mk.cv_sh)
                mk.wait("s", mk.cv_sh)
            for half in range(2):
                wg_loaded = [None, None]
                for sp_i in range(2):
                    pw = spi % 2
                    mk.wait("q", tc, wg_free[pw])
                    mk.wait("s", tc, wg_free[pw])
                    i0 = half * 1024 + sp_i * 512
                    wg_loaded[sp_i] = load_wgu(sh_i, i0, pw)
                    spi += 1
                mk.wait("q", tc, wd_free)
                mk.wait("s", tc, wd_free)
                ics = list(range(half * 8, half * 8 + 8))
                wd_ld = load_wd(sh_i, ics, 0)
                for sp_i in range(2):
                    pw, _wgl = wg_loaded[sp_i]
                    mk.wait("t", mk.qgw[pw], _wgl)
                    for il in range(4):
                        ic = sp_i * 4 + il
                        for t4 in range(4):
                            mk.wait("t", sc, gbank_sfree[t4])
                            mk.wait("t", vc, dn_free[t4])
                        for c in range(ND):
                            for t4 in range(4):
                                mk.ins("t", lambda E,
                                       o=pgu[:, t4 * 512:(t4 + 1) * 512],
                                       l=wgb[:, (pw * ND + c) * 512 + il * 128:
                                              (pw * ND + c) * 512 + (il + 1) * 128],
                                       r=dmaj[:, c * T + t4 * 512: c * T + (t4 + 1) * 512],
                                       st=(c == 0), sp=(c == ND - 1):
                                       E.matmul(o, l, r, start=st, stop=sp),
                                       inc=(tc if (c == ND - 1 and t4 == 3) else None))
                        gmm_done = tc.n
                        for t4 in range(4):
                            mk.wait("t", vc, ubank_vfree[t4])
                        for c in range(ND):
                            for t4 in range(4):
                                mk.ins("t", lambda E,
                                       o=pgu[:, (4 + t4) * 512:(5 + t4) * 512],
                                       l=wub[:, (pw * ND + c) * 512 + il * 128:
                                              (pw * ND + c) * 512 + (il + 1) * 128],
                                       r=dmaj[:, c * T + t4 * 512: c * T + (t4 + 1) * 512],
                                       st=(c == 0), sp=(c == ND - 1):
                                       E.matmul(o, l, r, start=st, stop=sp),
                                       inc=(tc if (c == ND - 1 and t4 == 3) else None))
                        umm_done = tc.n
                        for t4 in range(4):
                            mk.wait("s", tc, gmm_done)
                            mk.wait("s", vc, silu_free[t4])
                            mk.ins("s", lambda E, o=silu_g[t4][:, :],
                                   i=pgu[:, t4 * 512:(t4 + 1) * 512]:
                                   E.activation(out=o, in_=i, func=AF.Silu), inc=sc)
                            gbank_sfree[t4] = sc.n
                            mk.wait("v", sc)
                            mk.wait("v", tc, umm_done)
                            mk.ins("v", lambda E,
                                   o=actt_sh[:, ic * T + t4 * 512: ic * T + (t4 + 1) * 512],
                                   i=pgu[:, (4 + t4) * 512:(5 + t4) * 512],
                                   s_=silu_g[t4][:, :]:
                                   E.tensor_tensor(o, i, s_, ALU.mult), inc=vc)
                            ubank_vfree[t4] = vc.n
                            silu_free[t4] = vc.n
                    wg_free[pw] = tc.n
                act_done = vc.n
                mk.wait("t", mk.qwd, wd_ld)
                mk.wait("t", vc, act_done)
                for tch in range(NT):
                    for dh in range(2):
                        bank = 2 * (tch % 2) + dh
                        mk.wait("t", vc, dn_free[bank])
                        mk.wait("t", sc, gbank_sfree[bank])
                        for j in range(8):
                            mk.ins("t", lambda E,
                                   o=pgu[:, bank * 512:(bank + 1) * 512],
                                   l=actt_sh[:, j * T + tch * 128:
                                             j * T + (tch + 1) * 128],
                                   r=wdb[:, j * 1024 + dh * 512: j * 1024 + dh * 512 + 512],
                                   st=(j == 0), sp=(j == 7):
                                   E.matmul(o, l, r, start=st, stop=sp),
                                   inc=(tc if j == 7 else None))
                        dn_done = tc.n
                        a_sl = acc[:, tch * D + dh * 512: tch * D + dh * 512 + 512]
                        mk.wait("v", tc, dn_done)
                        if half == 0:
                            mk.ins("v", lambda E, o=a_sl,
                                   i=pgu[:, bank * 512:(bank + 1) * 512]:
                                   E.tensor_copy(o, i), inc=vc)
                        else:
                            mk.ins("v", lambda E, o=a_sl,
                                   i=pgu[:, bank * 512:(bank + 1) * 512]:
                                   E.tensor_tensor(o, o, i, ALU.add), inc=vc)
                        dn_free[bank] = vc.n
                wd_free = tc.n
        else:
            mk.wait("v", vc)
            mk.ins("v", lambda E: E.memset(acc[:, :], 0.0), inc=vc)
            mk.wait("v", vc)

        xstg3 = vb(0, 3 * 1024)
        ystg2 = vf(8192, 512)
        NSC = CAP // 128
        exp_list = [e for e in range(NE)
                    if branch_filter is None or e in branch_filter]
        mk.wait("q", mk.gsc[0])
        mk.wait("q", mk.gsc[1])
        dmaj_free = tc.n
        xstg_free = [dmaj_free] * 3
        ptr_free = [0, 0]
        xsT_free = [0, 0]
        xsT_done = [0, 0]
        acte_free = 0
        ystg_free = 0
        dnb_free = [0] * 6
        ptr_e = psAb[:, 0:2048]

        def issue_xstg_loads(e):
            lds = []
            for sc_ in range(NSC):
                mk.wait("q", tc, xstg_free[sc_])
                lds.append(mk.dma(
                    "q", xstg3[:, sc_ * D: sc_ * D + D],
                    scr_xsc[e * CAP + sc_ * 128: e * CAP + (sc_ + 1) * 128, :],
                    sem=mk.qxs[sc_]))
            return lds

        def prep_xsT(pbuf, lds):
            for sc_ in range(NSC):
                pxp = sc_ % 2
                mk.wait("t", mk.qxs[sc_], lds[sc_])
                mk.wait("t", sc, ptr_free[pxp])
                mk.wait("t", sc, gbank_sfree[pxp])
                mk.wait("t", vc, ubank_vfree[pxp])
                mk.wait("t", vc, dn_free[pxp])
                mk.wait("t", tc, xsT_free[pbuf])
                for c in range(ND):
                    mk.ins("t", lambda E,
                           o=ptr_e[:, pxp * 1024 + c * 128: pxp * 1024 + (c + 1) * 128],
                           i=xstg3[:, sc_ * D + c * 128: sc_ * D + (c + 1) * 128],
                           d=ident[:, :]: E.transpose(o, i, d),
                           inc=(tc if c == ND - 1 else None))
                xstg_free[sc_] = tc.n
                mk.wait("s", tc)
                mk.wait("s", tc, xsT_free[pbuf])
                mk.ins("s", lambda E,
                       o=xsT_e3[pbuf][:, :, sc_ * 128:(sc_ + 1) * 128],
                       i=ptr_e.rearrange("p (q c s) -> p q c s", q=2, c=ND)[:, pxp]:
                       E.copy(o, i), inc=sc)
                ptr_free[pxp] = sc.n
            xsT_done[pbuf] = sc.n

        pend_w = None
        for idx, e in enumerate(exp_list):
            pbuf = idx % 2
            if idx == 0:
                lds0 = issue_xstg_loads(e)
                prep_xsT(pbuf, lds0)
            wg_loaded = [None, None, None, None]

            def load_span(sp_i, ee=None):
                nonlocal spi
                ee = e if ee is None else ee
                pw = spi % 2
                mk.wait("q", tc, wg_free[pw])
                mk.wait("s", tc, wg_free[pw])
                wg_loaded[sp_i] = load_wgu(ee, sp_i * 512, pw)
                spi += 1

            if pend_w is None:
                if b == 0:
                    mk.wait("q", mk.cv_ex[e])
                    mk.wait("s", mk.cv_ex[e])
                load_span(0)
                load_span(1)
            else:
                wg_loaded[0], wg_loaded[1] = pend_w
            mk.wait("q", tc, wd_free)
            mk.wait("s", tc, wd_free)
            wd_ld = load_wd(e, list(range(16)), 0)
            for sp_i in range(4):
                pw, _wgl = wg_loaded[sp_i]
                mk.wait("t", mk.qgw[pw], _wgl)
                mk.wait("t", sc, xsT_done[pbuf])
                for il in range(4):
                    ic = sp_i * 4 + il
                    gb = 2 * (il % 2)
                    ub = 2 * (il % 2) + 1
                    mk.wait("t", sc, gbank_sfree[gb])
                    mk.wait("t", vc, ubank_vfree[ub])
                    mk.wait("t", vc, dn_free[gb])
                    mk.wait("t", vc, dn_free[ub])
                    if gb >= 2:
                        mk.wait("t", sc, dnb_free[gb - 2])
                    if ub >= 2:
                        mk.wait("t", sc, dnb_free[ub - 2])
                    for c in range(ND):
                        mk.ins("t", lambda E,
                               o=pgu[:, gb * 512: gb * 512 + CAP],
                               l=wgb[:, (pw * ND + c) * 512 + il * 128:
                                      (pw * ND + c) * 512 + (il + 1) * 128],
                               r=xsT_e3[pbuf].rearrange("p c s -> p (c s)")[:, c * CAP:(c + 1) * CAP],
                               st=(c == 0), sp=(c == ND - 1):
                               E.matmul(o, l, r, start=st, stop=sp),
                               inc=(tc if c == ND - 1 else None))
                    gmm_done = tc.n
                    for c in range(ND):
                        mk.ins("t", lambda E,
                               o=pgu[:, ub * 512: ub * 512 + CAP],
                               l=wub[:, (pw * ND + c) * 512 + il * 128:
                                      (pw * ND + c) * 512 + (il + 1) * 128],
                               r=xsT_e3[pbuf].rearrange("p c s -> p (c s)")[:, c * CAP:(c + 1) * CAP],
                               st=(c == 0), sp=(c == ND - 1):
                               E.matmul(o, l, r, start=st, stop=sp),
                               inc=(tc if c == ND - 1 else None))
                    umm_done = tc.n
                    sgi = il % 2
                    mk.wait("s", tc, gmm_done)
                    mk.wait("s", vc, silu_free[sgi])
                    mk.ins("s", lambda E, o=silu_g[sgi][:, 0:CAP],
                           i=pgu[:, gb * 512: gb * 512 + CAP]:
                           E.activation(out=o, in_=i, func=AF.Silu), inc=sc)
                    gbank_sfree[gb] = sc.n
                    mk.wait("v", sc)
                    mk.wait("v", tc, umm_done)
                    mk.wait("v", tc, acte_free)
                    mk.ins("v", lambda E,
                           o=acte[:, ic * CAP:(ic + 1) * CAP],
                           i=pgu[:, ub * 512: ub * 512 + CAP],
                           s_=silu_g[sgi][:, 0:CAP]:
                           E.tensor_tensor(o, i, s_, ALU.mult), inc=vc)
                    ubank_vfree[ub] = vc.n
                    silu_free[sgi] = vc.n
                wg_free[pw] = tc.n
                if sp_i + 2 < 4:
                    load_span(sp_i + 2)
            act_done = vc.n
            xsT_free[pbuf] = tc.n
            if idx + 1 < len(exp_list):
                e_next = exp_list[idx + 1]
                if b == 0:
                    mk.wait("q", mk.cv_ex[e_next])
                    mk.wait("s", mk.cv_ex[e_next])
                load_span(0, ee=e_next)
                load_span(1, ee=e_next)
                pend_w = (wg_loaded[0], wg_loaded[1])
                lds_n = issue_xstg_loads(e_next)
            mk.wait("t", mk.qwd, wd_ld)
            mk.wait("t", vc, act_done)
            for bnk in range(6):
                mk.wait("t", sc, dnb_free[bnk])
            for ic in range(16):
                for sc_ in range(NSC):
                    for dh in range(2):
                        bank = 2 + sc_ * 2 + dh
                        mk.ins("t", lambda E,
                               o=pgu[:, bank * 512:(bank + 1) * 512],
                               l=acte[:, ic * CAP + sc_ * 128: ic * CAP + (sc_ + 1) * 128],
                               r=wdb[:, ic * 1024 + dh * 512: ic * 1024 + dh * 512 + 512],
                               st=(ic == 0), sp=(ic == 15):
                               E.matmul(o, l, r, start=st, stop=sp),
                               inc=(tc if (ic == 15 and sc_ == NSC - 1 and dh == 1) else None))
            dnm_done = tc.n
            acte_free = tc.n
            wd_free = tc.n
            if idx + 1 < len(exp_list):
                prep_xsT(1 - pbuf, lds_n)
            for sc_ in range(NSC):
                for dh in range(2):
                    bank = 2 + sc_ * 2 + dh
                    mk.wait("s", tc, dnm_done)
                    mk.wait("s", mk.qy, ystg_free)
                    mk.ins("s", lambda E, o=ystg2[:, :],
                           i=pgu[:, bank * 512:(bank + 1) * 512]:
                           E.copy(o, i), inc=sc)
                    dnb_free[sc_ * 2 + dh] = sc.n
                    mk.wait("q", sc)
                    ystg_free = mk.dma(
                        "q",
                        scr_y[e * CAP + sc_ * 128: e * CAP + (sc_ + 1) * 128,
                              dh * 512:(dh + 1) * 512],
                        ystg2[:, :], sem=mk.qy)
        mk.fence()

        mk.phase(6)
        grt = vf(131072, 2 * D)
        mk.wait("v", vc)
        mk.ins("v", lambda E: E.reciprocal(zrec[:, :], zacc[:, :]), inc=vc)
        zr_done = vc.n
        mk.wait("s", vc, zr_done)
        mk.wait("g", mk.qy)
        cxt_free = [0, 0]
        csn_free = [0, 0]
        cgt_free = [0, 0]
        grt_free = [0, 0]
        cot_free = [0, 0]
        for k in range(NT):
            p = k % 2
            mk.wait("q", vc, cxt_free[p])
            mk.dma("q", cxt[:, p * D:(p + 1) * D], x[b, k * 128:(k + 1) * 128, :],
                   sem=mk.qc[p])
            mk.wait("q", vc, csn_free[p])
            ld_done = mk.dma("q", csn[:, p * D:(p + 1) * D],
                             scr_s[k * 128:(k + 1) * 128, :], sem=mk.qc[p])
            mk.wait("v", sc, grt_free[p])
            mk.ins("v", lambda E, o=grt[:, p * D:(p + 1) * D]: E.memset(o, 0.0), inc=vc)
            gz = vc.n
            mk.wait("g", vc, gz)
            if not mk.mute:
                mk.ggr[p].n += 16
            mk.emit("g", lambda E, o=grt[:, p * D:(p + 1) * D],
                    ix=sidx_sb[:, k:k + 1], h=mk.ggr[p].h:
                    E.indirect_dma_start(out=o, out_offset=None,
                                         in_=scr_y[:, :],
                                         in_offset=IOX(ap=ix, axis=0),
                                         bounds_check=breg,
                                         oob_is_err=False).then_inc(h, 16))
            g_ld = mk.ggr[p].n
            mk.wait("v", mk.ggr[p], g_ld)
            mk.ins("v", lambda E, o=grt[:, p * D:(p + 1) * D],
                   a=grt[:, p * D:(p + 1) * D], b_=acc[:, k * D:(k + 1) * D]:
                   E.tensor_tensor(o, a, b_, ALU.add), inc=vc)
            ffn_done = vc.n
            mk.wait("s", vc, ffn_done)
            mk.wait("s", vc, cgt_free[p])
            mk.ins("s", lambda E, o=cgt[:, p * D:(p + 1) * D],
                   i=grt[:, p * D:(p + 1) * D]:
                   E.activation(out=o, in_=i, func=AF.Silu), inc=sc)
            grt_free[p] = sc.n
            mk.wait("v", sc)
            mk.wait("v", mk.qc[p], ld_done)
            mk.wait("v", vc, zr_done)
            mk.wait("v", mk.qo[p], cot_free[p])
            mk.ins("v", lambda E, o=cot[:, p * D:(p + 1) * D],
                   i=csn[:, p * D:(p + 1) * D], z=zrec[:, k:k + 1],
                   g_=cgt[:, p * D:(p + 1) * D]: E.scalar_tensor_tensor(
                   out=o, in0=i, scalar=z, in1=g_, op0=ALU.mult, op1=ALU.mult), inc=vc)
            csn_free[p] = vc.n
            cgt_free[p] = vc.n
            mk.wait("v", vc)
            mk.ins("v", lambda E, o=cot[:, p * D:(p + 1) * D],
                   a=cot[:, p * D:(p + 1) * D], b_=cxt[:, p * D:(p + 1) * D]:
                   E.tensor_tensor(o, a, b_, ALU.add), inc=vc)
            cxt_free[p] = vc.n
            mk.wait("q", vc)
            cot_free[p] = mk.dma("q", out[b, k * 128:(k + 1) * 128, :],
                                 cot[:, p * D:(p + 1) * D], sem=mk.qo[p])
        mk.fence()

    for s_ in mk.final_sems:
        mk.wait("q", s_)
    mk.replay()
    es.close()
    return nc


_IDENT = np.concatenate([
    np.eye(128, dtype=ml_dtypes.bfloat16),
    np.triu(np.ones((128, 128), dtype=ml_dtypes.bfloat16)),
    np.ones((128, 128), dtype=ml_dtypes.bfloat16),
], axis=1)
_CVEC = np.zeros((128, 16), np.float32)
_CVEC[:, 0] = EPS
_CVEC[:, 1] = 1.0
_CVEC[:, 2:10] = np.arange(8, dtype=np.float32)[None, :]


def kernel(L0_attn_x, L0_attn_U, L0_ffn_gate_w, L0_ffn_sh_gate, L0_ffn_sh_up,
           L0_ffn_sh_down, L0_ffn_rt_gate, L0_ffn_rt_up, L0_ffn_rt_down):
    xf = np.ascontiguousarray(np.asarray(L0_attn_x, dtype=np.float32))
    wts = {
        "w_u": np.ascontiguousarray(np.asarray(L0_attn_U, np.float32)),
        "w_gate": np.ascontiguousarray(np.asarray(L0_ffn_gate_w, np.float32)),
        "w_shg": np.ascontiguousarray(np.asarray(L0_ffn_sh_gate, np.float32)),
        "w_shu": np.ascontiguousarray(np.asarray(L0_ffn_sh_up, np.float32)),
        "w_shd": np.ascontiguousarray(np.asarray(L0_ffn_sh_down, np.float32)),
        "w_rtg": np.ascontiguousarray(np.asarray(L0_ffn_rt_gate, np.float32)),
        "w_rtu": np.ascontiguousarray(np.asarray(L0_ffn_rt_up, np.float32)),
        "w_rtd": np.ascontiguousarray(np.asarray(L0_ffn_rt_down, np.float32)),
        "ident": _IDENT,
        "cvec": _CVEC,
    }
    nc = build_program()
    in_maps = []
    for c in range(NCORES):
        m = dict(wts)
        m["x"] = np.ascontiguousarray(xf[c * BPC:(c + 1) * BPC])
        in_maps.append(m)
    res = run_bass_kernel_spmd(nc, in_maps, core_ids=list(range(NCORES)))
    return np.concatenate([r["out"] for r in res.results], axis=0)
